# version 03
# baseline (speedup 1.0000x reference)
import math
import numpy as np
import ml_dtypes
import concourse.bass as bass
import concourse.mybir as mybir
from concourse.bass_utils import run_bass_kernel_spmd

F32 = mybir.dt.float32
BF16 = mybir.dt.bfloat16
AF = mybir.ActivationFunctionType
ALU = mybir.AluOpType
AX = mybir.AxisListType

D = 2048
T = 1024
FC = D // 128
HA = 16
HDA = 128
BLK = 64
NCA = T // BLK
HB = 8
DKB = 256
CB = 128
NCB = T // CB
HID = 5632
HC = HID // 128
INTER = 2816
EPS = 1e-5
RNDC = 12582912.0
GAMMA = [1.0 - 2.0 ** (-5.0 - h) for h in range(HB)]

BF = ml_dtypes.bfloat16
_DEV_STOP = None


class _DevStop(Exception):
    pass


class Prog:

    def __init__(self, nc, block, sems):
        self.nc = nc
        self.block = block
        self.s = sems
        self.c = {k: 0 for k in sems}

    def w(self, eng, *names):
        for n in names:
            eng.wait_ge(self.s[n], self.c[n])

    def dma(self, eng, name, **kw):
        inst = eng.dma_start(**kw)
        inst.then_inc(self.s[name], 16)
        self.c[name] += 16
        return inst

    def inc(self, inst, name, by=1):
        inst.then_inc(self.s[name], by)
        self.c[name] += by
        return inst


def ts(i, n):
    return slice(i * n, (i + 1) * n)



def _host_constants():
    c = {}
    c["identb"] = np.eye(128, dtype=BF)
    c["identf"] = np.eye(128, dtype=np.float32)
    i64 = np.arange(BLK, dtype=np.float32)
    c["ndiffT"] = (i64[:, None] - i64[None, :]).astype(np.float32)
    c["triT"] = (i64[None, :] >= i64[:, None]).astype(np.float32)
    c["negi_col"] = (-i64)[:, None].astype(np.float32)
    c["im64_col"] = (i64 - 64)[:, None].astype(np.float32)
    c["negt_row"] = (-np.arange(T, dtype=np.float32))[None, :]
    iC = np.arange(CB, dtype=np.float32)
    dmT = np.zeros((HB, CB, CB), dtype=np.float32)
    for h in range(HB):
        g = GAMMA[h]
        M = np.where(iC[None, :] >= iC[:, None],
                     g ** (iC[None, :] - iC[:, None]), 0.0)
        dmT[h] = M / 16.0
    c["dmT"] = dmT.astype(BF)
    qdi = np.zeros((HB, 128, CB), dtype=np.float32)
    dkc = np.zeros((HB, CB, 1), dtype=np.float32)
    qfix = np.zeros((HB, 128, T), dtype=np.float32)
    for h in range(HB):
        g = GAMMA[h]
        qdi[h, :, :] = (g ** (iC + 1.0) / 16.0)[None, :]
        dkc[h, :, 0] = g ** (CB - 1.0 - iC)
        tt = np.arange(T, dtype=np.float32)
        qfix[h, :, :] = (g ** (tt + 1.0) / 16.0)[None, :]
    c["qdi"] = qdi
    c["dkc"] = dkc
    c["qfix"] = qfix
    return c


def _host_rope(t0):
    dk_half = DKB // 2
    inv = 1.0 / (10000.0 ** (np.arange(0, DKB, 2, dtype=np.float32) / DKB))
    pos = np.arange(t0, t0 + T, dtype=np.float32)
    fr = pos[:, None] * inv[None, :]
    return np.cos(fr).T.copy(), np.sin(fr).T.copy()


def _prep_inputs(inputs):
    c = _host_constants()
    f32 = np.float32
    W = {k: np.ascontiguousarray(v, dtype=f32) for k, v in inputs.items()}
    x_full = W["L0_attn_a_x"]
    maps = []
    shared = {
        "Wqkv": W["L0_attn_a_Wqkv"],
        "bqkv": W["L0_attn_a_bqkv"].reshape(48, 128).T.copy(),
        "bqkv_row": W["L0_attn_a_bqkv"][None, :].copy(),
        "Wout": W["L0_attn_a_Wout"],
        "bout": W["L0_attn_a_bout"].reshape(16, 128).T.copy(),
        "slopes_rep": np.tile(W["L0_attn_a_slopes"][None, :], (128, 1)),
        "anw": W["L0_attn_b_attn_norm_w"].reshape(16, 128).T.copy(),
        "Wq": W["L0_attn_b_Wq"], "Wk": W["L0_attn_b_Wk"],
        "Wv": W["L0_attn_b_Wv"], "Wg": W["L0_attn_b_Wg"],
        "Wo": W["L0_attn_b_Wo"],
        "cwq": W["L0_attn_b_conv_wq"].reshape(16, 128, 4).transpose(1, 0, 2).copy(),
        "cwk": W["L0_attn_b_conv_wk"].reshape(16, 128, 4).transpose(1, 0, 2).copy(),
        "cwv": W["L0_attn_b_conv_wv"].reshape(16, 128, 4).transpose(1, 0, 2).copy(),
        "gnw": W["L0_attn_b_gnorm_w"].reshape(2, 128).T.copy(),
        "mnw": W["L0_attn_b_mlp_norm_w"].reshape(16, 128).T.copy(),
        "Wgate": W["L0_attn_b_W_gate"],
        "Wdown": W["L0_attn_b_W_down"],
        "w1T": np.ascontiguousarray(W["L0_ffn_w1"].T),
        "w3T": np.ascontiguousarray(W["L0_ffn_w3"].T),
        "w2T": np.ascontiguousarray(W["L0_ffn_w2"].T),
    }
    shared.update({k: np.ascontiguousarray(v, dtype=f32) if v.dtype != BF
                   else v for k, v in c.items()})
    for core in range(8):
        b, half = core // 2, core % 2
        xs = x_full[b, half * T:(half + 1) * T, :]
        cosT, sinT = _host_rope(half * T)
        m = dict(shared)
        m["x0"] = np.ascontiguousarray(xs.T)
        m["mask"] = np.full((128, 1), float(half), f32)
        m["cosT"] = cosT
        m["sinT"] = sinT
        maps.append(m)
    return maps


def kernel(**inputs):
    maps = _prep_inputs(inputs)
    nc = build_nc()
    res = run_bass_kernel_spmd(nc, maps, core_ids=list(range(8)))
    out = np.zeros((4, 2048, D), dtype=np.float32)
    for core in range(8):
        b, half = core // 2, core % 2
        y = np.asarray(res.results[core]["y"], dtype=np.float32)
        out[b, half * T:(half + 1) * T, :] = y.T
    return out



class Build:

    def __init__(self, nc):
        self.nc = nc
        self.st = None
        self.blk = None
        self.sem = {}
        self.s = self.sem
        self.cnt = {}

    def semget(self, name):
        if name not in self.sem:
            self.sem[name] = self.st.enter_context(self.nc.semaphore(name))
            self.cnt[name] = 0
        return self.sem[name]

    def step(self, ename, dma_issue=False):
        ch = self.semget("ch")
        before = self.cnt["ch"]
        self.cnt["ch"] += 1
        after = self.cnt["ch"]
        blk = self.blk

        def deco(fn):
            @getattr(blk, ename)
            def _(e):
                e.wait_ge(ch, before)
                last = fn(e)
                if last is None:
                    last = e.nop()
                last.then_inc(ch, 1)
            return None
        return deco

    def astep(self, ename):
        ch = self.semget("ch")
        before = self.cnt["ch"]
        blk = self.blk

        def deco(fn):
            @getattr(blk, ename)
            def _(e):
                e.wait_ge(ch, before)
                fn(e)
            return None
        return deco

    def join(self, ename, *sems):
        names = sems

        def fn(e):
            for n in names:
                self.wait(e, n)
            return e.nop()
        self.step(ename)(fn)

    def dma(self, e, sem_name, **kw):
        s = self.semget(sem_name)
        e.dma_start(**kw).then_inc(s, 16)
        self.cnt[sem_name] += 16
        return self.cnt[sem_name]

    def wait(self, e, sem_name, val=None):
        s = self.semget(sem_name)
        e.wait_ge(s, self.cnt[sem_name] if val is None else val)


def ap3(t, off, dims):
    return bass.AP(t, off, dims)


def apv(t, rowlen, p0, pc, foff, dims):
    return bass.AP(t, p0 * rowlen + foff, [[rowlen, pc]] + dims)



def _dev_dump(B, y, slices):
    @B.step("gpsimd")
    def _(g):
        for ap, r0, ncol in slices:
            B.dma(g, "do", out=y[r0:r0 + 128, 0:ncol], in_=ap)
        return None

    @B.step("gpsimd")
    def _(g):
        B.wait(g, "do")
        return None


def build_nc():
    import contextlib
    nc = bass.Bass("TRN2", target_bir_lowering=False)

    I = {}
    for name, shape in [
        ("x0", [D, T]), ("mask", [128, 1]),
        ("Wqkv", [D, 3 * D]), ("bqkv", [128, 48]), ("bqkv_row", [1, 3 * D]),
        ("Wout", [D, D]), ("bout", [128, FC]), ("slopes_rep", [128, HA]),
        ("anw", [128, FC]), ("Wq", [D, D]), ("Wk", [D, D]), ("Wv", [D, D]),
        ("Wg", [D, D]), ("Wo", [D, D]),
        ("cwq", [128, FC, 4]), ("cwk", [128, FC, 4]), ("cwv", [128, FC, 4]),
        ("gnw", [128, 2]), ("mnw", [128, FC]),
        ("Wgate", [D, HID]), ("Wdown", [INTER, D]),
        ("w1T", [D, HID]), ("w3T", [D, HID]), ("w2T", [HID, D]),
        ("identf", [128, 128]), ("ndiffT", [BLK, BLK]), ("triT", [BLK, BLK]),
        ("negi_col", [BLK, 1]), ("im64_col", [BLK, 1]), ("negt_row", [1, T]),
        ("qdi", [HB, 128, CB]), ("dkc", [HB, CB, 1]), ("qfix", [HB, 128, T]),
        ("cosT", [128, T]), ("sinT", [128, T]),
    ]:
        I[name] = nc.dram_tensor(name, shape, F32, kind="ExternalInput").ap()
    for name, shape in [("identb", [128, 128]), ("dmT", [HB, CB, CB])]:
        I[name] = nc.dram_tensor(name, shape, BF16, kind="ExternalInput").ap()
    y = nc.dram_tensor("y", [D, T], F32, kind="ExternalOutput").ap()

    cc_a_in = nc.dram_tensor("cc_a_in", [128, HA * 128], F32).ap()
    cc_a_out = nc.dram_tensor("cc_a_out", [256, HA * 128], F32).ap()
    cc_h_in = nc.dram_tensor("cc_h_in", [128, FC * 9], BF16).ap()
    cc_h_out = nc.dram_tensor("cc_h_out", [256, FC * 9], BF16).ap()
    cc_b_in = nc.dram_tensor("cc_b_in", [128, HB * 2 * 256], F32).ap()
    cc_b_out = nc.dram_tensor("cc_b_out", [256, HB * 2 * 256], F32).ap()
    x2scr = nc.dram_tensor("x2scr", [D, T], F32).ap()
    hscr = nc.dram_tensor("hscr", [D, T], F32).ap()
    oscr = nc.dram_tensor("oscr", [D, T], BF16).ap()
    tmpscr = nc.dram_tensor("tmpscr", [T, HID], F32).ap()
    ternd = {1: nc.dram_tensor("tern1", [D, HID], BF16).ap(),
             3: nc.dram_tensor("tern3", [D, HID], BF16).ap(),
             2: nc.dram_tensor("tern2", [HID, D], BF16).ap()}

    B = Build(nc)
    MG = []

    def mge(g):
        MG.append(g)
        return g.__enter__()

    def mgx(g):
        MG.remove(g)
        g.__exit__(None, None, None)

    def mg_cleanup():
        for g in reversed(MG):
            g.__exit__(None, None, None)
        MG.clear()

    with contextlib.ExitStack() as st:
      try:
            B.st = st
            sbuf = lambda name, shape, dt=F32, side="left": st.enter_context(
                nc.sbuf_tensor(name, shape, dt, side=side))
            psum = lambda name, shape, dt=F32: st.enter_context(
                nc.psum_tensor(name, shape, dt))

            c_identb = sbuf("c_identb", [128, 128], BF16)
            c_identf = sbuf("c_identf", [128, 128])
            c_slopes = sbuf("c_slopes", [128, HA])
            c_mask = sbuf("c_mask", [128, 1])
            onesf = sbuf("onesf", [1, 128])
            onesbr = sbuf("onesbr", [1, 128], BF16)
            onesb = sbuf("onesb", [128, 1], BF16)
            _tgs = []
            for _n, _s, _d in [("c_ndiffT", [BLK, BLK], F32),
                               ("c_triT", [BLK, BLK], F32),
                               ("c_im64", [BLK, 1], F32),
                               ("c_negt", [1, T], F32),
                               ("t_maskT", [BLK, HA, BLK], F32),
                               ("t_qd", [128, HA, BLK], BF16),
                               ("t_kd", [BLK, HA], F32),
                               ("t_bd", [128, HA], F32)]:
                _g = nc.sbuf_tensor(_n, _s, _d)
                _tgs.append(_g)
                MG.append(_g)
                _h = _g.__enter__()
                if _n == "c_ndiffT":
                    c_ndiffT = _h
                elif _n == "c_triT":
                    c_triT = _h
                elif _n == "c_im64":
                    c_im64 = _h
                elif _n == "c_negt":
                    c_negt = _h
                elif _n == "t_maskT":
                    t_maskT = _h
                elif _n == "t_qd":
                    t_qd = _h
                elif _n == "t_kd":
                    t_kd = _h
                elif _n == "t_bd":
                    t_bd = _h

            B.blk = st.enter_context(nc.Block())
            B.semget("ch")


            def pinc(inst, name):
                inst.then_inc(B.semget(name), 1)
                B.cnt[name] += 1
                return inst

            def fm_proj_island(wdram, col0, nblocks, wst, csum2, pjs, pcs,
                               kchunks, rhs_src, colsum_lhsT, negm_row,
                               evac1, evac2=None):
                ev = B.semget("ev")
                eb = B.semget("eb")
                wmark = [0] * nblocks
                blk_end_ch = [0] * (nblocks + 2)
                ntile = [0]
                ev0 = B.cnt["ev"]

                def load(blki):
                    @B.astep("gpsimd")
                    def _(g):
                        B.wait(g, "dw")
                        if blki >= 2:
                            g.wait_ge(B.s["ch"], blk_end_ch[blki - 2])
                        for c in range(kchunks):
                            B.dma(g, "dw", out=wst[:, blki % 2, c, :],
                                  in_=wdram[ts(c, 128),
                                            col0 + blki * 512:
                                            col0 + (blki + 1) * 512])
                        wmark[blki] = B.cnt["dw"]

                load(0)
                if nblocks > 1:
                    load(1)
                for blki in range(nblocks):
                    slot = blki % 2
                    if csum2 is not None:
                        @B.step("tensor")
                        def _(t, blki=blki, slot=slot, ebp=B.cnt["eb"]):
                            B.wait(t, "dw", wmark[blki])
                            t.wait_ge(eb, ebp)
                            last = None
                            for c in range(kchunks):
                                last = t.matmul(pcs[:, :], colsum_lhsT(c),
                                                wst[:, slot, c, :],
                                                start=(c == 0),
                                                stop=(c == kchunks - 1))
                            return last
                        csum_ch = B.cnt["ch"]
                        eb_after = B.cnt["eb"] + 1

                        @B.astep("vector")
                        def _(v, slot=slot, csum_ch=csum_ch):
                            v.wait_ge(B.s["ch"], csum_ch)
                            pinc(v.tensor_copy(csum2[0:1, slot, :], pcs[:, :]),
                                 "eb")

                    for dc in range(4):
                        fc_out = blki * 4 + dc
                        for tt in range(2):
                            j = ntile[0]
                            ntile[0] += 1
                            pj = pjs[j % 2]

                            @B.step("tensor")
                            def _(t, blki=blki, slot=slot, dc=dc, tt=tt, j=j,
                                  pj=pj, ebw=(B.cnt["eb"] if csum2 is not None
                                              else 0)):
                                B.wait(t, "dw", wmark[blki])
                                if j >= 2:
                                    t.wait_ge(ev, ev0 + j - 1)
                                if csum2 is not None and dc == 0 and tt == 0:
                                    t.wait_ge(eb, ebw)
                                last = None
                                for c in range(kchunks):
                                    last = t.matmul(
                                        pj[:, :], wst[:, slot, c, ts(dc, 128)],
                                        rhs_src(c, tt), start=(c == 0),
                                        stop=(c == kchunks - 1 and
                                              negm_row is None))
                                if negm_row is not None:
                                    last = t.matmul(
                                        pj[:, :], csum2[0:1, slot, ts(dc, 128)],
                                        negm_row[0:1, ts(tt, 512)],
                                        start=False, stop=True)
                                return last
                            tile_ch = B.cnt["ch"]
                            if dc == 3 and tt == 1:
                                blk_end_ch[blki] = tile_ch

                            @B.astep("vector")
                            def _(v, fc_out=fc_out, tt=tt, pj=pj,
                                  tile_ch=tile_ch):
                                v.wait_ge(B.s["ch"], tile_ch)
                                pinc(evac1(v, fc_out, tt, pj), "ev")

                            if evac2 is not None:
                                @B.astep("vector")
                                def _(v, fc_out=fc_out, tt=tt, j=j):
                                    v.wait_ge(ev, ev0 + j + 1)
                                    pinc(evac2(v, fc_out, tt), "eb")

                    if blki + 2 < nblocks:
                        load(blki + 2)

                B.join("vector", "ev", "eb")
                B.join("gpsimd", "dw")

            @B.step("sync")
            def _(s):
                for dst, src in [
                    (c_identb, I["identb"]), (c_identf, I["identf"]),
                    (c_slopes, I["slopes_rep"]), (c_mask, I["mask"]),
                    (c_ndiffT, I["ndiffT"]), (c_triT, I["triT"]),
                    (c_im64, I["im64_col"]), (c_negt, I["negt_row"]),
                ]:
                    B.dma(s, "dl", out=dst[:], in_=src)

            @B.step("vector")
            def _(v):
                v.memset(onesf[:], 1.0)
                v.memset(onesbr[:], 1.0)
                return v.memset(onesb[:], 1.0)

            @B.step("scalar")
            def _(s):
                B.wait(s, "dl")
                for h in range(HA):
                    sc = c_slopes[0:BLK, h:h + 1]
                    s.activation(out=t_maskT[:, h, :], in_=c_ndiffT[:],
                                 func=AF.Exp, scale=sc)
                last = None
                for h in range(HA):
                    sc = c_slopes[0:BLK, h:h + 1]
                    last = s.activation(out=t_kd[:, h:h + 1], in_=c_im64[:],
                                        func=AF.Exp, scale=sc)
                return last

            @B.step("vector")
            def _(v):
                last = None
                for h in range(HA):
                    last = v.tensor_tensor(out=t_maskT[:, h, :],
                                           in0=t_maskT[:, h, :],
                                           in1=c_triT[:], op=ALU.mult)
                return last

            with (
                nc.sbuf_tensor("p0_rows", [1, HA * BLK], F32) as p0_rows,
                nc.sbuf_tensor("p0_bdr", [1, HA], F32) as p0_bdr,
                nc.psum_tensor("p0_ps", [128, 512]) as p0_ps,
                nc.psum_tensor("p0_ps2", [128, HA]) as p0_ps2,
            ):
                @B.step("scalar")
                def _(s):
                    for h in range(HA):
                        s.activation(out=p0_rows[0:1, h * BLK:(h + 1) * BLK],
                                     in_=c_negt[0:1, 0:BLK], func=AF.Exp,
                                     scale=c_slopes[0:1, h:h + 1])
                    return s.activation(out=p0_bdr[:], in_=c_slopes[0:1, :],
                                        func=AF.Exp, scale=-64.0)

                for g in range(2):
                    @B.step("tensor")
                    def _(t, g=g):
                        return t.matmul(p0_ps[:, :], onesf[:],
                                        p0_rows[0:1, g * 512:(g + 1) * 512],
                                        start=True, stop=True)

                    @B.step("vector")
                    def _(v, g=g):
                        return v.tensor_copy(
                            ap3(t_qd, g * 8 * BLK, [[HA * BLK, 128], [1, 512]]),
                            p0_ps[:, :])

                @B.step("tensor")
                def _(t):
                    return t.matmul(p0_ps2[:, :], onesf[:], p0_bdr[:],
                                    start=True, stop=True)

                @B.step("vector")
                def _(v):
                    return v.tensor_copy(t_bd[:, :], p0_ps2[:, :])

            def alloc(name, shape, dt=F32):
                g = nc.sbuf_tensor(name, shape, dt)
                return g, g.__enter__()

            _ga = []
            for _n, _s, _d in [("q_fm", [128, FC, T], BF16),
                               ("k_fm", [128, FC, T], BF16),
                               ("v_tm", [128, 8, D], BF16),
                               ("stat_m", [1, T], F32),
                               ("stat_negm_bf", [1, T], BF16),
                               ("stat_r", [1, T], F32),
                               ("r_b", [128, T], F32),
                               ("r_cols", [128, 8], F32)]:
                _g, _h = alloc(_n, _s, _d)
                MG.append(_g)
                _ga.append(_g)
                if _n == "q_fm":
                    q_fm = _h
                elif _n == "k_fm":
                    k_fm = _h
                elif _n == "v_tm":
                    v_tm = _h
                elif _n == "stat_m":
                    stat_m = _h
                elif _n == "stat_negm_bf":
                    stat_negm_bf = _h
                elif _n == "stat_r":
                    stat_r = _h
                elif _n == "r_b":
                    r_b = _h
                elif _n == "r_cols":
                    r_cols = _h

            with nc.sbuf_tensor("x_bf", [128, FC, T], BF16) as x_bf:
              with (
                nc.sbuf_tensor("sq_t", [128, T], BF16) as sq_t,
                nc.sbuf_tensor("tmp_row", [1, T], F32) as tmp_row,
                nc.psum_tensor("ps_r0", [1, 512]) as ps_r0,
                nc.psum_tensor("ps_r1", [1, 512]) as ps_r1,
                nc.psum_tensor("ps_r2", [1, 512]) as ps_r2,
                nc.psum_tensor("ps_r3", [1, 512]) as ps_r3,
                nc.psum_tensor("ps_b", [128, T]) as ps_b,
                nc.psum_tensor("ps_tp", [128, 128]) as ps_tp,
              ):
                @B.step("gpsimd")
                def _(g):
                    B.wait(g, "dw")
                    for c in range(FC):
                        B.dma(g, "dw", out=x_bf[:, c, :],
                              in_=I["x0"][ts(c, 128), :])
                    return None

                for c in range(FC):
                    @B.step("scalar")
                    def _(s, c=c):
                        if c == 0:
                            B.wait(s, "dw")
                        return s.activation(out=sq_t[:, :], in_=x_bf[:, c, :],
                                            func=AF.Square)

                    @B.step("tensor")
                    def _(t, c=c):
                        f, l = (c == 0), (c == FC - 1)
                        t.matmul(ps_r0[:, :], onesb[:], x_bf[:, c, 0:512],
                                 start=f, stop=l)
                        t.matmul(ps_r1[:, :], onesb[:], x_bf[:, c, 512:1024],
                                 start=f, stop=l)
                        t.matmul(ps_r2[:, :], onesb[:], sq_t[:, 0:512],
                                 start=f, stop=l)
                        return t.matmul(ps_r3[:, :], onesb[:], sq_t[:, 512:1024],
                                        start=f, stop=l)

                @B.step("vector")
                def _(v):
                    v.tensor_scalar(out=stat_m[0:1, 0:512], in0=ps_r0[:, :],
                                    scalar1=1.0 / D, scalar2=None, op0=ALU.mult)
                    v.tensor_scalar(out=stat_m[0:1, 512:1024], in0=ps_r1[:, :],
                                    scalar1=1.0 / D, scalar2=None, op0=ALU.mult)
                    v.tensor_scalar(out=stat_r[0:1, 0:512], in0=ps_r2[:, :],
                                    scalar1=1.0 / D, scalar2=None, op0=ALU.mult)
                    return v.tensor_scalar(out=stat_r[0:1, 512:1024],
                                           in0=ps_r3[:, :],
                                           scalar1=1.0 / D, scalar2=None,
                                           op0=ALU.mult)

                @B.step("vector")
                def _(v):
                    v.tensor_tensor(out=tmp_row[:], in0=stat_m[:],
                                    in1=stat_m[:], op=ALU.mult)
                    return v.tensor_scalar(out=stat_negm_bf[:], in0=stat_m[:],
                                           scalar1=-1.0, scalar2=None,
                                           op0=ALU.mult)

                @B.step("vector")
                def _(v):
                    return v.scalar_tensor_tensor(out=stat_r[:], in0=stat_r[:],
                                                  scalar=EPS, in1=tmp_row[:],
                                                  op0=ALU.add, op1=ALU.subtract)

                @B.step("vector")
                def _(v):
                    return v.reciprocal(out=stat_r[:], in_=stat_r[:])

                @B.step("scalar")
                def _(s):
                    return s.activation(out=stat_r[:], in_=stat_r[:],
                                        func=AF.Sqrt)

                @B.step("tensor")
                def _(t):
                    t.matmul(ps_b[:, 0:512], onesf[:], stat_r[0:1, 0:512],
                             start=True, stop=True)
                    return t.matmul(ps_b[:, 512:1024], onesf[:],
                                    stat_r[0:1, 512:1024], start=True, stop=True)

                @B.step("vector")
                def _(v):
                    return v.tensor_copy(r_b[:, :], ps_b[:, :])

                for tb in range(8):
                    @B.step("tensor")
                    def _(t, tb=tb):
                        return t.transpose(ps_tp[:, :], r_b[:, ts(tb, 128)],
                                           c_identf[:])

                    @B.step("vector")
                    def _(v, tb=tb):
                        return v.tensor_copy(r_cols[:, tb:tb + 1], ps_tp[:, 0:1])

              with (
                    nc.sbuf_tensor("wst", [128, 2, FC, 512], BF16) as wst,
                    nc.sbuf_tensor("csum", [1, 2, 512], BF16) as csum,
                    nc.sbuf_tensor("bvb", [128, 512], F32) as bvb,
                    nc.sbuf_tensor("bqc", [128, 48], F32) as bqc,
                    nc.psum_tensor("pj0", [128, 512]) as pj0,
                    nc.psum_tensor("pj1", [128, 512]) as pj1,
                    nc.psum_tensor("pcs", [1, 512]) as pcs,
                    nc.psum_tensor("ps_bv", [128, 512]) as ps_bv,
                ):
                    @B.step("sync")
                    def _(s):
                        B.dma(s, "dl", out=bqc[:, :], in_=I["bqkv"])
                        return None

                    def stream_wblock(dram_ap, col0, slot, w_mark):
                        @B.step("gpsimd")
                        def _(g):
                            B.wait(g, "dw")
                            for c in range(FC):
                                B.dma(g, "dw", out=wst[:, slot, c, :],
                                      in_=dram_ap[ts(c, 128), col0:col0 + 512])
                            w_mark[0] = B.cnt["dw"]
                            return None

                    def colsum_block(slot, w_mark):
                        @B.step("tensor")
                        def _(t):
                            B.wait(t, "dw", w_mark[0])
                            last = None
                            for c in range(FC):
                                last = t.matmul(pcs[:, :], onesb[:],
                                                wst[:, slot, c, :],
                                                start=(c == 0), stop=(c == FC - 1))
                            return last

                        @B.step("vector")
                        def _(v):
                            return v.tensor_copy(csum[:, :], pcs[:, :])

                    def qk_evac1(v, fc_out, tt, pj):
                        dstbuf = q_fm if fc_out < FC else k_fm
                        return v.tensor_tensor(
                            out=dstbuf[:, fc_out % FC, ts(tt, 512)],
                            in0=pj[:, :], in1=r_b[:, ts(tt, 512)], op=ALU.mult)

                    def qk_evac2(v, fc_out, tt):
                        dstbuf = q_fm if fc_out < FC else k_fm
                        return v.tensor_scalar(
                            out=dstbuf[:, fc_out % FC, ts(tt, 512)],
                            in0=dstbuf[:, fc_out % FC, ts(tt, 512)],
                            scalar1=bqc[:, fc_out:fc_out + 1],
                            scalar2=None, op0=ALU.add)

                    B.join("vector", "dl")
                    fm_proj_island(I["Wqkv"], 0, 8, wst, csum, [pj0, pj1], pcs,
                                   FC, lambda c, tt: x_bf[:, c, ts(tt, 512)],
                                   lambda c: onesb[:], stat_negm_bf,
                                   qk_evac1, qk_evac2)

                    ev = B.semget("ev")
                    eb = B.semget("eb")
                    vwmark = [0] * 4
                    vblk_end = [0] * 6
                    ev0v = B.cnt["ev"]

                    def vload(vb):
                        @B.astep("gpsimd")
                        def _(g):
                            B.wait(g, "dw")
                            if vb >= 2:
                                g.wait_ge(B.s["ch"], vblk_end[vb - 2])
                            for c in range(FC):
                                B.dma(g, "dw", out=wst[:, vb % 2, c, :],
                                      in_=I["Wqkv"][ts(c, 128),
                                                    2 * D + vb * 512:
                                                    2 * D + (vb + 1) * 512])
                            vwmark[vb] = B.cnt["dw"]

                    vload(0)
                    vload(1)
                    for vb in range(4):
                        slot = vb % 2

                        @B.step("sync")
                        def _(s, vb=vb):
                            B.wait(s, "eb")
                            B.dma(s, "dl", out=bvb[0:1, :],
                                  in_=I["bqkv_row"][0:1, 2 * D + vb * 512:
                                                    2 * D + (vb + 1) * 512])
                            return None

                        @B.step("tensor")
                        def _(t, vb=vb, slot=slot, ebp=B.cnt["eb"]):
                            B.wait(t, "dw", vwmark[vb])
                            B.wait(t, "dl")
                            t.wait_ge(eb, ebp)
                            for c in range(FC):
                                t.matmul(pcs[:, :], onesb[:], wst[:, slot, c, :],
                                         start=(c == 0), stop=(c == FC - 1))
                            return t.matmul(ps_bv[:, :], onesf[:], bvb[0:1, :],
                                            start=True, stop=True)
                        cs_ch = B.cnt["ch"]

                        @B.astep("vector")
                        def _(v, slot=slot, cs_ch=cs_ch):
                            v.wait_ge(B.s["ch"], cs_ch)
                            v.tensor_copy(csum[0:1, slot, :], pcs[:, :])
                            pinc(v.tensor_copy(bvb[:, :], ps_bv[:, :]), "eb")

                        ebw = B.cnt["eb"]
                        for tb in range(8):
                            j = 8 * vb + tb
                            pj = pj0 if j % 2 == 0 else pj1

                            @B.step("tensor")
                            def _(t, vb=vb, slot=slot, tb=tb, j=j, pj=pj,
                                  ebw=ebw):
                                if j >= 2:
                                    t.wait_ge(ev, ev0v + j - 1)
                                if tb == 0:
                                    t.wait_ge(eb, ebw)
                                for c in range(FC):
                                    t.matmul(pj[:, :], x_bf[:, c, ts(tb, 128)],
                                             wst[:, slot, c, :],
                                             start=(c == 0), stop=False)
                                return t.matmul(
                                    pj[:, :], stat_negm_bf[0:1, ts(tb, 128)],
                                    csum[0:1, slot, :], start=False, stop=True)
                            tile_ch = B.cnt["ch"]
                            if tb == 7:
                                vblk_end[vb] = tile_ch

                            @B.astep("vector")
                            def _(v, vb=vb, tb=tb, pj=pj, tile_ch=tile_ch,
                                  ebw=ebw):
                                v.wait_ge(B.s["ch"], tile_ch)
                                v.wait_ge(eb, ebw)
                                pinc(v.tensor_scalar(
                                    out=v_tm[:, tb, ts(vb, 512)], in0=pj[:, :],
                                    scalar1=r_cols[:, tb:tb + 1], scalar2=None,
                                    op0=ALU.mult), "ev")

                            @B.astep("vector")
                            def _(v, vb=vb, tb=tb, j=j):
                                v.wait_ge(ev, ev0v + j + 1)
                                pinc(v.tensor_tensor(
                                    out=v_tm[:, tb, ts(vb, 512)],
                                    in0=v_tm[:, tb, ts(vb, 512)],
                                    in1=bvb[:, :], op=ALU.add), "eb")

                        if vb + 2 < 4:
                            vload(vb + 2)

                    B.join("vector", "ev", "eb")
                    B.join("gpsimd", "dw")

              if _DEV_STOP == "qkv":
                  sl = [(q_fm[:, c, :], c * 128, 1024) for c in range(8)]
                  sl += [(k_fm[:, c, :], 1024 + c * 128, 1024) for c in range(8)]
                  _dev_dump(B, y, sl)
                  raise _DevStop()
              if _DEV_STOP == "qkv2":
                  sl = [(q_fm[:, 8 + c, :], c * 128, 1024) for c in range(8)]
                  sl += [(k_fm[:, 8 + c, :], 1024 + c * 128, 1024) for c in range(8)]
                  _dev_dump(B, y, sl)
                  raise _DevStop()
              if _DEV_STOP == "vtm":
                  sl = [(v_tm[:, tb, 0:1024], tb * 128, 1024) for tb in range(8)]
                  sl += [(v_tm[:, tb, 1024:2048], 1024 + tb * 128, 1024)
                         for tb in range(8)]
                  _dev_dump(B, y, sl)
                  raise _DevStop()

            attn_fm_g = nc.sbuf_tensor("attn_fm", [128, FC, T], BF16, side="right")
            attn_fm = mge(attn_fm_g)
            with (
                nc.sbuf_tensor("S_all", [128, HA * 128], F32) as S_all,
                nc.sbuf_tensor("S_bf", [128, HA * 128], BF16) as S_bf,
                nc.sbuf_tensor("qd8", [128, 8, T], BF16) as qd8,
                nc.sbuf_tensor("sco_sb", [128, 8 * BLK], BF16) as sco_sb,
                nc.sbuf_tensor("ktm_sb", [128, 8 * 128], BF16) as ktm_sb,
                nc.psum_tensor("pscore", [128, 512], F32) as pscore,
                nc.psum_tensor("pktm", [128, 8 * 128], BF16) as pktm,
                nc.psum_tensor("po", [128, 512], F32) as po,
                nc.psum_tensor("psinc", [128, 1024], F32) as psinc,
            ):
                @B.step("vector")
                def _(v):
                    v.memset(S_all[:], 0.0)
                    return v.memset(S_bf[:], 0.0)

                for g in range(2):
                    @B.step("vector")
                    def _(v, g=g):
                        last = None
                        for j in range(8):
                            h = g * 8 + j
                            last = v.tensor_tensor(
                                out=ap3(qd8, j * T, [[8 * T, 128], [64, FC],
                                                     [1, 64]]),
                                in0=ap3(q_fm, h * T, [[FC * T, 128], [64, FC],
                                                      [1, 64]]),
                                in1=ap3(t_qd, h * BLK, [[HA * BLK, 128], [0, FC],
                                                        [1, 64]]),
                                op=ALU.mult)
                        return last

                    for c in range(NCA):
                        tl = c * BLK
                        tb, toff = c // 2, (c % 2) * BLK

                        @B.step("tensor")
                        def _(t, g=g, c=c, tl=tl, toff=toff):
                            pe = toff + BLK
                            for j in range(8):
                                h = g * 8 + j
                                t.transpose(pktm[toff:pe, ts(j, 128)],
                                            k_fm[:, h, tl:tl + BLK], c_identb[:])
                            last = None
                            for j in range(8):
                                h = g * 8 + j
                                last = t.matmul(pscore[toff:pe, ts(j, BLK)],
                                                k_fm[:, h, tl:tl + BLK],
                                                q_fm[:, h, tl:tl + BLK],
                                                start=True, stop=True)
                            return last

                        @B.step("vector")
                        def _(v, g=g, toff=toff):
                            pe = toff + BLK
                            for j in range(8):
                                h = g * 8 + j
                                v.tensor_scalar(out=ktm_sb[toff:pe, ts(j, 128)],
                                                in0=pktm[toff:pe, ts(j, 128)],
                                                scalar1=t_kd[:, h:h + 1],
                                                scalar2=None, op0=ALU.mult)
                            return v.tensor_tensor(
                                out=apv(sco_sb, 8 * BLK, toff, BLK, 0,
                                        [[BLK, 8], [1, BLK]]),
                                in0=apv(pscore, 512, toff, BLK, 0,
                                        [[BLK, 8], [1, BLK]]),
                                in1=apv(t_maskT, HA * BLK, 0, BLK, g * 8 * BLK,
                                        [[BLK, 8], [1, BLK]]),
                                op=ALU.mult)

                        @B.step("tensor")
                        def _(t, g=g, tl=tl, tb=tb, toff=toff):
                            pe = toff + BLK
                            for j in range(8):
                                h = g * 8 + j
                                vchunk = v_tm[toff:pe, tb,
                                              h * 128:(h + 1) * 128]
                                t.matmul(po[:, ts(j, BLK)], vchunk,
                                         sco_sb[toff:pe, ts(j, BLK)],
                                         start=True, stop=False)
                                t.matmul(po[:, ts(j, BLK)],
                                         S_bf[:, h * 128:(h + 1) * 128],
                                         qd8[:, j, tl:tl + BLK],
                                         start=False, stop=True)
                            last = None
                            for j in range(8):
                                h = g * 8 + j
                                vchunk = v_tm[toff:pe, tb,
                                              h * 128:(h + 1) * 128]
                                last = t.matmul(psinc[:, ts(j, 128)],
                                                ktm_sb[toff:pe, ts(j, 128)],
                                                vchunk, start=True, stop=True)
                            return last

                        @B.step("vector")
                        def _(v, g=g, tl=tl):
                            v.tensor_copy(
                                ap3(attn_fm, g * 8 * T + tl,
                                    [[FC * T, 128], [T, 8], [1, BLK]]),
                                ap3(po, 0, [[512, 128], [BLK, 8], [1, BLK]]))
                            return v.tensor_tensor(
                                out=ap3(S_all, g * 8 * 128,
                                        [[HA * 128, 128], [128, 8], [1, 128]]),
                                in0=ap3(S_all, g * 8 * 128,
                                        [[HA * 128, 128], [128, 8], [1, 128]]),
                                in1=ap3(t_bd, g * 8, [[HA, 128], [1, 8],
                                                      [0, 128]]),
                                op=ALU.mult)

                        @B.step("vector")
                        def _(v, g=g):
                            sl = slice(g * 8 * 128, (g + 1) * 8 * 128)
                            return v.tensor_tensor(out=S_all[:, sl],
                                                   in0=S_all[:, sl],
                                                   in1=psinc[:, :], op=ALU.add)

                        @B.step("vector")
                        def _(v, g=g):
                            sl = slice(g * 8 * 128, (g + 1) * 8 * 128)
                            return v.tensor_copy(S_bf[:, sl], S_all[:, sl])

                with (
                    nc.sbuf_tensor("S_in", [128, HA * 128], F32) as S_in,
                    nc.sbuf_tensor("S_inb", [128, HA * 128], BF16) as S_inb,
                    nc.sbuf_tensor("exp_row", [1, T], F32) as exp_row,
                    nc.sbuf_tensor("expt_bf", [128, T], BF16) as expt_bf,
                    nc.sbuf_tensor("qfix_bf", [128, T], BF16) as qfix_bf,
                    nc.psum_tensor("psfix", [128, T], F32) as psfix,
                ):
                    @B.step("sync")
                    def _(s):
                        B.dma(s, "dl", out=cc_a_in, in_=S_all[:])
                        return None

                    @B.step("gpsimd")
                    def _(g):
                        B.wait(g, "dl")
                        inst = g.collective_compute(
                            "AllGather", ALU.bypass,
                            replica_groups=[[0, 1], [2, 3], [4, 5], [6, 7]],
                            ins=[cc_a_in.opt()], outs=[cc_a_out.opt()])
                        inst.then_inc(B.semget("cc"))
                        B.cnt["cc"] += 1
                        return None

                    @B.step("gpsimd")
                    def _(g):
                        B.wait(g, "dw")
                        B.wait(g, "cc")
                        B.dma(g, "dw", out=S_in[:], in_=cc_a_out[0:128, :])
                        return None

                    @B.step("vector")
                    def _(v):
                        B.wait(v, "dw")
                        return v.tensor_scalar(out=S_inb[:], in0=S_in[:],
                                               scalar1=c_mask[:, 0:1],
                                               scalar2=None, op0=ALU.mult)

                    for h in range(HA):
                        @B.step("scalar")
                        def _(s, h=h):
                            return s.activation(out=exp_row[:], in_=c_negt[:],
                                                func=AF.Exp,
                                                scale=c_slopes[0:1, h:h + 1])

                        @B.step("tensor")
                        def _(t):
                            t.matmul(psfix[:, 0:512], onesf[:],
                                     exp_row[0:1, 0:512], start=True, stop=True)
                            return t.matmul(psfix[:, 512:1024], onesf[:],
                                            exp_row[0:1, 512:1024],
                                            start=True, stop=True)

                        @B.step("vector")
                        def _(v):
                            return v.tensor_copy(expt_bf[:], psfix[:, :])

                        @B.step("vector")
                        def _(v, h=h):
                            return v.tensor_tensor(out=qfix_bf[:],
                                                   in0=q_fm[:, h, :],
                                                   in1=expt_bf[:], op=ALU.mult)

                        @B.step("tensor")
                        def _(t, h=h):
                            t.matmul(psfix[:, 0:512],
                                     S_inb[:, h * 128:(h + 1) * 128],
                                     qfix_bf[:, 0:512], start=True, stop=True)
                            return t.matmul(psfix[:, 512:1024],
                                            S_inb[:, h * 128:(h + 1) * 128],
                                            qfix_bf[:, 512:1024],
                                            start=True, stop=True)

                        @B.step("vector")
                        def _(v, h=h):
                            return v.tensor_tensor(out=attn_fm[:, h, :],
                                                   in0=attn_fm[:, h, :],
                                                   in1=psfix[:, :], op=ALU.add)

            for _g in reversed(_ga):
                mgx(_g)
            xres_g = nc.sbuf_tensor("xres", [128, FC, T], F32, side="right")
            xres = mge(xres_g)
            with (
                nc.sbuf_tensor("wst2", [128, 2, FC, 512], BF16) as wst2,
                nc.sbuf_tensor("boutc", [128, FC], F32) as boutc,
                nc.sbuf_tensor("x0st", [128, 512], F32) as x0st,
                nc.sbuf_tensor("tmp_ev", [128, 512], F32) as tmp_ev,
                nc.psum_tensor("pw0", [128, 512]) as pw0,
            ):
                @B.step("sync")
                def _(s):
                    B.dma(s, "dl", out=boutc[:, :], in_=I["bout"])
                    return None

                for blki in range(4):
                    slot = blki % 2
                    w_mark = [0]

                    @B.step("gpsimd")
                    def _(g, blki=blki, slot=slot, w_mark=w_mark):
                        B.wait(g, "dw")
                        for c in range(FC):
                            B.dma(g, "dw", out=wst2[:, slot, c, :],
                                  in_=I["Wout"][ts(c, 128),
                                                blki * 512:(blki + 1) * 512])
                        w_mark[0] = B.cnt["dw"]
                        return None

                    for dc in range(4):
                        fc_out = blki * 4 + dc
                        for tt in range(2):
                            @B.step("sync")
                            def _(s, fc_out=fc_out, tt=tt):
                                B.dma(s, "dl", out=x0st[:],
                                      in_=I["x0"][ts(fc_out, 128), ts(tt, 512)])
                                return None

                            @B.step("tensor")
                            def _(t, slot=slot, dc=dc, tt=tt, w_mark=w_mark):
                                B.wait(t, "dw", w_mark[0])
                                last = None
                                for c in range(FC):
                                    last = t.matmul(pw0[:, :],
                                                    wst2[:, slot, c, ts(dc, 128)],
                                                    attn_fm[:, c, ts(tt, 512)],
                                                    start=(c == 0),
                                                    stop=(c == FC - 1))
                                return last

                            @B.step("scalar")
                            def _(s, fc_out=fc_out):
                                B.wait(s, "dl")
                                return s.activation(
                                    out=tmp_ev[:], in_=pw0[:, :],
                                    func=AF.Identity,
                                    bias=boutc[:, fc_out:fc_out + 1])

                            @B.step("vector")
                            def _(v, fc_out=fc_out, tt=tt):
                                B.wait(v, "dl")
                                return v.tensor_tensor(
                                    out=xres[:, fc_out, ts(tt, 512)],
                                    in0=tmp_ev[:], in1=x0st[:], op=ALU.add)

            for _g in reversed(_tgs):
                mgx(_g)

            if _DEV_STOP == "x1":
                sl = [(xres[:, c, :], c * 128, 1024) for c in range(FC)]
                _dev_dump(B, y, sl)
                raise _DevStop()

            invsw = sbuf("invsw", [128, 3])
            thrw = sbuf("thrw", [128, 3])
            nthrw = sbuf("nthrw", [128, 3])

            def ffn_prep():
                specs = [(0, I["w1T"], ternd[1], D, HID),
                         (1, I["w3T"], ternd[3], D, HID),
                         (2, I["w2T"], ternd[2], HID, D)]
                with (
                    nc.sbuf_tensor("fw_st", [128, HID], F32) as fw_st,
                    nc.sbuf_tensor("fw_ab", [128, HID], BF16) as fw_ab,
                    nc.sbuf_tensor("fw_g", [128, HID], BF16) as fw_g,
                    nc.sbuf_tensor("fw_l", [128, HID], BF16) as fw_l,
                    nc.sbuf_tensor("sc_row", [1, 512], F32) as sc_row,
                    nc.sbuf_tensor("sc_one", [1, 1], F32) as sc_one,
                    nc.psum_tensor("ps_sc", [1, 512]) as ps_sc,
                    nc.psum_tensor("ps_bc", [128, 1]) as ps_bc,
                ):
                    for wi, wsrc, wdst, nrows, ncols in specs:
                        nrc = nrows // 128
                        nmm = ncols // 512
                        for rc in range(nrc):
                            @B.step("sync")
                            def _(s, wsrc=wsrc, rc=rc, ncols=ncols):
                                B.dma(s, "dl", out=fw_st[:, 0:ncols],
                                      in_=wsrc[ts(rc, 128), :])
                                return None

                            @B.step("scalar")
                            def _(s, ncols=ncols):
                                B.wait(s, "dl")
                                return s.activation(out=fw_ab[:, 0:ncols],
                                                    in_=fw_st[:, 0:ncols],
                                                    func=AF.Abs)

                            @B.step("tensor")
                            def _(t, rc=rc, nmm=nmm):
                                last = None
                                for i2 in range(nmm):
                                    last = t.matmul(ps_sc[:, :], onesb[:],
                                                    fw_ab[:, ts(i2, 512)],
                                                    start=(rc == 0 and i2 == 0),
                                                    stop=(rc == nrc - 1 and
                                                          i2 == nmm - 1))
                                return last

                        @B.step("vector")
                        def _(v):
                            return v.tensor_copy(sc_row[:], ps_sc[:, :])

                        @B.step("vector")
                        def _(v, nrows=nrows, ncols=ncols):
                            return v.tensor_reduce(out=sc_one[:], in_=sc_row[:],
                                                   axis=AX.X, op=ALU.add)

                        @B.step("vector")
                        def _(v, nrows=nrows, ncols=ncols):
                            return v.tensor_scalar(
                                out=sc_one[:], in0=sc_one[:],
                                scalar1=1.0 / (nrows * ncols), scalar2=EPS,
                                op0=ALU.mult, op1=ALU.max)

                        @B.step("tensor")
                        def _(t):
                            return t.matmul(ps_bc[:, :], onesf[:], sc_one[:],
                                            start=True, stop=True)

                        @B.step("vector")
                        def _(v, wi=wi):
                            return v.tensor_copy(invsw[:, wi:wi + 1], ps_bc[:, :])

                        @B.step("vector")
                        def _(v, wi=wi):
                            return v.tensor_scalar(out=thrw[:, wi:wi + 1],
                                                   in0=invsw[:, wi:wi + 1],
                                                   scalar1=0.5, scalar2=None,
                                                   op0=ALU.mult)

                        @B.step("vector")
                        def _(v, wi=wi):
                            return v.tensor_scalar(out=nthrw[:, wi:wi + 1],
                                                   in0=invsw[:, wi:wi + 1],
                                                   scalar1=-0.5, scalar2=None,
                                                   op0=ALU.mult)

                        for rc in range(nrc):
                            @B.step("sync")
                            def _(s, wsrc=wsrc, rc=rc, ncols=ncols):
                                B.dma(s, "dl", out=fw_st[:, 0:ncols],
                                      in_=wsrc[ts(rc, 128), :])
                                return None

                            @B.step("vector")
                            def _(v, wi=wi, ncols=ncols):
                                B.wait(v, "dl")
                                B.wait(v, "dw")
                                v.tensor_scalar(out=fw_g[:, 0:ncols],
                                                in0=fw_st[:, 0:ncols],
                                                scalar1=thrw[:, wi:wi + 1],
                                                scalar2=None, op0=ALU.is_gt)
                                return v.tensor_scalar(
                                    out=fw_l[:, 0:ncols], in0=fw_st[:, 0:ncols],
                                    scalar1=nthrw[:, wi:wi + 1], scalar2=None,
                                    op0=ALU.is_lt)

                            @B.step("vector")
                            def _(v, ncols=ncols):
                                return v.tensor_tensor(out=fw_g[:, 0:ncols],
                                                       in0=fw_g[:, 0:ncols],
                                                       in1=fw_l[:, 0:ncols],
                                                       op=ALU.subtract)

                            @B.step("gpsimd")
                            def _(g, wdst=wdst, rc=rc, ncols=ncols):
                                B.wait(g, "dw")
                                B.dma(g, "dw", out=wdst[ts(rc, 128), :],
                                      in_=fw_g[:, 0:ncols])
                                return None

                    @B.step("gpsimd")
                    def _(g):
                        B.wait(g, "dw")
                        return None

            cols = sbuf("cols", [128, 12])
            oc1 = sbuf("oc1", [128, 8])
            oc3 = sbuf("oc3", [128, 8])
            ucq = sbuf("ucq", [128, 8])
            o2c = sbuf("o2c", [128, 8])
            mxh = sbuf("mxh", [128, 8])

            def ffn_use(tag, xres):
                cA, cB, cNM, cM2, cVA, cR, cMX, cCL, cT1 = range(9)

                with nc.sbuf_tensor(tag + "I_fm", [128, FC, T], BF16) as I_fm:
                    with (
                        nc.sbuf_tensor(tag + "x_tm", [128, D], F32) as x_tm,
                        nc.sbuf_tensor(tag + "d_tm", [128, D], F32) as d_tm,
                        nc.sbuf_tensor(tag + "sqd", [128, D], BF16) as sqd,
                        nc.sbuf_tensor(tag + "I_tm", [128, D], BF16) as I_tm,
                        nc.psum_tensor(tag + "ps_t", [128, 512], F32) as ps_t,
                        nc.psum_tensor(tag + "ps_tb", [128, 512], BF16) as ps_tb,
                    ):
                        for tb in range(8):
                            for g4 in range(4):
                                @B.step("tensor")
                                def _(t, tb=tb, g4=g4):
                                    last = None
                                    for k in range(4):
                                        c = g4 * 4 + k
                                        last = t.transpose(
                                            ps_t[:, ts(k, 128)],
                                            xres[:, c, ts(tb, 128)], c_identf[:])
                                    return last

                                @B.step("vector")
                                def _(v, g4=g4):
                                    return v.tensor_copy(
                                        x_tm[:, g4 * 512:(g4 + 1) * 512],
                                        ps_t[:, :])

                            @B.step("vector")
                            def _(v):
                                return v.tensor_reduce(
                                    out=cols[:, cA:cA + 1], in_=x_tm[:],
                                    axis=AX.X, op=ALU.add)

                            @B.step("scalar")
                            def _(s):
                                return s.activation(
                                    out=sqd[:], in_=x_tm[:], func=AF.Square,
                                    accum_out=cols[:, cB:cB + 1])

                            @B.step("vector")
                            def _(v):
                                return v.tensor_scalar(
                                    out=cols[:, cNM:cNM + 1],
                                    in0=cols[:, cA:cA + 1],
                                    scalar1=-1.0 / D, scalar2=None, op0=ALU.mult)

                            @B.step("vector")
                            def _(v):
                                v.tensor_tensor(out=cols[:, cM2:cM2 + 1],
                                                in0=cols[:, cNM:cNM + 1],
                                                in1=cols[:, cNM:cNM + 1],
                                                op=ALU.mult)
                                return v.tensor_scalar(
                                    out=cols[:, cVA:cVA + 1],
                                    in0=cols[:, cB:cB + 1], scalar1=1.0 / D,
                                    scalar2=EPS, op0=ALU.mult, op1=ALU.add)

                            @B.step("vector")
                            def _(v):
                                return v.tensor_tensor(out=cols[:, cVA:cVA + 1],
                                                       in0=cols[:, cVA:cVA + 1],
                                                       in1=cols[:, cM2:cM2 + 1],
                                                       op=ALU.subtract)

                            @B.step("vector")
                            def _(v):
                                return v.reciprocal(out=cols[:, cR:cR + 1],
                                                    in_=cols[:, cVA:cVA + 1])

                            @B.step("scalar")
                            def _(s):
                                return s.activation(out=cols[:, cR:cR + 1],
                                                    in_=cols[:, cR:cR + 1],
                                                    func=AF.Sqrt)

                            @B.step("scalar")
                            def _(s):
                                return s.activation(out=d_tm[:], in_=x_tm[:],
                                                    func=AF.Identity,
                                                    bias=cols[:, cNM:cNM + 1])

                            @B.step("vector")
                            def _(v):
                                return v.tensor_reduce(
                                    out=cols[:, cMX:cMX + 1], in_=d_tm[:],
                                    axis=AX.X, op=ALU.max,
                                    apply_absolute_value=True)

                            @B.step("vector")
                            def _(v):
                                return v.tensor_tensor(out=cols[:, cCL:cCL + 1],
                                                       in0=cols[:, cR:cR + 1],
                                                       in1=cols[:, cMX:cMX + 1],
                                                       op=ALU.mult)

                            @B.step("vector")
                            def _(v):
                                return v.tensor_scalar(out=cols[:, cCL:cCL + 1],
                                                       in0=cols[:, cCL:cCL + 1],
                                                       scalar1=EPS, scalar2=None,
                                                       op0=ALU.max)

                            @B.step("vector")
                            def _(v):
                                return v.reciprocal(out=cols[:, cT1:cT1 + 1],
                                                    in_=cols[:, cCL:cCL + 1])

                            @B.step("vector")
                            def _(v):
                                return v.tensor_scalar(out=cols[:, cT1:cT1 + 1],
                                                       in0=cols[:, cT1:cT1 + 1],
                                                       scalar1=127.0, scalar2=None,
                                                       op0=ALU.mult)

                            @B.step("vector")
                            def _(v, tb=tb):
                                v.tensor_tensor(out=cols[:, cT1:cT1 + 1],
                                                in0=cols[:, cT1:cT1 + 1],
                                                in1=cols[:, cR:cR + 1],
                                                op=ALU.mult)
                                v.tensor_scalar(out=oc1[:, tb:tb + 1],
                                                in0=cols[:, cCL:cCL + 1],
                                                scalar1=invsw[:, 0:1],
                                                scalar2=1.0 / 127.0,
                                                op0=ALU.mult, op1=ALU.mult)
                                return v.tensor_scalar(out=oc3[:, tb:tb + 1],
                                                       in0=cols[:, cCL:cCL + 1],
                                                       scalar1=invsw[:, 1:2],
                                                       scalar2=1.0 / 127.0,
                                                       op0=ALU.mult,
                                                       op1=ALU.mult)

                            @B.step("vector")
                            def _(v):
                                return v.tensor_scalar(out=d_tm[:], in0=d_tm[:],
                                                       scalar1=cols[:, cT1:cT1 + 1],
                                                       scalar2=RNDC,
                                                       op0=ALU.mult, op1=ALU.add)

                            @B.step("vector")
                            def _(v):
                                return v.tensor_scalar(out=I_tm[:], in0=d_tm[:],
                                                       scalar1=RNDC, scalar2=None,
                                                       op0=ALU.subtract)

                            for g4 in range(4):
                                @B.step("tensor")
                                def _(t, tb=tb, g4=g4):
                                    last = None
                                    for k in range(4):
                                        c = g4 * 4 + k
                                        last = t.transpose(
                                            ps_tb[:, ts(k, 128)],
                                            I_tm[:, ts(c, 128)], c_identb[:])
                                    return last

                                @B.step("vector")
                                def _(v, tb=tb, g4=g4):
                                    last = None
                                    for k in range(4):
                                        c = g4 * 4 + k
                                        last = v.tensor_copy(
                                            I_fm[:, c, ts(tb, 128)],
                                            ps_tb[:, ts(k, 128)])
                                    return last

                    with (
                        nc.sbuf_tensor(tag + "wA", [128, FC, 512], BF16) as wA,
                        nc.sbuf_tensor(tag + "wB", [128, FC, 512], BF16) as wB,
                        nc.sbuf_tensor(tag + "silu", [128, 512], F32) as silu_sb,
                        nc.sbuf_tensor(tag + "tmp_t", [128, 512], F32) as tmp_t,
                        nc.sbuf_tensor(tag + "rcol", [128, 1], F32) as rcol,
                        nc.psum_tensor(tag + "pA", [128, 512], F32) as pA,
                        nc.psum_tensor(tag + "pB", [128, 512], F32) as pB,
                    ):
                        @B.step("vector")
                        def _(v):
                            return v.memset(mxh[:], 0.0)

                        for hb in range(11):
                            @B.step("gpsimd")
                            def _(g, hb=hb):
                                B.wait(g, "dw")
                                for c in range(FC):
                                    B.dma(g, "dw", out=wA[:, c, :],
                                          in_=ternd[1][ts(c, 128),
                                                       hb * 512:(hb + 1) * 512])
                                    B.dma(g, "dw", out=wB[:, c, :],
                                          in_=ternd[3][ts(c, 128),
                                                       hb * 512:(hb + 1) * 512])
                                return None

                            for tb in range(8):
                                @B.step("tensor")
                                def _(t, tb=tb):
                                    B.wait(t, "dw")
                                    for c in range(FC):
                                        t.matmul(pA[:, :], I_fm[:, c, ts(tb, 128)],
                                                 wA[:, c, :], start=(c == 0),
                                                 stop=(c == FC - 1))
                                    last = None
                                    for c in range(FC):
                                        last = t.matmul(
                                            pB[:, :], I_fm[:, c, ts(tb, 128)],
                                            wB[:, c, :], start=(c == 0),
                                            stop=(c == FC - 1))
                                    return last

                                @B.step("scalar")
                                def _(s, tb=tb):
                                    return s.activation(out=silu_sb[:],
                                                        in_=pA[:, :],
                                                        func=AF.Silu,
                                                        scale=oc1[:, tb:tb + 1])

                                @B.step("vector")
                                def _(v):
                                    B.wait(v, "dt")
                                    return v.tensor_tensor(out=tmp_t[:],
                                                           in0=silu_sb[:],
                                                           in1=pB[:, :],
                                                           op=ALU.mult)

                                @B.step("vector")
                                def _(v):
                                    return v.tensor_reduce(
                                        out=rcol[:], in_=tmp_t[:], axis=AX.X,
                                        op=ALU.max, apply_absolute_value=True)

                                @B.step("vector")
                                def _(v, tb=tb):
                                    return v.tensor_tensor(out=mxh[:, tb:tb + 1],
                                                           in0=mxh[:, tb:tb + 1],
                                                           in1=rcol[:],
                                                           op=ALU.max)

                                @B.step("sync")
                                def _(s, hb=hb, tb=tb):
                                    B.dma(s, "dt",
                                          out=tmpscr[ts(tb, 128),
                                                     hb * 512:(hb + 1) * 512],
                                          in_=tmp_t[:])
                                    return None

                @B.step("vector")
                def _(v):
                    last = None
                    for tb in range(8):
                        last = v.tensor_tensor(out=mxh[:, tb:tb + 1],
                                               in0=mxh[:, tb:tb + 1],
                                               in1=oc3[:, tb:tb + 1], op=ALU.mult)
                    return last

                @B.step("vector")
                def _(v):
                    return v.tensor_scalar(out=mxh[:], in0=mxh[:], scalar1=EPS,
                                           scalar2=None, op0=ALU.max)

                @B.step("vector")
                def _(v):
                    return v.reciprocal(out=ucq[:], in_=mxh[:])

                @B.step("vector")
                def _(v):
                    last = None
                    for tb in range(8):
                        last = v.tensor_scalar(out=ucq[:, tb:tb + 1],
                                               in0=ucq[:, tb:tb + 1],
                                               scalar1=127.0,
                                               scalar2=oc3[:, tb:tb + 1],
                                               op0=ALU.mult, op1=ALU.mult)
                    return last

                @B.step("vector")
                def _(v):
                    last = None
                    for tb in range(8):
                        last = v.tensor_scalar(out=o2c[:, tb:tb + 1],
                                               in0=mxh[:, tb:tb + 1],
                                               scalar1=invsw[:, 2:3],
                                               scalar2=1.0 / 127.0,
                                               op0=ALU.mult, op1=ALU.mult)
                    return last

                with (
                    nc.sbuf_tensor(tag + "tmp_tm", [128, HID], F32) as tmp_tm,
                    nc.sbuf_tensor(tag + "ih_tm", [128, HID], BF16) as ih_tm,
                    nc.sbuf_tensor(tag + "ihf", [128, HC, 512], BF16) as ihf,
                    nc.sbuf_tensor(tag + "w2st", [128, 512], BF16) as w2st,
                    nc.sbuf_tensor(tag + "out4", [128, 4, D], BF16) as out4,
                    nc.psum_tensor(tag + "ps4", [128, 4, 512], F32) as ps4,
                    nc.psum_tensor(tag + "ps_t2", [128, 512], BF16) as ps_t2,
                    nc.psum_tensor(tag + "ps_tb2", [128, 512], BF16) as ps_tb2,
                ):
                    for tt in range(2):
                        for tb4 in range(4):
                            tb = tt * 4 + tb4

                            @B.step("sync")
                            def _(s, tb=tb):
                                B.wait(s, "dt")
                                B.dma(s, "dl", out=tmp_tm[:],
                                      in_=tmpscr[ts(tb, 128), :])
                                return None

                            @B.step("vector")
                            def _(v, tb=tb):
                                B.wait(v, "dl")
                                return v.tensor_scalar(
                                    out=tmp_tm[:], in0=tmp_tm[:],
                                    scalar1=ucq[:, tb:tb + 1], scalar2=RNDC,
                                    op0=ALU.mult, op1=ALU.add)

                            @B.step("vector")
                            def _(v):
                                return v.tensor_scalar(out=ih_tm[:],
                                                       in0=tmp_tm[:],
                                                       scalar1=RNDC, scalar2=None,
                                                       op0=ALU.subtract)

                            for g11 in range(11):
                                @B.step("tensor")
                                def _(t, g11=g11):
                                    last = None
                                    for k in range(4):
                                        hc = g11 * 4 + k
                                        last = t.transpose(
                                            ps_tb2[:, ts(k, 128)],
                                            ih_tm[:, ts(hc, 128)], c_identb[:])
                                    return last

                                @B.step("vector")
                                def _(v, tb4=tb4, g11=g11):
                                    last = None
                                    for k in range(4):
                                        hc = g11 * 4 + k
                                        last = v.tensor_copy(
                                            ihf[:, hc, ts(tb4, 128)],
                                            ps_tb2[:, ts(k, 128)])
                                    return last

                        for do in range(4):
                            for hc in range(HC):
                                @B.step("gpsimd")
                                def _(g, do=do, hc=hc):
                                    B.wait(g, "dw")
                                    B.dma(g, "dw", out=w2st[:],
                                          in_=ternd[2][ts(hc, 128),
                                                       do * 512:(do + 1) * 512])
                                    return None

                                @B.step("tensor")
                                def _(t, do=do, hc=hc):
                                    B.wait(t, "dw")
                                    last = None
                                    for tb4 in range(4):
                                        last = t.matmul(
                                            ps4[:, tb4, :],
                                            ihf[:, hc, ts(tb4, 128)], w2st[:],
                                            start=(hc == 0), stop=(hc == HC - 1))
                                    return last

                            @B.step("vector")
                            def _(v, tt=tt, do=do):
                                last = None
                                for tb4 in range(4):
                                    tb = tt * 4 + tb4
                                    last = v.tensor_scalar(
                                        out=out4[:, tb4, ts(do, 512)],
                                        in0=ps4[:, tb4, :],
                                        scalar1=o2c[:, tb:tb + 1], scalar2=None,
                                        op0=ALU.mult)
                                return last

                        for tb4 in range(4):
                            tb = tt * 4 + tb4
                            for g4 in range(4):
                                @B.step("tensor")
                                def _(t, tb4=tb4, g4=g4):
                                    last = None
                                    for k in range(4):
                                        c = g4 * 4 + k
                                        last = t.transpose(
                                            ps_t2[:, ts(k, 128)],
                                            out4[:, tb4, ts(c, 128)],
                                            c_identb[:])
                                    return last

                                @B.step("vector")
                                def _(v, tb=tb, g4=g4):
                                    last = None
                                    for k in range(4):
                                        c = g4 * 4 + k
                                        last = v.tensor_tensor(
                                            out=xres[:, c, ts(tb, 128)],
                                            in0=xres[:, c, ts(tb, 128)],
                                            in1=ps_t2[:, ts(k, 128)], op=ALU.add)
                                    return last

            B.semget("dt")
            ffn_prep()
            ffn_use("f1_", xres)

            if _DEV_STOP == "x2":
                sl = [(xres[:, c, :], c * 128, 1024) for c in range(FC)]
                _dev_dump(B, y, sl)
                raise _DevStop()

            bs_negm_bf = sbuf("bs_negm_bf", [1, T], BF16)
            m3_b = sbuf("m3_b", [128, T])
            r3_b = sbuf("r3_b", [128, T])
            c3_b = sbuf("c3_b", [128, T])
            anw_bf = sbuf("anw_bf", [128, FC], BF16)
            anw_f = sbuf("anw_f", [128, FC])
            cwsb = sbuf("cwsb", [128, FC, 12])
            with (
                nc.sbuf_tensor("bs_m", [1, T], F32) as bs_m,
                nc.sbuf_tensor("bs_v", [1, T], F32) as bs_v,
                nc.sbuf_tensor("bs_r3", [1, T], F32) as bs_r3,
                nc.sbuf_tensor("bs_c3", [1, T], F32) as bs_c3,
                nc.sbuf_tensor("b_cb", [128, T], BF16) as b_cb,
                nc.sbuf_tensor("b_sq", [128, T], BF16) as b_sq,
                nc.sbuf_tensor("b_row", [1, T], F32) as b_row,
                nc.psum_tensor("bp0", [1, 512]) as bp0,
                nc.psum_tensor("bp1", [1, 512]) as bp1,
                nc.psum_tensor("bp2", [1, 512]) as bp2,
                nc.psum_tensor("bp3", [1, 512]) as bp3,
                nc.psum_tensor("bpb", [128, T]) as bpb,
            ):
                @B.step("gpsimd")
                def _(g):
                    B.wait(g, "dw")
                    B.dma(g, "dw", out=anw_bf[:, :], in_=I["anw"])
                    return None

                @B.step("sync")
                def _(s):
                    B.dma(s, "dl", out=anw_f[:, :], in_=I["anw"])
                    for j, nm in enumerate(["cwq", "cwk", "cwv"]):
                        B.dma(s, "dl", out=cwsb[:, :, j * 4:(j + 1) * 4],
                              in_=I[nm])
                    return None

                for c in range(FC):
                    @B.step("vector")
                    def _(v, c=c):
                        return v.tensor_copy(b_cb[:, :], xres[:, c, :])

                    @B.step("scalar")
                    def _(s, c=c):
                        return s.activation(out=b_sq[:, :], in_=b_cb[:, :],
                                            func=AF.Square)

                    @B.step("tensor")
                    def _(t, c=c):
                        f, l = (c == 0), (c == FC - 1)
                        t.matmul(bp0[:, :], onesb[:], b_cb[:, 0:512],
                                 start=f, stop=l)
                        t.matmul(bp1[:, :], onesb[:], b_cb[:, 512:1024],
                                 start=f, stop=l)
                        t.matmul(bp2[:, :], onesb[:], b_sq[:, 0:512],
                                 start=f, stop=l)
                        return t.matmul(bp3[:, :], onesb[:], b_sq[:, 512:1024],
                                        start=f, stop=l)

                @B.step("sync")
                def _(s):
                    for c in range(FC):
                        B.dma(s, "dx", out=x2scr[ts(c, 128), :],
                              in_=xres[:, c, :])
                    return None

                @B.step("vector")
                def _(v):
                    v.tensor_scalar(out=bs_m[0:1, 0:512], in0=bp0[:, :],
                                    scalar1=1.0 / D, scalar2=None, op0=ALU.mult)
                    v.tensor_scalar(out=bs_m[0:1, 512:1024], in0=bp1[:, :],
                                    scalar1=1.0 / D, scalar2=None, op0=ALU.mult)
                    v.tensor_scalar(out=bs_v[0:1, 0:512], in0=bp2[:, :],
                                    scalar1=1.0 / D, scalar2=None, op0=ALU.mult)
                    return v.tensor_scalar(out=bs_v[0:1, 512:1024], in0=bp3[:, :],
                                           scalar1=1.0 / D, scalar2=None,
                                           op0=ALU.mult)

                @B.step("vector")
                def _(v):
                    v.tensor_tensor(out=b_row[:], in0=bs_m[:], in1=bs_m[:],
                                    op=ALU.mult)
                    return v.tensor_scalar(out=bs_negm_bf[:], in0=bs_m[:],
                                           scalar1=-1.0, scalar2=None,
                                           op0=ALU.mult)

                @B.step("vector")
                def _(v):
                    return v.tensor_tensor(out=bs_v[:], in0=bs_v[:], in1=b_row[:],
                                           op=ALU.subtract)

                @B.step("vector")
                def _(v):
                    return v.tensor_scalar(out=b_row[:], in0=bs_v[:], scalar1=EPS,
                                           scalar2=None, op0=ALU.add)

                @B.step("vector")
                def _(v):
                    return v.reciprocal(out=bs_r3[:], in_=b_row[:])

                @B.step("vector")
                def _(v):
                    return v.scalar_tensor_tensor(out=bs_c3[:], in0=bs_v[:],
                                                  scalar=EPS, in1=bs_r3[:],
                                                  op0=ALU.bypass, op1=ALU.mult)

                @B.step("vector")
                def _(v):
                    return v.tensor_scalar(out=bs_c3[:], in0=bs_c3[:],
                                           scalar1=EPS, scalar2=None, op0=ALU.add)

                @B.step("vector")
                def _(v):
                    return v.reciprocal(out=bs_c3[:], in_=bs_c3[:])

                @B.step("scalar")
                def _(s):
                    s.activation(out=bs_r3[:], in_=bs_r3[:], func=AF.Sqrt)
                    return s.activation(out=bs_c3[:], in_=bs_c3[:], func=AF.Sqrt)

                @B.step("vector")
                def _(v):
                    return v.tensor_tensor(out=bs_c3[:], in0=bs_c3[:],
                                           in1=bs_r3[:], op=ALU.mult)

                for (row, dst) in [(bs_m, m3_b), (bs_r3, r3_b), (bs_c3, c3_b)]:
                    @B.step("tensor")
                    def _(t, row=row):
                        t.matmul(bpb[:, 0:512], onesf[:], row[0:1, 0:512],
                                 start=True, stop=True)
                        return t.matmul(bpb[:, 512:1024], onesf[:],
                                        row[0:1, 512:1024], start=True, stop=True)

                    @B.step("vector")
                    def _(v, dst=dst):
                        return v.tensor_copy(dst[:, :], bpb[:, :])


            @B.step("sync")
            def _(s):
                B.wait(s, "dx")
                return None

            mgx(xres_g)
            mgx(attn_fm_g)

            x2w_g = nc.sbuf_tensor("x2w", [128, FC, T], BF16, side="right")
            x2w = mge(x2w_g)

            @B.step("gpsimd")
            def _(g):
                B.wait(g, "dw")
                B.wait(g, "dx")
                for c in range(FC):
                    B.dma(g, "dw", out=x2w[:, c, :], in_=x2scr[ts(c, 128), :])
                return None

            for c in range(FC):
                @B.step("vector")
                def _(v, c=c):
                    if c == 0:
                        B.wait(v, "dl")
                        B.wait(v, "dw")
                    return v.tensor_scalar(out=x2w[:, c, :], in0=x2w[:, c, :],
                                           scalar1=anw_f[:, c:c + 1],
                                           scalar2=None, op0=ALU.mult)

            out_q_g = nc.sbuf_tensor("out_q", [128, FC, T], BF16)
            out_q = mge(out_q_g)
            out_k_g = nc.sbuf_tensor("out_k", [128, FC, T], BF16)
            out_k = mge(out_k_g)
            v_tm_g = nc.sbuf_tensor("v_tmb", [128, NCB, 2048], BF16)
            v_tm = mge(v_tm_g)
            sendh_g = nc.sbuf_tensor("sendh", [128, FC * 9], BF16)
            sendh = mge(sendh_g)
            recvh_g = nc.sbuf_tensor("recvh", [128, FC * 9], BF16)
            recvh = mge(recvh_g)
            first3_g = nc.sbuf_tensor("first3", [128, FC * 9], BF16)
            first3 = mge(first3_g)

            def emit_proj_b(wdram, hoff, conv_j, dst, dstname, post_chunk=None):
                with (
                    nc.sbuf_tensor(dstname + "_pad", [128, FC, T + 3], BF16) as pad,
                    nc.sbuf_tensor(dstname + "_wst", [128, FC, 512], BF16) as wstb,
                    nc.sbuf_tensor(dstname + "_cs", [1, 512], BF16) as csb,
                    nc.sbuf_tensor(dstname + "_acc", [128, T], F32) as acc,
                    nc.psum_tensor(dstname + "_pj", [128, 512]) as pjb,
                    nc.psum_tensor(dstname + "_pc", [1, 512]) as pcb,
                ):
                    for blki in range(4):
                        w_mark = [0]

                        @B.step("gpsimd")
                        def _(g, blki=blki, w_mark=w_mark):
                            B.wait(g, "dw")
                            for c in range(FC):
                                B.dma(g, "dw", out=wstb[:, c, :],
                                      in_=wdram[ts(c, 128),
                                                blki * 512:(blki + 1) * 512])
                            w_mark[0] = B.cnt["dw"]
                            return None

                        @B.step("tensor")
                        def _(t, w_mark=w_mark):
                            B.wait(t, "dw", w_mark[0])
                            last = None
                            for c in range(FC):
                                last = t.matmul(pcb[:, :], anw_bf[:, c:c + 1],
                                                wstb[:, c, :],
                                                start=(c == 0), stop=(c == FC - 1))
                            return last

                        @B.step("vector")
                        def _(v):
                            return v.tensor_copy(csb[:, :], pcb[:, :])

                        for dc in range(4):
                            fc_out = blki * 4 + dc
                            for tt in range(2):
                                @B.step("tensor")
                                def _(t, dc=dc, tt=tt):
                                    for c in range(FC):
                                        t.matmul(pjb[:, :],
                                                 wstb[:, c, ts(dc, 128)],
                                                 x2w[:, c, ts(tt, 512)],
                                                 start=(c == 0), stop=False)
                                    return t.matmul(
                                        pjb[:, :], csb[0:1, ts(dc, 128)],
                                        bs_negm_bf[0:1, ts(tt, 512)],
                                        start=False, stop=True)

                                @B.step("vector")
                                def _(v, fc_out=fc_out, tt=tt):
                                    return v.tensor_tensor(
                                        out=pad[:, fc_out,
                                                3 + tt * 512:3 + (tt + 1) * 512],
                                        in0=pjb[:, :], in1=c3_b[:, ts(tt, 512)],
                                        op=ALU.mult)

                    @B.step("vector")
                    def _(v):
                        v.tensor_copy(
                            ap3(sendh, hoff, [[FC * 9, 128], [3, FC], [1, 3]]),
                            ap3(pad, 1024, [[FC * (T + 3), 128],
                                            [T + 3, FC], [1, 3]]))
                        return v.tensor_copy(
                            ap3(first3, hoff, [[FC * 9, 128], [3, FC], [1, 3]]),
                            ap3(pad, 3, [[FC * (T + 3), 128], [T + 3, FC],
                                         [1, 3]]))

                    for c in range(FC):
                        @B.step("vector")
                        def _(v, c=c):
                            return v.tensor_scalar(
                                out=acc[:, 3:T], in0=pad[:, c, 6:T + 3],
                                scalar1=cwsb[:, c, conv_j * 4 + 3:conv_j * 4 + 4],
                                scalar2=None, op0=ALU.mult)

                        for i in range(3):
                            @B.step("vector")
                            def _(v, c=c, i=i):
                                return v.scalar_tensor_tensor(
                                    out=acc[:, 3:T],
                                    in0=pad[:, c, 3 + i:3 + i + T - 3],
                                    scalar=cwsb[:, c,
                                                conv_j * 4 + i:conv_j * 4 + i + 1],
                                    in1=acc[:, 3:T], op0=ALU.mult, op1=ALU.add)

                        @B.step("scalar")
                        def _(s, c=c):
                            return s.activation(out=dst[:, c, 3:T],
                                                in_=acc[:, 3:T], func=AF.Silu)

                        if post_chunk is not None:
                            post_chunk(c)

            emit_proj_b(I["Wq"], 0, 0, out_q, "pq")
            emit_proj_b(I["Wk"], FC * 3, 1, out_k, "pk")

            vfirst_g = nc.sbuf_tensor("vfirst", [128, FC, 128], BF16)
            vfirst = mge(vfirst_g)
            vch_g = nc.sbuf_tensor("vch", [128, T], BF16)
            vch = mge(vch_g)
            pvt_g = nc.psum_tensor("pvt", [128, 512], BF16)
            pvt = mge(pvt_g)

            vdst = vch[None] if False else None

            class _VD:
                def __getitem__(self, idx):
                    return vch[:, idx[2]]

            def v_post_chunk(ec):
                @B.step("vector")
                def _(v, ec=ec):
                    return v.tensor_copy(vfirst[:, ec, 3:128], vch[:, 3:128])

                for g4, nk in ((0, 4), (1, 3)):
                    @B.step("tensor")
                    def _(t, ec=ec, g4=g4, nk=nk):
                        last = None
                        for k in range(nk):
                            cb = 1 + g4 * 4 + k
                            last = t.transpose(pvt[:, ts(k, 128)],
                                               vch[:, ts(cb, 128)], c_identb[:])
                        return last

                    @B.step("vector")
                    def _(v, ec=ec, g4=g4, nk=nk):
                        last = None
                        for k in range(nk):
                            cb = 1 + g4 * 4 + k
                            last = v.tensor_copy(v_tm[:, cb, ts(ec, 128)],
                                                 pvt[:, ts(k, 128)])
                        return last

            emit_proj_b(I["Wv"], FC * 6, 2, _VD(), "pv", post_chunk=v_post_chunk)

            @B.step("gpsimd")
            def _(g):
                B.wait(g, "dw")
                B.dma(g, "dw", out=cc_h_in, in_=sendh[:])
                return None

            @B.step("gpsimd")
            def _(g):
                B.wait(g, "dw")
                inst = g.collective_compute(
                    "AllGather", ALU.bypass,
                    replica_groups=[[0, 1], [2, 3], [4, 5], [6, 7]],
                    ins=[cc_h_in.opt()], outs=[cc_h_out.opt()])
                inst.then_inc(B.semget("cc"))
                B.cnt["cc"] += 1
                return None

            @B.step("gpsimd")
            def _(g):
                B.wait(g, "dw")
                B.wait(g, "cc")
                B.dma(g, "dw", out=recvh[:], in_=cc_h_out[0:128, :])
                return None

            @B.step("vector")
            def _(v):
                B.wait(v, "dw")
                return v.tensor_scalar(out=recvh[:], in0=recvh[:],
                                       scalar1=c_mask[:, 0:1], scalar2=None,
                                       op0=ALU.mult)

            with (
                nc.sbuf_tensor("fpad", [128, 6], BF16) as fpad,
                nc.sbuf_tensor("facc", [128, 3], F32) as facc,
            ):
                for dstb, hoff, conv_j, isv in [(out_q, 0, 0, False),
                                                (out_k, FC * 3, 1, False),
                                                (vfirst, FC * 6, 2, True)]:
                    for c in range(FC):
                        @B.step("vector")
                        def _(v, hoff=hoff, c=c):
                            v.tensor_copy(fpad[:, 0:3],
                                          recvh[:, hoff + c * 3:hoff + (c + 1) * 3])
                            return v.tensor_copy(
                                fpad[:, 3:6],
                                first3[:, hoff + c * 3:hoff + (c + 1) * 3])

                        @B.step("vector")
                        def _(v, c=c, conv_j=conv_j):
                            return v.tensor_scalar(
                                out=facc[:, :], in0=fpad[:, 3:6],
                                scalar1=cwsb[:, c, conv_j * 4 + 3:conv_j * 4 + 4],
                                scalar2=None, op0=ALU.mult)

                        for i in range(3):
                            @B.step("vector")
                            def _(v, c=c, i=i, conv_j=conv_j):
                                return v.scalar_tensor_tensor(
                                    out=facc[:, :], in0=fpad[:, i:i + 3],
                                    scalar=cwsb[:, c, conv_j * 4 + i:
                                                conv_j * 4 + i + 1],
                                    in1=facc[:, :], op0=ALU.mult, op1=ALU.add)

                        @B.step("scalar")
                        def _(s, dstb=dstb, c=c, isv=isv):
                            if isv:
                                return s.activation(out=dstb[:, c, 0:3],
                                                    in_=facc[:, :], func=AF.Silu)
                            return s.activation(out=dstb[:, c, 0:3],
                                                in_=facc[:, :], func=AF.Silu)

            for ec in range(FC):
                @B.step("tensor")
                def _(t, ec=ec):
                    return t.transpose(pvt[:, 0:128], vfirst[:, ec, :],
                                       c_identb[:])

                @B.step("vector")
                def _(v, ec=ec):
                    return v.tensor_copy(v_tm[:, 0, ts(ec, 128)], pvt[:, 0:128])

            mgx(pvt_g)
            mgx(vch_g)
            mgx(vfirst_g)

            with (
                nc.sbuf_tensor("rcos", [128, T], F32) as rcos,
                nc.sbuf_tensor("rsin", [128, T], F32) as rsin,
                nc.sbuf_tensor("rt1", [128, T], F32) as rt1,
                nc.sbuf_tensor("rt2", [128, T], F32) as rt2,
                nc.sbuf_tensor("rt3", [128, T], F32) as rt3,
                nc.sbuf_tensor("rt4", [128, T], F32) as rt4,
            ):
                @B.step("sync")
                def _(s):
                    B.dma(s, "dl", out=rcos[:], in_=I["cosT"])
                    B.dma(s, "dl", out=rsin[:], in_=I["sinT"])
                    return None

                first = [True]
                for buf in (out_q, out_k):
                    for h in range(HB):
                        c1, c2 = 2 * h, 2 * h + 1

                        @B.step("vector")
                        def _(v, buf=buf, c1=c1, c2=c2):
                            if first[0]:
                                B.wait(v, "dl")
                                first[0] = False
                            v.tensor_tensor(out=rt1[:], in0=buf[:, c1, :],
                                            in1=rcos[:], op=ALU.mult)
                            v.tensor_tensor(out=rt2[:], in0=buf[:, c2, :],
                                            in1=rsin[:], op=ALU.mult)
                            v.tensor_tensor(out=rt3[:], in0=buf[:, c2, :],
                                            in1=rcos[:], op=ALU.mult)
                            return v.tensor_tensor(out=rt4[:], in0=buf[:, c1, :],
                                                   in1=rsin[:], op=ALU.mult)

                        @B.step("vector")
                        def _(v, buf=buf, c1=c1, c2=c2):
                            v.tensor_tensor(out=buf[:, c1, :], in0=rt1[:],
                                            in1=rt2[:], op=ALU.subtract)
                            return v.tensor_tensor(out=buf[:, c2, :], in0=rt3[:],
                                                   in1=rt4[:], op=ALU.add)

            if _DEV_STOP == "bqkv":
                sl = [(out_q[:, c, :], c * 128, 1024) for c in range(8)]
                sl += [(out_k[:, c, :], 1024 + c * 128, 1024) for c in range(8)]
                _dev_dump(B, y, sl)
                raise _DevStop()
            if _DEV_STOP == "bqkv2":
                sl = [(v_tm[:, cb, 0:1024], cb * 128, 1024) for cb in range(8)]
                sl += [(v_tm[:, cb, 1024:2048], 1024 + cb * 128, 1024)
                       for cb in range(8)]
                _dev_dump(B, y, sl)
                raise _DevStop()

            o_fm_g = nc.sbuf_tensor("o_fm", [128, FC, T], BF16, side="right")
            o_fm = mge(o_fm_g)

            with (
                nc.sbuf_tensor("dm_t", [128, CB], BF16) as dm_t,
                nc.sbuf_tensor("qd_t", [128, CB], F32) as qd_t,
                nc.sbuf_tensor("dk_col", [128, 1], F32) as dk_col,
                nc.sbuf_tensor("qdih", [128, 2, T], BF16) as qdih,
                nc.sbuf_tensor("S_h", [128, 2, 256], F32) as S_h,
                nc.sbuf_tensor("S_hbf", [128, 2, 256], BF16) as S_hbf,
                nc.sbuf_tensor("b_ktm_sb", [128, 256], BF16) as ktm_sb,
                nc.sbuf_tensor("b_sco_b", [128, CB], BF16) as sco_b,
                nc.psum_tensor("b_pktm", [128, 256], BF16) as pktm,
                nc.psum_tensor("b_psc", [128, CB], F32) as psc,
                nc.psum_tensor("po_b", [128, 256], F32) as po_b,
                nc.psum_tensor("pinc", [128, 2, 256], F32) as pinc,
            ):
                for h in range(HB):
                    @B.step("sync")
                    def _(s, h=h):
                        B.dma(s, "dl", out=dm_t[:], in_=I["dmT"][h])
                        B.dma(s, "dl", out=qd_t[:], in_=I["qdi"][h])
                        B.dma(s, "dl", out=dk_col[:], in_=I["dkc"][h])
                        return None

                    @B.step("vector")
                    def _(v, h=h):
                        B.wait(v, "dl")
                        B.wait(v, "dw")
                        v.memset(S_h[:], 0.0)
                        v.memset(S_hbf[:], 0.0)
                        last = None
                        for dc in range(2):
                            last = v.tensor_tensor(
                                out=ap3(qdih, dc * T,
                                        [[2 * T, 128], [CB, NCB], [1, CB]]),
                                in0=ap3(out_q, (2 * h + dc) * T,
                                        [[FC * T, 128], [CB, NCB], [1, CB]]),
                                in1=ap3(qd_t, 0, [[CB, 128], [0, NCB], [1, CB]]),
                                op=ALU.mult)
                        return last

                    for c in range(NCB):
                        cs = slice(c * CB, (c + 1) * CB)

                        @B.step("tensor")
                        def _(t, h=h, cs=cs):
                            last = None
                            for dc in range(2):
                                last = t.transpose(pktm[:, ts(dc, 128)],
                                                   out_k[:, 2 * h + dc, cs],
                                                   c_identb[:])
                            return last

                        @B.step("vector")
                        def _(v):
                            return v.tensor_scalar(out=ktm_sb[:], in0=pktm[:, :],
                                                   scalar1=dk_col[:, 0:1],
                                                   scalar2=None, op0=ALU.mult)

                        @B.step("tensor")
                        def _(t, h=h, cs=cs):
                            last = None
                            for dc in range(2):
                                last = t.matmul(psc[:, :],
                                                out_k[:, 2 * h + dc, cs],
                                                out_q[:, 2 * h + dc, cs],
                                                start=(dc == 0), stop=(dc == 1))
                            return last

                        @B.step("vector")
                        def _(v):
                            return v.tensor_tensor(out=sco_b[:], in0=psc[:, :],
                                                   in1=dm_t[:], op=ALU.mult)

                        @B.step("tensor")
                        def _(t, h=h, c=c, cs=cs):
                            for ec in range(2):
                                t.matmul(po_b[:, ts(ec, 128)],
                                         v_tm[:, c, h * 256 + ec * 128:
                                              h * 256 + (ec + 1) * 128],
                                         sco_b[:], start=True, stop=False)
                                t.matmul(po_b[:, ts(ec, 128)],
                                         S_hbf[:, 0, ts(ec, 128)],
                                         qdih[:, 0, cs], start=False, stop=False)
                                t.matmul(po_b[:, ts(ec, 128)],
                                         S_hbf[:, 1, ts(ec, 128)],
                                         qdih[:, 1, cs], start=False, stop=True)
                            last = None
                            for dc in range(2):
                                last = t.matmul(pinc[:, dc, :],
                                                ktm_sb[:, ts(dc, 128)],
                                                v_tm[:, c, h * 256:(h + 1) * 256],
                                                start=True, stop=True)
                            return last

                        @B.step("vector")
                        def _(v, h=h, c=c):
                            return v.tensor_copy(
                                ap3(o_fm, 2 * h * T + c * CB,
                                    [[FC * T, 128], [T, 2], [1, CB]]),
                                ap3(po_b, 0, [[256, 128], [CB, 2], [1, CB]]))

                        @B.step("vector")
                        def _(v, h=h):
                            last = None
                            for dc in range(2):
                                last = v.scalar_tensor_tensor(
                                    out=S_h[:, dc, :], in0=S_h[:, dc, :],
                                    scalar=GAMMA[h] ** CB, in1=pinc[:, dc, :],
                                    op0=ALU.mult, op1=ALU.add)
                            return last

                        @B.step("vector")
                        def _(v):
                            return v.tensor_copy(S_hbf[:, :, :], S_h[:, :, :])

                    @B.step("gpsimd")
                    def _(g, h=h):
                        B.wait(g, "dw")
                        B.dma(g, "dw", out=cc_b_in[:, h * 512:(h + 1) * 512],
                              in_=S_h[:, :, :])
                        return None

            mgx(first3_g)
            mgx(recvh_g)
            mgx(sendh_g)
            mgx(v_tm_g)
            mgx(out_k_g)

            with (
                nc.sbuf_tensor("b_S_in", [128, HB * 512], F32) as S_in,
                nc.sbuf_tensor("b_S_inb", [128, HB * 512], BF16) as S_inb,
                nc.sbuf_tensor("qfx_f", [128, T], F32) as qfx_f,
                nc.sbuf_tensor("qfb", [128, 2, T], BF16) as qfb,
                nc.psum_tensor("b_psfx", [128, 512], F32) as psfx,
            ):
                @B.step("gpsimd")
                def _(g):
                    B.wait(g, "dw")
                    inst = g.collective_compute(
                        "AllGather", ALU.bypass,
                        replica_groups=[[0, 1], [2, 3], [4, 5], [6, 7]],
                        ins=[cc_b_in.opt()], outs=[cc_b_out.opt()])
                    inst.then_inc(B.semget("cc"))
                    B.cnt["cc"] += 1
                    return None

                @B.step("gpsimd")
                def _(g):
                    B.wait(g, "dw")
                    B.wait(g, "cc")
                    B.dma(g, "dw", out=S_in[:], in_=cc_b_out[0:128, :])
                    return None

                @B.step("vector")
                def _(v):
                    B.wait(v, "dw")
                    return v.tensor_scalar(out=S_inb[:], in0=S_in[:],
                                           scalar1=c_mask[:, 0:1], scalar2=None,
                                           op0=ALU.mult)

                for h in range(HB):
                    @B.step("sync")
                    def _(s, h=h):
                        B.dma(s, "dl", out=qfx_f[:], in_=I["qfix"][h])
                        return None

                    @B.step("vector")
                    def _(v, h=h):
                        B.wait(v, "dl")
                        last = None
                        for dc in range(2):
                            last = v.tensor_tensor(out=qfb[:, dc, :],
                                                   in0=out_q[:, 2 * h + dc, :],
                                                   in1=qfx_f[:], op=ALU.mult)
                        return last

                    for ec in range(2):
                        for tt in range(2):
                            @B.step("tensor")
                            def _(t, h=h, ec=ec, tt=tt):
                                last = None
                                for dc in range(2):
                                    last = t.matmul(
                                        psfx[:, :],
                                        S_inb[:, (h * 2 + dc) * 256 +
                                              ec * 128:(h * 2 + dc) * 256 +
                                              (ec + 1) * 128],
                                        qfb[:, dc, ts(tt, 512)],
                                        start=(dc == 0), stop=(dc == 1))
                                return last

                            @B.step("vector")
                            def _(v, h=h, ec=ec, tt=tt):
                                c = 2 * h + ec
                                return v.tensor_tensor(
                                    out=o_fm[:, c, ts(tt, 512)],
                                    in0=o_fm[:, c, ts(tt, 512)],
                                    in1=psfx[:, :], op=ALU.add)

            mgx(out_q_g)

            if _DEV_STOP == "bo":
                sl = [(o_fm[:, c, :], c * 128, 1024) for c in range(FC)]
                _dev_dump(B, y, sl)
                raise _DevStop()

            g_fm_g = nc.sbuf_tensor("g_fm", [128, FC, T], BF16)
            g_fm = mge(g_fm_g)
            with (
                nc.sbuf_tensor("gw_st", [128, FC, 512], BF16) as gw_st,
                nc.sbuf_tensor("gw_cs", [1, 512], BF16) as gw_cs,
                nc.psum_tensor("gpj", [128, 512]) as gpj,
                nc.psum_tensor("gpc", [1, 512]) as gpc,
            ):
                for blki in range(4):
                    w_mark = [0]

                    @B.step("gpsimd")
                    def _(g, blki=blki, w_mark=w_mark):
                        B.wait(g, "dw")
                        for c in range(FC):
                            B.dma(g, "dw", out=gw_st[:, c, :],
                                  in_=I["Wg"][ts(c, 128),
                                              blki * 512:(blki + 1) * 512])
                        w_mark[0] = B.cnt["dw"]
                        return None

                    @B.step("tensor")
                    def _(t, w_mark=w_mark):
                        B.wait(t, "dw", w_mark[0])
                        last = None
                        for c in range(FC):
                            last = t.matmul(gpc[:, :], anw_bf[:, c:c + 1],
                                            gw_st[:, c, :],
                                            start=(c == 0), stop=(c == FC - 1))
                        return last

                    @B.step("vector")
                    def _(v):
                        return v.tensor_copy(gw_cs[:, :], gpc[:, :])

                    for dc in range(4):
                        fc_out = blki * 4 + dc
                        for tt in range(2):
                            @B.step("tensor")
                            def _(t, dc=dc, tt=tt):
                                for c in range(FC):
                                    t.matmul(gpj[:, :], gw_st[:, c, ts(dc, 128)],
                                             x2w[:, c, ts(tt, 512)],
                                             start=(c == 0), stop=False)
                                return t.matmul(gpj[:, :], gw_cs[0:1, ts(dc, 128)],
                                                bs_negm_bf[0:1, ts(tt, 512)],
                                                start=False, stop=True)

                            @B.step("vector")
                            def _(v, fc_out=fc_out, tt=tt):
                                return v.tensor_tensor(
                                    out=g_fm[:, fc_out, ts(tt, 512)],
                                    in0=gpj[:, :], in1=c3_b[:, ts(tt, 512)],
                                    op=ALU.mult)

            og = x2w
            with (
                nc.sbuf_tensor("gnw_sb", [128, 2], F32) as gnw_sb,
                nc.sbuf_tensor("go_sq", [128, T], BF16) as go_sq,
                nc.sbuf_tensor("go_row", [1, T], F32) as go_row,
                nc.sbuf_tensor("go_b", [128, T], F32) as go_b,
                nc.sbuf_tensor("go_sil", [128, T], BF16) as go_sil,
                nc.psum_tensor("gp0", [1, 512]) as gp0,
                nc.psum_tensor("gp1", [1, 512]) as gp1,
                nc.psum_tensor("gpb", [128, T]) as gpb,
            ):
                @B.step("sync")
                def _(s):
                    B.dma(s, "dl", out=gnw_sb[:], in_=I["gnw"])
                    return None

                for h in range(HB):
                    for ec in range(2):
                        @B.step("scalar")
                        def _(s, h=h, ec=ec):
                            return s.activation(out=go_sq[:],
                                                in_=o_fm[:, 2 * h + ec, :],
                                                func=AF.Square)

                        @B.step("tensor")
                        def _(t, ec=ec):
                            t.matmul(gp0[:, :], onesb[:], go_sq[:, 0:512],
                                     start=(ec == 0), stop=(ec == 1))
                            return t.matmul(gp1[:, :], onesb[:], go_sq[:, 512:1024],
                                            start=(ec == 0), stop=(ec == 1))

                    @B.step("vector")
                    def _(v):
                        v.tensor_scalar(out=go_row[0:1, 0:512], in0=gp0[:, :],
                                        scalar1=1.0 / 256.0, scalar2=EPS,
                                        op0=ALU.mult, op1=ALU.add)
                        return v.tensor_scalar(out=go_row[0:1, 512:1024],
                                               in0=gp1[:, :],
                                               scalar1=1.0 / 256.0, scalar2=EPS,
                                               op0=ALU.mult, op1=ALU.add)

                    @B.step("vector")
                    def _(v):
                        return v.reciprocal(out=go_row[:], in_=go_row[:])

                    @B.step("scalar")
                    def _(s):
                        return s.activation(out=go_row[:], in_=go_row[:],
                                            func=AF.Sqrt)

                    @B.step("tensor")
                    def _(t):
                        t.matmul(gpb[:, 0:512], onesf[:], go_row[0:1, 0:512],
                                 start=True, stop=True)
                        return t.matmul(gpb[:, 512:1024], onesf[:],
                                        go_row[0:1, 512:1024],
                                        start=True, stop=True)

                    for ec in range(2):
                        c = 2 * h + ec

                        @B.step("scalar")
                        def _(s, c=c):
                            return s.activation(out=go_sil[:], in_=g_fm[:, c, :],
                                                func=AF.Silu)

                        @B.step("vector")
                        def _(v, c=c, ec=ec):
                            B.wait(v, "dl")
                            return v.tensor_tensor(out=go_b[:],
                                                   in0=o_fm[:, c, :],
                                                   in1=gpb[:, :], op=ALU.mult)

                        @B.step("vector")
                        def _(v, ec=ec):
                            return v.tensor_scalar(out=go_b[:], in0=go_b[:],
                                                   scalar1=gnw_sb[:, ec:ec + 1],
                                                   scalar2=None, op0=ALU.mult)

                        @B.step("vector")
                        def _(v, c=c):
                            return v.tensor_tensor(out=og[:, c, :], in0=go_b[:],
                                                   in1=go_sil[:], op=ALU.mult)

            mgx(g_fm_g)
            mgx(o_fm_g)

            with (
                nc.sbuf_tensor("wo_st", [128, FC, 512], BF16) as wo_st,
                nc.sbuf_tensor("wo_x2", [128, 512], F32) as wo_x2,
                nc.sbuf_tensor("wo_te", [128, 512], F32) as wo_te,
                nc.psum_tensor("wo_pj", [128, 512]) as wo_pj,
            ):
                for blki in range(4):
                    w_mark = [0]

                    @B.step("gpsimd")
                    def _(g, blki=blki, w_mark=w_mark):
                        B.wait(g, "dw")
                        for c in range(FC):
                            B.dma(g, "dw", out=wo_st[:, c, :],
                                  in_=I["Wo"][ts(c, 128),
                                              blki * 512:(blki + 1) * 512])
                        w_mark[0] = B.cnt["dw"]
                        return None

                    for dc in range(4):
                        fc_out = blki * 4 + dc
                        for tt in range(2):
                            @B.step("sync")
                            def _(s, fc_out=fc_out, tt=tt):
                                B.dma(s, "dl", out=wo_x2[:],
                                      in_=x2scr[ts(fc_out, 128), ts(tt, 512)])
                                return None

                            @B.step("tensor")
                            def _(t, dc=dc, tt=tt, w_mark=w_mark):
                                B.wait(t, "dw", w_mark[0])
                                last = None
                                for c in range(FC):
                                    last = t.matmul(wo_pj[:, :],
                                                    wo_st[:, c, ts(dc, 128)],
                                                    og[:, c, ts(tt, 512)],
                                                    start=(c == 0),
                                                    stop=(c == FC - 1))
                                return last

                            @B.step("vector")
                            def _(v, fc_out=fc_out, tt=tt):
                                B.wait(v, "dl")
                                B.wait(v, "dh")
                                return v.tensor_tensor(
                                    out=wo_te[:], in0=wo_x2[:],
                                    in1=m3_b[:, ts(tt, 512)], op=ALU.subtract)

                            @B.step("vector")
                            def _(v, tt=tt):
                                return v.tensor_tensor(out=wo_te[:], in0=wo_te[:],
                                                       in1=r3_b[:, ts(tt, 512)],
                                                       op=ALU.mult)

                            @B.step("vector")
                            def _(v):
                                return v.tensor_tensor(out=wo_te[:], in0=wo_te[:],
                                                       in1=wo_pj[:, :], op=ALU.add)

                            @B.step("sync")
                            def _(s, fc_out=fc_out, tt=tt):
                                B.dma(s, "dh",
                                      out=hscr[ts(fc_out, 128), ts(tt, 512)],
                                      in_=wo_te[:])
                                return None

            mgx(x2w_g)

            if _DEV_STOP == "bh":
                @B.step("sync")
                def _(s):
                    B.wait(s, "dh")
                    B.dma(s, "do", out=y, in_=hscr)
                    return None

                @B.step("sync")
                def _(s):
                    B.wait(s, "do")
                    return None
                raise _DevStop()

            z_fm_g = nc.sbuf_tensor("z_fm", [128, 22, T], BF16)
            z_fm = mge(z_fm_g)
            gates_g = nc.sbuf_tensor("gates", [128, 22, T], BF16)
            gates = mge(gates_g)
            _mg1 = [nc.sbuf_tensor("y_bf", [128, FC, T], BF16),
                    nc.sbuf_tensor("mn_f", [128, FC], F32),
                    nc.sbuf_tensor("m_hst", [128, T], F32),
                    nc.sbuf_tensor("m_sq", [128, T], BF16),
                    nc.sbuf_tensor("m_row", [1, T], F32),
                    nc.sbuf_tensor("r4_b", [128, T], F32),
                    nc.psum_tensor("mp0", [1, 512]),
                    nc.psum_tensor("mp1", [1, 512]),
                    nc.psum_tensor("mpb", [128, T])]
            (y_bf, mn_f, m_hst, m_sq, m_row, r4_b, mp0, mp1, mpb) = [
                mge(g) for g in _mg1]
            if True:
                @B.step("sync")
                def _(s):
                    B.dma(s, "dl", out=mn_f[:], in_=I["mnw"])
                    return None

                for c in range(FC):
                    @B.step("sync")
                    def _(s, c=c):
                        if c == 0:
                            B.wait(s, "dh")
                        B.dma(s, "dl", out=m_hst[:], in_=hscr[ts(c, 128), :])
                        return None

                    @B.step("scalar")
                    def _(s, c=c):
                        B.wait(s, "dl")
                        return s.activation(out=m_sq[:], in_=m_hst[:],
                                            func=AF.Square)

                    @B.step("vector")
                    def _(v, c=c):
                        return v.tensor_scalar(out=y_bf[:, c, :], in0=m_hst[:],
                                               scalar1=mn_f[:, c:c + 1],
                                               scalar2=None, op0=ALU.mult)

                    @B.step("tensor")
                    def _(t, c=c):
                        f, l = (c == 0), (c == FC - 1)
                        t.matmul(mp0[:, :], onesb[:], m_sq[:, 0:512],
                                 start=f, stop=l)
                        return t.matmul(mp1[:, :], onesb[:], m_sq[:, 512:1024],
                                        start=f, stop=l)

                @B.step("vector")
                def _(v):
                    v.tensor_scalar(out=m_row[0:1, 0:512], in0=mp0[:, :],
                                    scalar1=1.0 / D, scalar2=EPS,
                                    op0=ALU.mult, op1=ALU.add)
                    return v.tensor_scalar(out=m_row[0:1, 512:1024], in0=mp1[:, :],
                                           scalar1=1.0 / D, scalar2=EPS,
                                           op0=ALU.mult, op1=ALU.add)

                @B.step("vector")
                def _(v):
                    return v.reciprocal(out=m_row[:], in_=m_row[:])

                @B.step("scalar")
                def _(s):
                    return s.activation(out=m_row[:], in_=m_row[:], func=AF.Sqrt)

                @B.step("tensor")
                def _(t):
                    t.matmul(mpb[:, 0:512], onesf[:], m_row[0:1, 0:512],
                             start=True, stop=True)
                    return t.matmul(mpb[:, 512:1024], onesf[:],
                                    m_row[0:1, 512:1024], start=True, stop=True)

                @B.step("vector")
                def _(v):
                    return v.tensor_copy(r4_b[:, :], mpb[:, :])

                for c in range(FC):
                    @B.step("vector")
                    def _(v, c=c):
                        return v.tensor_tensor(out=y_bf[:, c, :],
                                               in0=y_bf[:, c, :], in1=r4_b[:, :],
                                               op=ALU.mult)

            for g in reversed(_mg1[1:6]):
                mgx(g)
            _mg2 = [nc.sbuf_tensor("mg_st", [128, FC, 512], BF16),
                    nc.psum_tensor("mg_pj", [128, 512])]
            (mg_st, mg_pj) = [mge(g) for g in _mg2]
            if True:
                for blki in range(11):
                    w_mark = [0]

                    @B.step("gpsimd")
                    def _(g, blki=blki, w_mark=w_mark):
                        B.wait(g, "dw")
                        for c in range(FC):
                            B.dma(g, "dw", out=mg_st[:, c, :],
                                  in_=I["Wgate"][ts(c, 128),
                                                 blki * 512:(blki + 1) * 512])
                        w_mark[0] = B.cnt["dw"]
                        return None

                    for dc in range(4):
                        hc = blki * 4 + dc
                        for tt in range(2):
                            @B.step("tensor")
                            def _(t, dc=dc, tt=tt, w_mark=w_mark):
                                B.wait(t, "dw", w_mark[0])
                                last = None
                                for c in range(FC):
                                    last = t.matmul(mg_pj[:, :],
                                                    mg_st[:, c, ts(dc, 128)],
                                                    y_bf[:, c, ts(tt, 512)],
                                                    start=(c == 0),
                                                    stop=(c == FC - 1))
                                return last

                            if hc < 22:
                                @B.step("scalar")
                                def _(s, hc=hc, tt=tt):
                                    return s.activation(
                                        out=gates[:, hc, ts(tt, 512)],
                                        in_=mg_pj[:, :], func=AF.Silu)
                            else:
                                @B.step("vector")
                                def _(v, hc=hc, tt=tt):
                                    return v.tensor_tensor(
                                        out=z_fm[:, hc - 22, ts(tt, 512)],
                                        in0=gates[:, hc - 22, ts(tt, 512)],
                                        in1=mg_pj[:, :], op=ALU.mult)

            mgx(_mg2[1])
            mgx(_mg2[0])
            mgx(_mg1[0])
            mgx(gates_g)
            xres2_g = nc.sbuf_tensor("xres2", [128, FC, T], F32, side="right")
            xres2 = mge(xres2_g)
            _mg3 = [nc.sbuf_tensor("md_st", [128, 22, 512], BF16),
                    nc.sbuf_tensor("md_x2", [128, 512], F32),
                    nc.sbuf_tensor("md_h", [128, 512], F32),
                    nc.sbuf_tensor("md_te", [128, 512], F32),
                    nc.psum_tensor("md_pj", [128, 512])]
            (md_st, md_x2, md_h, md_te, md_pj) = [mge(g) for g in _mg3]
            if True:
                for blki in range(4):
                    w_mark = [0]

                    @B.step("gpsimd")
                    def _(g, blki=blki, w_mark=w_mark):
                        B.wait(g, "dw")
                        for kc in range(22):
                            B.dma(g, "dw", out=md_st[:, kc, :],
                                  in_=I["Wdown"][ts(kc, 128),
                                                 blki * 512:
                                                 (blki + 1) * 512])
                        w_mark[0] = B.cnt["dw"]
                        return None

                    for dc in range(4):
                        fc_out = blki * 4 + dc
                        for tt in range(2):
                            @B.step("sync")
                            def _(s, fc_out=fc_out, tt=tt):
                                B.dma(s, "dl", out=md_x2[:],
                                      in_=x2scr[ts(fc_out, 128),
                                                ts(tt, 512)])
                                B.dma(s, "dl", out=md_h[:],
                                      in_=hscr[ts(fc_out, 128),
                                               ts(tt, 512)])
                                return None

                            @B.step("tensor")
                            def _(t, dc=dc, tt=tt, w_mark=w_mark):
                                B.wait(t, "dw", w_mark[0])
                                last = None
                                for kc in range(22):
                                    last = t.matmul(
                                        md_pj[:, :],
                                        md_st[:, kc, ts(dc, 128)],
                                        z_fm[:, kc, ts(tt, 512)],
                                        start=(kc == 0), stop=(kc == 21))
                                return last

                            @B.step("vector")
                            def _(v):
                                B.wait(v, "dl")
                                return v.tensor_tensor(out=md_te[:],
                                                       in0=md_h[:],
                                                       in1=md_pj[:, :],
                                                       op=ALU.add)

                            @B.step("vector")
                            def _(v, fc_out=fc_out, tt=tt):
                                return v.tensor_tensor(
                                    out=xres2[:, fc_out, ts(tt, 512)],
                                    in0=md_te[:], in1=md_x2[:],
                                    op=ALU.add)

            for g in reversed(_mg3):
                mgx(g)
            for g in reversed(_mg1[6:9]):
                mgx(g)
            mgx(z_fm_g)

            if _DEV_STOP == "x3":
                sl = [(xres2[:, c, :], c * 128, 1024) for c in range(FC)]
                _dev_dump(B, y, sl)
                raise _DevStop()

            ffn_use("f2_", xres2)

            @B.step("sync")
            def _(s):
                for c in range(FC):
                    B.dma(s, "do", out=y[ts(c, 128), :], in_=xres2[:, c, :])
                return None

            @B.step("sync")
            def _(s):
                B.wait(s, "do")
                return None

      except _DevStop:
          pass
      mg_cleanup()
    return nc


# version 04
# speedup vs baseline: 1.3361x; 1.3361x over previous
import math
import numpy as np
import ml_dtypes
import concourse.bass as bass
import concourse.mybir as mybir
from concourse.bass_utils import run_bass_kernel_spmd

F32 = mybir.dt.float32
BF16 = mybir.dt.bfloat16
AF = mybir.ActivationFunctionType
ALU = mybir.AluOpType
AX = mybir.AxisListType

D = 2048
T = 1024
FC = D // 128
HA = 16
HDA = 128
BLK = 64
NCA = T // BLK
HB = 8
DKB = 256
CB = 128
NCB = T // CB
HID = 5632
HC = HID // 128
INTER = 2816
EPS = 1e-5
RNDC = 12582912.0
GAMMA = [1.0 - 2.0 ** (-5.0 - h) for h in range(HB)]

BF = ml_dtypes.bfloat16
_DEV_STOP = None


class _DevStop(Exception):
    pass


class Prog:

    def __init__(self, nc, block, sems):
        self.nc = nc
        self.block = block
        self.s = sems
        self.c = {k: 0 for k in sems}

    def w(self, eng, *names):
        for n in names:
            eng.wait_ge(self.s[n], self.c[n])

    def dma(self, eng, name, **kw):
        inst = eng.dma_start(**kw)
        inst.then_inc(self.s[name], 16)
        self.c[name] += 16
        return inst

    def inc(self, inst, name, by=1):
        inst.then_inc(self.s[name], by)
        self.c[name] += by
        return inst


def ts(i, n):
    return slice(i * n, (i + 1) * n)



def _host_constants():
    c = {}
    c["identb"] = np.eye(128, dtype=BF)
    c["identf"] = np.eye(128, dtype=np.float32)
    i64 = np.arange(BLK, dtype=np.float32)
    c["ndiffT"] = (i64[:, None] - i64[None, :]).astype(np.float32)
    c["triT"] = (i64[None, :] >= i64[:, None]).astype(np.float32)
    c["negi_col"] = (-i64)[:, None].astype(np.float32)
    c["im64_col"] = (i64 - 64)[:, None].astype(np.float32)
    c["negt_row"] = (-np.arange(T, dtype=np.float32))[None, :]
    iC = np.arange(CB, dtype=np.float32)
    dmT = np.zeros((HB, CB, CB), dtype=np.float32)
    for h in range(HB):
        g = GAMMA[h]
        M = np.where(iC[None, :] >= iC[:, None],
                     g ** (iC[None, :] - iC[:, None]), 0.0)
        dmT[h] = M / 16.0
    c["dmT"] = dmT.astype(BF)
    qdi = np.zeros((HB, 128, CB), dtype=np.float32)
    dkc = np.zeros((HB, CB, 1), dtype=np.float32)
    qfix = np.zeros((HB, 128, T), dtype=np.float32)
    for h in range(HB):
        g = GAMMA[h]
        qdi[h, :, :] = (g ** (iC + 1.0) / 16.0)[None, :]
        dkc[h, :, 0] = g ** (CB - 1.0 - iC)
        tt = np.arange(T, dtype=np.float32)
        qfix[h, :, :] = (g ** (tt + 1.0) / 16.0)[None, :]
    c["qdi"] = qdi
    c["dkc"] = dkc
    c["qfix"] = qfix
    return c


def _host_rope(t0):
    dk_half = DKB // 2
    inv = 1.0 / (10000.0 ** (np.arange(0, DKB, 2, dtype=np.float32) / DKB))
    pos = np.arange(t0, t0 + T, dtype=np.float32)
    fr = pos[:, None] * inv[None, :]
    return np.cos(fr).T.copy(), np.sin(fr).T.copy()


def _prep_inputs(inputs):
    c = _host_constants()
    f32 = np.float32
    W = {k: np.ascontiguousarray(v, dtype=f32) for k, v in inputs.items()}
    x_full = W["L0_attn_a_x"]
    maps = []
    shared = {
        "Wqkv": W["L0_attn_a_Wqkv"],
        "bqkv": W["L0_attn_a_bqkv"].reshape(48, 128).T.copy(),
        "bqkv_row": W["L0_attn_a_bqkv"][None, :].copy(),
        "Wout": W["L0_attn_a_Wout"],
        "bout": W["L0_attn_a_bout"].reshape(16, 128).T.copy(),
        "slopes_rep": np.tile(W["L0_attn_a_slopes"][None, :], (128, 1)),
        "anw": W["L0_attn_b_attn_norm_w"].reshape(16, 128).T.copy(),
        "Wq": W["L0_attn_b_Wq"], "Wk": W["L0_attn_b_Wk"],
        "Wv": W["L0_attn_b_Wv"], "Wg": W["L0_attn_b_Wg"],
        "Wo": W["L0_attn_b_Wo"],
        "cwq": W["L0_attn_b_conv_wq"].reshape(16, 128, 4).transpose(1, 0, 2).copy(),
        "cwk": W["L0_attn_b_conv_wk"].reshape(16, 128, 4).transpose(1, 0, 2).copy(),
        "cwv": W["L0_attn_b_conv_wv"].reshape(16, 128, 4).transpose(1, 0, 2).copy(),
        "gnw": W["L0_attn_b_gnorm_w"].reshape(2, 128).T.copy(),
        "mnw": W["L0_attn_b_mlp_norm_w"].reshape(16, 128).T.copy(),
        "Wgate": W["L0_attn_b_W_gate"],
        "Wdown": W["L0_attn_b_W_down"],
        "w1T": np.ascontiguousarray(W["L0_ffn_w1"].T),
        "w3T": np.ascontiguousarray(W["L0_ffn_w3"].T),
        "w2T": np.ascontiguousarray(W["L0_ffn_w2"].T),
    }
    shared.update({k: np.ascontiguousarray(v, dtype=f32) if v.dtype != BF
                   else v for k, v in c.items()})
    for core in range(8):
        b, half = core // 2, core % 2
        xs = x_full[b, half * T:(half + 1) * T, :]
        cosT, sinT = _host_rope(half * T)
        m = dict(shared)
        m["x0"] = np.ascontiguousarray(xs.T)
        m["mask"] = np.full((128, 1), float(half), f32)
        m["cosT"] = cosT
        m["sinT"] = sinT
        maps.append(m)
    return maps


def kernel(**inputs):
    maps = _prep_inputs(inputs)
    nc = build_nc()
    res = run_bass_kernel_spmd(nc, maps, core_ids=list(range(8)))
    out = np.zeros((4, 2048, D), dtype=np.float32)
    for core in range(8):
        b, half = core // 2, core % 2
        y = np.asarray(res.results[core]["y"], dtype=np.float32)
        out[b, half * T:(half + 1) * T, :] = y.T
    return out



class Build:

    def __init__(self, nc):
        self.nc = nc
        self.st = None
        self.blk = None
        self.sem = {}
        self.s = self.sem
        self.cnt = {}

    def semget(self, name):
        if name not in self.sem:
            self.sem[name] = self.st.enter_context(self.nc.semaphore(name))
            self.cnt[name] = 0
        return self.sem[name]

    def step(self, ename, dma_issue=False):
        ch = self.semget("ch")
        before = self.cnt["ch"]
        self.cnt["ch"] += 1
        after = self.cnt["ch"]
        blk = self.blk

        def deco(fn):
            @getattr(blk, ename)
            def _(e):
                e.wait_ge(ch, before)
                last = fn(e)
                if last is None:
                    last = e.nop()
                last.then_inc(ch, 1)
            return None
        return deco

    def astep(self, ename):
        ch = self.semget("ch")
        before = self.cnt["ch"]
        blk = self.blk

        def deco(fn):
            @getattr(blk, ename)
            def _(e):
                e.wait_ge(ch, before)
                fn(e)
            return None
        return deco

    def join(self, ename, *sems):
        names = sems

        def fn(e):
            for n in names:
                self.wait(e, n)
            return e.nop()
        self.step(ename)(fn)

    def dma(self, e, sem_name, **kw):
        s = self.semget(sem_name)
        e.dma_start(**kw).then_inc(s, 16)
        self.cnt[sem_name] += 16
        return self.cnt[sem_name]

    def wait(self, e, sem_name, val=None):
        s = self.semget(sem_name)
        e.wait_ge(s, self.cnt[sem_name] if val is None else val)


def ap3(t, off, dims):
    return bass.AP(t, off, dims)


def apv(t, rowlen, p0, pc, foff, dims):
    return bass.AP(t, p0 * rowlen + foff, [[rowlen, pc]] + dims)



def _dev_dump(B, y, slices):
    @B.step("gpsimd")
    def _(g):
        for ap, r0, ncol in slices:
            B.dma(g, "do", out=y[r0:r0 + 128, 0:ncol], in_=ap)
        return None

    @B.step("gpsimd")
    def _(g):
        B.wait(g, "do")
        return None


def build_nc():
    import contextlib
    nc = bass.Bass("TRN2", target_bir_lowering=False)

    I = {}
    for name, shape in [
        ("x0", [D, T]), ("mask", [128, 1]),
        ("Wqkv", [D, 3 * D]), ("bqkv", [128, 48]), ("bqkv_row", [1, 3 * D]),
        ("Wout", [D, D]), ("bout", [128, FC]), ("slopes_rep", [128, HA]),
        ("anw", [128, FC]), ("Wq", [D, D]), ("Wk", [D, D]), ("Wv", [D, D]),
        ("Wg", [D, D]), ("Wo", [D, D]),
        ("cwq", [128, FC, 4]), ("cwk", [128, FC, 4]), ("cwv", [128, FC, 4]),
        ("gnw", [128, 2]), ("mnw", [128, FC]),
        ("Wgate", [D, HID]), ("Wdown", [INTER, D]),
        ("w1T", [D, HID]), ("w3T", [D, HID]), ("w2T", [HID, D]),
        ("identf", [128, 128]), ("ndiffT", [BLK, BLK]), ("triT", [BLK, BLK]),
        ("negi_col", [BLK, 1]), ("im64_col", [BLK, 1]), ("negt_row", [1, T]),
        ("qdi", [HB, 128, CB]), ("dkc", [HB, CB, 1]), ("qfix", [HB, 128, T]),
        ("cosT", [128, T]), ("sinT", [128, T]),
    ]:
        I[name] = nc.dram_tensor(name, shape, F32, kind="ExternalInput").ap()
    for name, shape in [("identb", [128, 128]), ("dmT", [HB, CB, CB])]:
        I[name] = nc.dram_tensor(name, shape, BF16, kind="ExternalInput").ap()
    y = nc.dram_tensor("y", [D, T], F32, kind="ExternalOutput").ap()

    cc_a_in = nc.dram_tensor("cc_a_in", [128, HA * 128], F32).ap()
    cc_a_out = nc.dram_tensor("cc_a_out", [256, HA * 128], F32).ap()
    cc_h_in = nc.dram_tensor("cc_h_in", [128, FC * 9], BF16).ap()
    cc_h_out = nc.dram_tensor("cc_h_out", [256, FC * 9], BF16).ap()
    cc_b_in = nc.dram_tensor("cc_b_in", [128, HB * 2 * 256], F32).ap()
    cc_b_out = nc.dram_tensor("cc_b_out", [256, HB * 2 * 256], F32).ap()
    x2scr = nc.dram_tensor("x2scr", [D, T], F32).ap()
    hscr = nc.dram_tensor("hscr", [D, T], F32).ap()
    oscr = nc.dram_tensor("oscr", [D, T], BF16).ap()
    tmpscr = nc.dram_tensor("tmpscr", [T, HID], F32).ap()
    ternd = {1: nc.dram_tensor("tern1", [D, HID], BF16).ap(),
             3: nc.dram_tensor("tern3", [D, HID], BF16).ap(),
             2: nc.dram_tensor("tern2", [HID, D], BF16).ap()}

    B = Build(nc)
    MG = []

    def mge(g):
        MG.append(g)
        return g.__enter__()

    def mgx(g):
        MG.remove(g)
        g.__exit__(None, None, None)

    def mg_cleanup():
        for g in reversed(MG):
            g.__exit__(None, None, None)
        MG.clear()

    with contextlib.ExitStack() as st:
      try:
            B.st = st
            sbuf = lambda name, shape, dt=F32, side="left": st.enter_context(
                nc.sbuf_tensor(name, shape, dt, side=side))
            psum = lambda name, shape, dt=F32: st.enter_context(
                nc.psum_tensor(name, shape, dt))

            c_identb = sbuf("c_identb", [128, 128], BF16)
            c_identf = sbuf("c_identf", [128, 128])
            c_slopes = sbuf("c_slopes", [128, HA])
            c_mask = sbuf("c_mask", [128, 1])
            onesf = sbuf("onesf", [1, 128])
            onesbr = sbuf("onesbr", [1, 128], BF16)
            onesb = sbuf("onesb", [128, 1], BF16)
            _tgs = []
            for _n, _s, _d in [("c_ndiffT", [BLK, BLK], F32),
                               ("c_triT", [BLK, BLK], F32),
                               ("c_im64", [BLK, 1], F32),
                               ("c_negt", [1, T], F32),
                               ("t_maskT", [BLK, HA, BLK], F32),
                               ("t_qd", [128, HA, BLK], BF16),
                               ("t_kd", [BLK, HA], F32),
                               ("t_bd", [128, HA], F32)]:
                _g = nc.sbuf_tensor(_n, _s, _d)
                _tgs.append(_g)
                MG.append(_g)
                _h = _g.__enter__()
                if _n == "c_ndiffT":
                    c_ndiffT = _h
                elif _n == "c_triT":
                    c_triT = _h
                elif _n == "c_im64":
                    c_im64 = _h
                elif _n == "c_negt":
                    c_negt = _h
                elif _n == "t_maskT":
                    t_maskT = _h
                elif _n == "t_qd":
                    t_qd = _h
                elif _n == "t_kd":
                    t_kd = _h
                elif _n == "t_bd":
                    t_bd = _h

            B.blk = st.enter_context(nc.Block())
            B.semget("ch")


            def pinc(inst, name):
                inst.then_inc(B.semget(name), 1)
                B.cnt[name] += 1
                return inst

            def fm_proj_island(wdram, col0, nblocks, wst, csum2, pjs, pcs,
                               kchunks, rhs_src, colsum_lhsT, negm_row,
                               evac1, evac2=None, bw=512):
                ev = B.semget("ev")
                eb = B.semget("eb")
                wmark = [0] * nblocks
                blk_end_ch = [0] * (nblocks + 2)
                ntile = [0]
                ev0 = B.cnt["ev"]

                def load(blki):
                    @B.astep("gpsimd")
                    def _(g):
                        B.wait(g, "dw")
                        if blki >= 2:
                            g.wait_ge(B.s["ch"], blk_end_ch[blki - 2])
                        for c in range(kchunks):
                            B.dma(g, "dw", out=wst[:, blki % 2, c, 0:bw],
                                  in_=wdram[ts(c, 128),
                                            col0 + blki * bw:
                                            col0 + (blki + 1) * bw])
                        wmark[blki] = B.cnt["dw"]

                load(0)
                if nblocks > 1:
                    load(1)
                for blki in range(nblocks):
                    slot = blki % 2
                    if csum2 is not None:
                        @B.step("tensor")
                        def _(t, blki=blki, slot=slot, ebp=B.cnt["eb"]):
                            B.wait(t, "dw", wmark[blki])
                            t.wait_ge(eb, ebp)
                            last = None
                            for c in range(kchunks):
                                last = t.matmul(pcs[:, 0:bw], colsum_lhsT(c),
                                                wst[:, slot, c, 0:bw],
                                                start=(c == 0),
                                                stop=(c == kchunks - 1))
                            return last
                        csum_ch = B.cnt["ch"]
                        eb_after = B.cnt["eb"] + 1

                        @B.astep("vector")
                        def _(v, slot=slot, csum_ch=csum_ch):
                            v.wait_ge(B.s["ch"], csum_ch)
                            pinc(v.tensor_copy(csum2[0:1, slot, 0:bw],
                                               pcs[:, 0:bw]), "eb")

                    for dc in range(bw // 128):
                        fc_out = blki * (bw // 128) + dc
                        for tt in range(2):
                            j = ntile[0]
                            ntile[0] += 1
                            pj = pjs[j % 2]

                            @B.step("tensor")
                            def _(t, blki=blki, slot=slot, dc=dc, tt=tt, j=j,
                                  pj=pj, ebw=(B.cnt["eb"] if csum2 is not None
                                              else 0)):
                                B.wait(t, "dw", wmark[blki])
                                if j >= 2:
                                    t.wait_ge(ev, ev0 + j - 1)
                                if csum2 is not None and dc == 0 and tt == 0:
                                    t.wait_ge(eb, ebw)
                                last = None
                                for c in range(kchunks):
                                    last = t.matmul(
                                        pj[:, :], wst[:, slot, c, ts(dc, 128)],
                                        rhs_src(c, tt), start=(c == 0),
                                        stop=(c == kchunks - 1 and
                                              negm_row is None))
                                if negm_row is not None:
                                    last = t.matmul(
                                        pj[:, :], csum2[0:1, slot, ts(dc, 128)],
                                        negm_row[0:1, ts(tt, 512)],
                                        start=False, stop=True)
                                return last
                            tile_ch = B.cnt["ch"]
                            if dc == bw // 128 - 1 and tt == 1:
                                blk_end_ch[blki] = tile_ch

                            @B.astep("vector")
                            def _(v, fc_out=fc_out, tt=tt, pj=pj,
                                  tile_ch=tile_ch):
                                v.wait_ge(B.s["ch"], tile_ch)
                                pinc(evac1(v, fc_out, tt, pj), "ev")

                            if evac2 is not None:
                                @B.astep("vector")
                                def _(v, fc_out=fc_out, tt=tt, j=j):
                                    v.wait_ge(ev, ev0 + j + 1)
                                    pinc(evac2(v, fc_out, tt), "eb")

                    if blki + 2 < nblocks:
                        load(blki + 2)

                B.join("vector", "ev", "eb")
                B.join("gpsimd", "dw")

            @B.step("sync")
            def _(s):
                for dst, src in [
                    (c_identb, I["identb"]), (c_identf, I["identf"]),
                    (c_slopes, I["slopes_rep"]), (c_mask, I["mask"]),
                    (c_ndiffT, I["ndiffT"]), (c_triT, I["triT"]),
                    (c_im64, I["im64_col"]), (c_negt, I["negt_row"]),
                ]:
                    B.dma(s, "dl", out=dst[:], in_=src)

            @B.step("vector")
            def _(v):
                v.memset(onesf[:], 1.0)
                v.memset(onesbr[:], 1.0)
                return v.memset(onesb[:], 1.0)

            @B.step("scalar")
            def _(s):
                B.wait(s, "dl")
                for h in range(HA):
                    sc = c_slopes[0:BLK, h:h + 1]
                    s.activation(out=t_maskT[:, h, :], in_=c_ndiffT[:],
                                 func=AF.Exp, scale=sc)
                last = None
                for h in range(HA):
                    sc = c_slopes[0:BLK, h:h + 1]
                    last = s.activation(out=t_kd[:, h:h + 1], in_=c_im64[:],
                                        func=AF.Exp, scale=sc)
                return last

            @B.step("vector")
            def _(v):
                last = None
                for h in range(HA):
                    last = v.tensor_tensor(out=t_maskT[:, h, :],
                                           in0=t_maskT[:, h, :],
                                           in1=c_triT[:], op=ALU.mult)
                return last

            with (
                nc.sbuf_tensor("p0_rows", [1, HA * BLK], F32) as p0_rows,
                nc.sbuf_tensor("p0_bdr", [1, HA], F32) as p0_bdr,
                nc.psum_tensor("p0_ps", [128, 512]) as p0_ps,
                nc.psum_tensor("p0_ps2", [128, HA]) as p0_ps2,
            ):
                @B.step("scalar")
                def _(s):
                    for h in range(HA):
                        s.activation(out=p0_rows[0:1, h * BLK:(h + 1) * BLK],
                                     in_=c_negt[0:1, 0:BLK], func=AF.Exp,
                                     scale=c_slopes[0:1, h:h + 1])
                    return s.activation(out=p0_bdr[:], in_=c_slopes[0:1, :],
                                        func=AF.Exp, scale=-64.0)

                for g in range(2):
                    @B.step("tensor")
                    def _(t, g=g):
                        return t.matmul(p0_ps[:, :], onesf[:],
                                        p0_rows[0:1, g * 512:(g + 1) * 512],
                                        start=True, stop=True)

                    @B.step("vector")
                    def _(v, g=g):
                        return v.tensor_copy(
                            ap3(t_qd, g * 8 * BLK, [[HA * BLK, 128], [1, 512]]),
                            p0_ps[:, :])

                @B.step("tensor")
                def _(t):
                    return t.matmul(p0_ps2[:, :], onesf[:], p0_bdr[:],
                                    start=True, stop=True)

                @B.step("vector")
                def _(v):
                    return v.tensor_copy(t_bd[:, :], p0_ps2[:, :])

            def alloc(name, shape, dt=F32):
                g = nc.sbuf_tensor(name, shape, dt)
                return g, g.__enter__()

            _ga = []
            for _n, _s, _d in [("q_fm", [128, FC, T], BF16),
                               ("k_fm", [128, FC, T], BF16),
                               ("v_tm", [128, 8, D], BF16),
                               ("stat_m", [1, T], F32),
                               ("stat_negm_bf", [1, T], BF16),
                               ("stat_r", [1, T], F32),
                               ("r_b", [128, T], F32),
                               ("r_cols", [128, 8], F32)]:
                _g, _h = alloc(_n, _s, _d)
                MG.append(_g)
                _ga.append(_g)
                if _n == "q_fm":
                    q_fm = _h
                elif _n == "k_fm":
                    k_fm = _h
                elif _n == "v_tm":
                    v_tm = _h
                elif _n == "stat_m":
                    stat_m = _h
                elif _n == "stat_negm_bf":
                    stat_negm_bf = _h
                elif _n == "stat_r":
                    stat_r = _h
                elif _n == "r_b":
                    r_b = _h
                elif _n == "r_cols":
                    r_cols = _h

            with nc.sbuf_tensor("x_bf", [128, FC, T], BF16) as x_bf:
              with (
                nc.sbuf_tensor("sq_t", [128, T], BF16) as sq_t,
                nc.sbuf_tensor("tmp_row", [1, T], F32) as tmp_row,
                nc.psum_tensor("ps_r0", [1, 512]) as ps_r0,
                nc.psum_tensor("ps_r1", [1, 512]) as ps_r1,
                nc.psum_tensor("ps_r2", [1, 512]) as ps_r2,
                nc.psum_tensor("ps_r3", [1, 512]) as ps_r3,
                nc.psum_tensor("ps_b", [128, T]) as ps_b,
                nc.psum_tensor("ps_tp", [128, 128]) as ps_tp,
              ):
                @B.step("gpsimd")
                def _(g):
                    B.wait(g, "dw")
                    for c in range(FC):
                        B.dma(g, "dw", out=x_bf[:, c, :],
                              in_=I["x0"][ts(c, 128), :])
                    return None

                for c in range(FC):
                    @B.step("scalar")
                    def _(s, c=c):
                        if c == 0:
                            B.wait(s, "dw")
                        return s.activation(out=sq_t[:, :], in_=x_bf[:, c, :],
                                            func=AF.Square)

                    @B.step("tensor")
                    def _(t, c=c):
                        f, l = (c == 0), (c == FC - 1)
                        t.matmul(ps_r0[:, :], onesb[:], x_bf[:, c, 0:512],
                                 start=f, stop=l)
                        t.matmul(ps_r1[:, :], onesb[:], x_bf[:, c, 512:1024],
                                 start=f, stop=l)
                        t.matmul(ps_r2[:, :], onesb[:], sq_t[:, 0:512],
                                 start=f, stop=l)
                        return t.matmul(ps_r3[:, :], onesb[:], sq_t[:, 512:1024],
                                        start=f, stop=l)

                @B.step("vector")
                def _(v):
                    v.tensor_scalar(out=stat_m[0:1, 0:512], in0=ps_r0[:, :],
                                    scalar1=1.0 / D, scalar2=None, op0=ALU.mult)
                    v.tensor_scalar(out=stat_m[0:1, 512:1024], in0=ps_r1[:, :],
                                    scalar1=1.0 / D, scalar2=None, op0=ALU.mult)
                    v.tensor_scalar(out=stat_r[0:1, 0:512], in0=ps_r2[:, :],
                                    scalar1=1.0 / D, scalar2=None, op0=ALU.mult)
                    return v.tensor_scalar(out=stat_r[0:1, 512:1024],
                                           in0=ps_r3[:, :],
                                           scalar1=1.0 / D, scalar2=None,
                                           op0=ALU.mult)

                @B.step("vector")
                def _(v):
                    v.tensor_tensor(out=tmp_row[:], in0=stat_m[:],
                                    in1=stat_m[:], op=ALU.mult)
                    return v.tensor_scalar(out=stat_negm_bf[:], in0=stat_m[:],
                                           scalar1=-1.0, scalar2=None,
                                           op0=ALU.mult)

                @B.step("vector")
                def _(v):
                    return v.scalar_tensor_tensor(out=stat_r[:], in0=stat_r[:],
                                                  scalar=EPS, in1=tmp_row[:],
                                                  op0=ALU.add, op1=ALU.subtract)

                @B.step("vector")
                def _(v):
                    return v.reciprocal(out=stat_r[:], in_=stat_r[:])

                @B.step("scalar")
                def _(s):
                    return s.activation(out=stat_r[:], in_=stat_r[:],
                                        func=AF.Sqrt)

                @B.step("tensor")
                def _(t):
                    t.matmul(ps_b[:, 0:512], onesf[:], stat_r[0:1, 0:512],
                             start=True, stop=True)
                    return t.matmul(ps_b[:, 512:1024], onesf[:],
                                    stat_r[0:1, 512:1024], start=True, stop=True)

                @B.step("vector")
                def _(v):
                    return v.tensor_copy(r_b[:, :], ps_b[:, :])

                for tb in range(8):
                    @B.step("tensor")
                    def _(t, tb=tb):
                        return t.transpose(ps_tp[:, :], r_b[:, ts(tb, 128)],
                                           c_identf[:])

                    @B.step("vector")
                    def _(v, tb=tb):
                        return v.tensor_copy(r_cols[:, tb:tb + 1], ps_tp[:, 0:1])

              with (
                    nc.sbuf_tensor("wst", [128, 2, FC, 512], BF16) as wst,
                    nc.sbuf_tensor("csum", [1, 2, 512], BF16) as csum,
                    nc.sbuf_tensor("bvb", [128, 512], F32) as bvb,
                    nc.sbuf_tensor("bqc", [128, 48], F32) as bqc,
                    nc.psum_tensor("pj0", [128, 512]) as pj0,
                    nc.psum_tensor("pj1", [128, 512]) as pj1,
                    nc.psum_tensor("pcs", [1, 512]) as pcs,
                    nc.psum_tensor("ps_bv", [128, 512]) as ps_bv,
                ):
                    @B.step("sync")
                    def _(s):
                        B.dma(s, "dl", out=bqc[:, :], in_=I["bqkv"])
                        return None

                    def stream_wblock(dram_ap, col0, slot, w_mark):
                        @B.step("gpsimd")
                        def _(g):
                            B.wait(g, "dw")
                            for c in range(FC):
                                B.dma(g, "dw", out=wst[:, slot, c, :],
                                      in_=dram_ap[ts(c, 128), col0:col0 + 512])
                            w_mark[0] = B.cnt["dw"]
                            return None

                    def colsum_block(slot, w_mark):
                        @B.step("tensor")
                        def _(t):
                            B.wait(t, "dw", w_mark[0])
                            last = None
                            for c in range(FC):
                                last = t.matmul(pcs[:, :], onesb[:],
                                                wst[:, slot, c, :],
                                                start=(c == 0), stop=(c == FC - 1))
                            return last

                        @B.step("vector")
                        def _(v):
                            return v.tensor_copy(csum[:, :], pcs[:, :])

                    def qk_evac1(v, fc_out, tt, pj):
                        dstbuf = q_fm if fc_out < FC else k_fm
                        return v.tensor_tensor(
                            out=dstbuf[:, fc_out % FC, ts(tt, 512)],
                            in0=pj[:, :], in1=r_b[:, ts(tt, 512)], op=ALU.mult)

                    def qk_evac2(v, fc_out, tt):
                        dstbuf = q_fm if fc_out < FC else k_fm
                        return v.tensor_scalar(
                            out=dstbuf[:, fc_out % FC, ts(tt, 512)],
                            in0=dstbuf[:, fc_out % FC, ts(tt, 512)],
                            scalar1=bqc[:, fc_out:fc_out + 1],
                            scalar2=None, op0=ALU.add)

                    B.join("vector", "dl")
                    fm_proj_island(I["Wqkv"], 0, 8, wst, csum, [pj0, pj1], pcs,
                                   FC, lambda c, tt: x_bf[:, c, ts(tt, 512)],
                                   lambda c: onesb[:], stat_negm_bf,
                                   qk_evac1, qk_evac2)

                    ev = B.semget("ev")
                    eb = B.semget("eb")
                    vwmark = [0] * 4
                    vblk_end = [0] * 6
                    ev0v = B.cnt["ev"]

                    def vload(vb):
                        @B.astep("gpsimd")
                        def _(g):
                            B.wait(g, "dw")
                            if vb >= 2:
                                g.wait_ge(B.s["ch"], vblk_end[vb - 2])
                            for c in range(FC):
                                B.dma(g, "dw", out=wst[:, vb % 2, c, :],
                                      in_=I["Wqkv"][ts(c, 128),
                                                    2 * D + vb * 512:
                                                    2 * D + (vb + 1) * 512])
                            vwmark[vb] = B.cnt["dw"]

                    vload(0)
                    vload(1)
                    for vb in range(4):
                        slot = vb % 2

                        @B.step("sync")
                        def _(s, vb=vb):
                            B.wait(s, "eb")
                            B.dma(s, "dl", out=bvb[0:1, :],
                                  in_=I["bqkv_row"][0:1, 2 * D + vb * 512:
                                                    2 * D + (vb + 1) * 512])
                            return None

                        @B.step("tensor")
                        def _(t, vb=vb, slot=slot, ebp=B.cnt["eb"]):
                            B.wait(t, "dw", vwmark[vb])
                            B.wait(t, "dl")
                            t.wait_ge(eb, ebp)
                            for c in range(FC):
                                t.matmul(pcs[:, :], onesb[:], wst[:, slot, c, :],
                                         start=(c == 0), stop=(c == FC - 1))
                            return t.matmul(ps_bv[:, :], onesf[:], bvb[0:1, :],
                                            start=True, stop=True)
                        cs_ch = B.cnt["ch"]

                        @B.astep("vector")
                        def _(v, slot=slot, cs_ch=cs_ch):
                            v.wait_ge(B.s["ch"], cs_ch)
                            v.tensor_copy(csum[0:1, slot, :], pcs[:, :])
                            pinc(v.tensor_copy(bvb[:, :], ps_bv[:, :]), "eb")

                        ebw = B.cnt["eb"]
                        for tb in range(8):
                            j = 8 * vb + tb
                            pj = pj0 if j % 2 == 0 else pj1

                            @B.step("tensor")
                            def _(t, vb=vb, slot=slot, tb=tb, j=j, pj=pj,
                                  ebw=ebw):
                                if j >= 2:
                                    t.wait_ge(ev, ev0v + j - 1)
                                if tb == 0:
                                    t.wait_ge(eb, ebw)
                                for c in range(FC):
                                    t.matmul(pj[:, :], x_bf[:, c, ts(tb, 128)],
                                             wst[:, slot, c, :],
                                             start=(c == 0), stop=False)
                                return t.matmul(
                                    pj[:, :], stat_negm_bf[0:1, ts(tb, 128)],
                                    csum[0:1, slot, :], start=False, stop=True)
                            tile_ch = B.cnt["ch"]
                            if tb == 7:
                                vblk_end[vb] = tile_ch

                            @B.astep("vector")
                            def _(v, vb=vb, tb=tb, pj=pj, tile_ch=tile_ch,
                                  ebw=ebw):
                                v.wait_ge(B.s["ch"], tile_ch)
                                v.wait_ge(eb, ebw)
                                pinc(v.tensor_scalar(
                                    out=v_tm[:, tb, ts(vb, 512)], in0=pj[:, :],
                                    scalar1=r_cols[:, tb:tb + 1], scalar2=None,
                                    op0=ALU.mult), "ev")

                            @B.astep("vector")
                            def _(v, vb=vb, tb=tb, j=j):
                                v.wait_ge(ev, ev0v + j + 1)
                                pinc(v.tensor_tensor(
                                    out=v_tm[:, tb, ts(vb, 512)],
                                    in0=v_tm[:, tb, ts(vb, 512)],
                                    in1=bvb[:, :], op=ALU.add), "eb")

                        if vb + 2 < 4:
                            vload(vb + 2)

                    B.join("vector", "ev", "eb")
                    B.join("gpsimd", "dw")

              if _DEV_STOP == "qkv":
                  sl = [(q_fm[:, c, :], c * 128, 1024) for c in range(8)]
                  sl += [(k_fm[:, c, :], 1024 + c * 128, 1024) for c in range(8)]
                  _dev_dump(B, y, sl)
                  raise _DevStop()
              if _DEV_STOP == "qkv2":
                  sl = [(q_fm[:, 8 + c, :], c * 128, 1024) for c in range(8)]
                  sl += [(k_fm[:, 8 + c, :], 1024 + c * 128, 1024) for c in range(8)]
                  _dev_dump(B, y, sl)
                  raise _DevStop()
              if _DEV_STOP == "vtm":
                  sl = [(v_tm[:, tb, 0:1024], tb * 128, 1024) for tb in range(8)]
                  sl += [(v_tm[:, tb, 1024:2048], 1024 + tb * 128, 1024)
                         for tb in range(8)]
                  _dev_dump(B, y, sl)
                  raise _DevStop()

            attn_fm_g = nc.sbuf_tensor("attn_fm", [128, FC, T], BF16, side="right")
            attn_fm = mge(attn_fm_g)
            with (
                nc.sbuf_tensor("S_all", [128, HA * 128], F32) as S_all,
                nc.sbuf_tensor("S_bf", [128, HA * 128], BF16) as S_bf,
                nc.sbuf_tensor("qd8", [128, 8, T], BF16) as qd8,
                nc.sbuf_tensor("sco_sb", [128, 8 * BLK], BF16) as sco_sb,
                nc.sbuf_tensor("ktm_sb", [128, 8 * 128], BF16) as ktm_sb,
                nc.psum_tensor("pscore", [128, 512], F32) as pscore,
                nc.psum_tensor("pktm", [128, 8 * 128], BF16) as pktm,
                nc.psum_tensor("po", [128, 512], F32) as po,
                nc.psum_tensor("psinc", [128, 1024], F32) as psinc,
            ):
                @B.step("vector")
                def _(v):
                    v.memset(S_all[:], 0.0)
                    return v.memset(S_bf[:], 0.0)

                for g in range(2):
                    @B.step("vector")
                    def _(v, g=g):
                        last = None
                        for j in range(8):
                            h = g * 8 + j
                            last = v.tensor_tensor(
                                out=ap3(qd8, j * T, [[8 * T, 128], [64, FC],
                                                     [1, 64]]),
                                in0=ap3(q_fm, h * T, [[FC * T, 128], [64, FC],
                                                      [1, 64]]),
                                in1=ap3(t_qd, h * BLK, [[HA * BLK, 128], [0, FC],
                                                        [1, 64]]),
                                op=ALU.mult)
                        return last

                    for c in range(NCA):
                        tl = c * BLK
                        tb, toff = c // 2, (c % 2) * BLK

                        @B.step("tensor")
                        def _(t, g=g, c=c, tl=tl, toff=toff):
                            pe = toff + BLK
                            for j in range(8):
                                h = g * 8 + j
                                t.transpose(pktm[toff:pe, ts(j, 128)],
                                            k_fm[:, h, tl:tl + BLK], c_identb[:])
                            last = None
                            for j in range(8):
                                h = g * 8 + j
                                last = t.matmul(pscore[toff:pe, ts(j, BLK)],
                                                k_fm[:, h, tl:tl + BLK],
                                                q_fm[:, h, tl:tl + BLK],
                                                start=True, stop=True)
                            return last

                        @B.step("vector")
                        def _(v, g=g, toff=toff):
                            pe = toff + BLK
                            for j in range(8):
                                h = g * 8 + j
                                v.tensor_scalar(out=ktm_sb[toff:pe, ts(j, 128)],
                                                in0=pktm[toff:pe, ts(j, 128)],
                                                scalar1=t_kd[:, h:h + 1],
                                                scalar2=None, op0=ALU.mult)
                            return v.tensor_tensor(
                                out=apv(sco_sb, 8 * BLK, toff, BLK, 0,
                                        [[BLK, 8], [1, BLK]]),
                                in0=apv(pscore, 512, toff, BLK, 0,
                                        [[BLK, 8], [1, BLK]]),
                                in1=apv(t_maskT, HA * BLK, 0, BLK, g * 8 * BLK,
                                        [[BLK, 8], [1, BLK]]),
                                op=ALU.mult)

                        @B.step("tensor")
                        def _(t, g=g, tl=tl, tb=tb, toff=toff):
                            pe = toff + BLK
                            for j in range(8):
                                h = g * 8 + j
                                vchunk = v_tm[toff:pe, tb,
                                              h * 128:(h + 1) * 128]
                                t.matmul(po[:, ts(j, BLK)], vchunk,
                                         sco_sb[toff:pe, ts(j, BLK)],
                                         start=True, stop=False)
                                t.matmul(po[:, ts(j, BLK)],
                                         S_bf[:, h * 128:(h + 1) * 128],
                                         qd8[:, j, tl:tl + BLK],
                                         start=False, stop=True)
                            last = None
                            for j in range(8):
                                h = g * 8 + j
                                vchunk = v_tm[toff:pe, tb,
                                              h * 128:(h + 1) * 128]
                                last = t.matmul(psinc[:, ts(j, 128)],
                                                ktm_sb[toff:pe, ts(j, 128)],
                                                vchunk, start=True, stop=True)
                            return last

                        @B.step("vector")
                        def _(v, g=g, tl=tl):
                            v.tensor_copy(
                                ap3(attn_fm, g * 8 * T + tl,
                                    [[FC * T, 128], [T, 8], [1, BLK]]),
                                ap3(po, 0, [[512, 128], [BLK, 8], [1, BLK]]))
                            return v.tensor_tensor(
                                out=ap3(S_all, g * 8 * 128,
                                        [[HA * 128, 128], [128, 8], [1, 128]]),
                                in0=ap3(S_all, g * 8 * 128,
                                        [[HA * 128, 128], [128, 8], [1, 128]]),
                                in1=ap3(t_bd, g * 8, [[HA, 128], [1, 8],
                                                      [0, 128]]),
                                op=ALU.mult)

                        @B.step("vector")
                        def _(v, g=g):
                            sl = slice(g * 8 * 128, (g + 1) * 8 * 128)
                            return v.tensor_tensor(out=S_all[:, sl],
                                                   in0=S_all[:, sl],
                                                   in1=psinc[:, :], op=ALU.add)

                        @B.step("vector")
                        def _(v, g=g):
                            sl = slice(g * 8 * 128, (g + 1) * 8 * 128)
                            return v.tensor_copy(S_bf[:, sl], S_all[:, sl])

                with (
                    nc.sbuf_tensor("S_in", [128, HA * 128], F32) as S_in,
                    nc.sbuf_tensor("S_inb", [128, HA * 128], BF16) as S_inb,
                    nc.sbuf_tensor("exp_row", [1, T], F32) as exp_row,
                    nc.sbuf_tensor("expt_bf", [128, T], BF16) as expt_bf,
                    nc.sbuf_tensor("qfix_bf", [128, T], BF16) as qfix_bf,
                    nc.psum_tensor("psfix", [128, T], F32) as psfix,
                ):
                    @B.step("sync")
                    def _(s):
                        B.dma(s, "dl", out=cc_a_in, in_=S_all[:])
                        return None

                    @B.step("gpsimd")
                    def _(g):
                        B.wait(g, "dl")
                        inst = g.collective_compute(
                            "AllGather", ALU.bypass,
                            replica_groups=[[0, 1], [2, 3], [4, 5], [6, 7]],
                            ins=[cc_a_in.opt()], outs=[cc_a_out.opt()])
                        inst.then_inc(B.semget("cc"))
                        B.cnt["cc"] += 1
                        return None

                    @B.step("gpsimd")
                    def _(g):
                        B.wait(g, "dw")
                        B.wait(g, "cc")
                        B.dma(g, "dw", out=S_in[:], in_=cc_a_out[0:128, :])
                        return None

                    @B.step("vector")
                    def _(v):
                        B.wait(v, "dw")
                        return v.tensor_scalar(out=S_inb[:], in0=S_in[:],
                                               scalar1=c_mask[:, 0:1],
                                               scalar2=None, op0=ALU.mult)

                    for h in range(HA):
                        @B.step("scalar")
                        def _(s, h=h):
                            return s.activation(out=exp_row[:], in_=c_negt[:],
                                                func=AF.Exp,
                                                scale=c_slopes[0:1, h:h + 1])

                        @B.step("tensor")
                        def _(t):
                            t.matmul(psfix[:, 0:512], onesf[:],
                                     exp_row[0:1, 0:512], start=True, stop=True)
                            return t.matmul(psfix[:, 512:1024], onesf[:],
                                            exp_row[0:1, 512:1024],
                                            start=True, stop=True)

                        @B.step("vector")
                        def _(v):
                            return v.tensor_copy(expt_bf[:], psfix[:, :])

                        @B.step("vector")
                        def _(v, h=h):
                            return v.tensor_tensor(out=qfix_bf[:],
                                                   in0=q_fm[:, h, :],
                                                   in1=expt_bf[:], op=ALU.mult)

                        @B.step("tensor")
                        def _(t, h=h):
                            t.matmul(psfix[:, 0:512],
                                     S_inb[:, h * 128:(h + 1) * 128],
                                     qfix_bf[:, 0:512], start=True, stop=True)
                            return t.matmul(psfix[:, 512:1024],
                                            S_inb[:, h * 128:(h + 1) * 128],
                                            qfix_bf[:, 512:1024],
                                            start=True, stop=True)

                        @B.step("vector")
                        def _(v, h=h):
                            return v.tensor_tensor(out=attn_fm[:, h, :],
                                                   in0=attn_fm[:, h, :],
                                                   in1=psfix[:, :], op=ALU.add)

            for _g in reversed(_ga):
                mgx(_g)
            xres_g = nc.sbuf_tensor("xres", [128, FC, T], F32, side="right")
            xres = mge(xres_g)
            with (
                nc.sbuf_tensor("wst2", [128, 2, FC, 512], BF16) as wst2,
                nc.sbuf_tensor("boutc", [128, FC], F32) as boutc,
                nc.sbuf_tensor("x0st", [128, 512], F32) as x0st,
                nc.sbuf_tensor("tmp_ev", [128, 512], F32) as tmp_ev,
                nc.psum_tensor("pw0", [128, 512]) as pw0,
            ):
                @B.step("sync")
                def _(s):
                    B.dma(s, "dl", out=boutc[:, :], in_=I["bout"])
                    return None

                for blki in range(4):
                    slot = blki % 2
                    w_mark = [0]

                    @B.step("gpsimd")
                    def _(g, blki=blki, slot=slot, w_mark=w_mark):
                        B.wait(g, "dw")
                        for c in range(FC):
                            B.dma(g, "dw", out=wst2[:, slot, c, :],
                                  in_=I["Wout"][ts(c, 128),
                                                blki * 512:(blki + 1) * 512])
                        w_mark[0] = B.cnt["dw"]
                        return None

                    for dc in range(4):
                        fc_out = blki * 4 + dc
                        for tt in range(2):
                            @B.step("sync")
                            def _(s, fc_out=fc_out, tt=tt):
                                B.dma(s, "dl", out=x0st[:],
                                      in_=I["x0"][ts(fc_out, 128), ts(tt, 512)])
                                return None

                            @B.step("tensor")
                            def _(t, slot=slot, dc=dc, tt=tt, w_mark=w_mark):
                                B.wait(t, "dw", w_mark[0])
                                last = None
                                for c in range(FC):
                                    last = t.matmul(pw0[:, :],
                                                    wst2[:, slot, c, ts(dc, 128)],
                                                    attn_fm[:, c, ts(tt, 512)],
                                                    start=(c == 0),
                                                    stop=(c == FC - 1))
                                return last

                            @B.step("scalar")
                            def _(s, fc_out=fc_out):
                                B.wait(s, "dl")
                                return s.activation(
                                    out=tmp_ev[:], in_=pw0[:, :],
                                    func=AF.Identity,
                                    bias=boutc[:, fc_out:fc_out + 1])

                            @B.step("vector")
                            def _(v, fc_out=fc_out, tt=tt):
                                B.wait(v, "dl")
                                return v.tensor_tensor(
                                    out=xres[:, fc_out, ts(tt, 512)],
                                    in0=tmp_ev[:], in1=x0st[:], op=ALU.add)

            for _g in reversed(_tgs):
                mgx(_g)

            if _DEV_STOP == "x1":
                sl = [(xres[:, c, :], c * 128, 1024) for c in range(FC)]
                _dev_dump(B, y, sl)
                raise _DevStop()

            invsw = sbuf("invsw", [128, 3])
            thrw = sbuf("thrw", [128, 3])
            nthrw = sbuf("nthrw", [128, 3])

            def ffn_prep():
                specs = [(0, I["w1T"], ternd[1], D, HID),
                         (1, I["w3T"], ternd[3], HID and D, HID),
                         (2, I["w2T"], ternd[2], HID, D)]
                specs[1] = (1, I["w3T"], ternd[3], D, HID)
                dts = B.semget("dts")
                with (
                    nc.sbuf_tensor("fw_st", [128, 2, HID], F32) as fw_st,
                    nc.sbuf_tensor("fw_ab", [128, HID], BF16) as fw_ab,
                    nc.sbuf_tensor("fw_g", [128, HID], BF16) as fw_g,
                    nc.sbuf_tensor("fw_l", [128, HID], BF16) as fw_l,
                    nc.sbuf_tensor("sc_row", [1, 512], F32) as sc_row,
                    nc.sbuf_tensor("sc_one", [1, 1], F32) as sc_one,
                    nc.psum_tensor("ps_sc", [1, 512]) as ps_sc,
                    nc.psum_tensor("ps_bc", [128, 1]) as ps_bc,
                ):
                    for wi, wsrc, wdst, nrows, ncols in specs:
                        nrc = nrows // 128
                        nmm = ncols // 512
                        wmk = [0] * nrc
                        rc_end = [0] * (nrc + 2)

                        def ld(rc, wsrc=wsrc, ncols=ncols, wmk=wmk,
                               rc_end=rc_end):
                            @B.astep("gpsimd")
                            def _(g):
                                if rc >= 2:
                                    g.wait_ge(B.s["ch"], rc_end[rc - 2])
                                B.wait(g, "dw")
                                B.dma(g, "dw", out=fw_st[:, rc % 2, 0:ncols],
                                      in_=wsrc[ts(rc, 128), :])
                                wmk[rc] = B.cnt["dw"]

                        ld(0)
                        if nrc > 1:
                            ld(1)
                        for rc in range(nrc):
                            @B.step("scalar")
                            def _(s, rc=rc, ncols=ncols):
                                B.wait(s, "dw", wmk[rc])
                                return s.activation(out=fw_ab[:, 0:ncols],
                                                    in_=fw_st[:, rc % 2,
                                                              0:ncols],
                                                    func=AF.Abs)
                            rc_end[rc] = B.cnt["ch"]

                            @B.step("tensor")
                            def _(t, rc=rc, nmm=nmm, nrc=nrc):
                                last = None
                                for i2 in range(nmm):
                                    last = t.matmul(
                                        ps_sc[:, :], onesb[:],
                                        fw_ab[:, ts(i2, 512)],
                                        start=(rc == 0 and i2 == 0),
                                        stop=(rc == nrc - 1 and
                                              i2 == nmm - 1))
                                return last
                            if rc + 2 < nrc:
                                ld(rc + 2)

                        @B.step("vector")
                        def _(v):
                            return v.tensor_copy(sc_row[:], ps_sc[:, :])

                        @B.step("vector")
                        def _(v):
                            return v.tensor_reduce(out=sc_one[:],
                                                   in_=sc_row[:],
                                                   axis=AX.X, op=ALU.add)

                        @B.step("vector")
                        def _(v, nrows=nrows, ncols=ncols):
                            return v.tensor_scalar(
                                out=sc_one[:], in0=sc_one[:],
                                scalar1=1.0 / (nrows * ncols), scalar2=EPS,
                                op0=ALU.mult, op1=ALU.max)

                        @B.step("tensor")
                        def _(t):
                            return t.matmul(ps_bc[:, :], onesf[:], sc_one[:],
                                            start=True, stop=True)

                        @B.step("vector")
                        def _(v, wi=wi):
                            return v.tensor_copy(invsw[:, wi:wi + 1],
                                                 ps_bc[:, :])

                        @B.step("vector")
                        def _(v, wi=wi):
                            return v.tensor_scalar(out=thrw[:, wi:wi + 1],
                                                   in0=invsw[:, wi:wi + 1],
                                                   scalar1=0.5, scalar2=None,
                                                   op0=ALU.mult)

                        @B.step("vector")
                        def _(v, wi=wi):
                            return v.tensor_scalar(out=nthrw[:, wi:wi + 1],
                                                   in0=invsw[:, wi:wi + 1],
                                                   scalar1=-0.5, scalar2=None,
                                                   op0=ALU.mult)

                        wmk2 = [0] * nrc
                        rc_end2 = [0] * (nrc + 2)

                        def ld2(rc, wsrc=wsrc, ncols=ncols, wmk2=wmk2,
                                rc_end2=rc_end2):
                            @B.astep("gpsimd")
                            def _(g):
                                if rc >= 2:
                                    g.wait_ge(B.s["ch"], rc_end2[rc - 2])
                                B.wait(g, "dw")
                                B.dma(g, "dw", out=fw_st[:, rc % 2, 0:ncols],
                                      in_=wsrc[ts(rc, 128), :])
                                wmk2[rc] = B.cnt["dw"]

                        ld2(0)
                        if nrc > 1:
                            ld2(1)
                        for rc in range(nrc):
                            @B.step("vector")
                            def _(v, rc=rc, wi=wi, ncols=ncols):
                                B.wait(v, "dw", wmk2[rc])
                                B.wait(v, "dts")
                                v.tensor_scalar(out=fw_g[:, 0:ncols],
                                                in0=fw_st[:, rc % 2, 0:ncols],
                                                scalar1=thrw[:, wi:wi + 1],
                                                scalar2=None, op0=ALU.is_gt)
                                return v.tensor_scalar(
                                    out=fw_l[:, 0:ncols],
                                    in0=fw_st[:, rc % 2, 0:ncols],
                                    scalar1=nthrw[:, wi:wi + 1],
                                    scalar2=None, op0=ALU.is_lt)
                            rc_end2[rc] = B.cnt["ch"]

                            @B.step("vector")
                            def _(v, ncols=ncols):
                                return v.tensor_tensor(out=fw_g[:, 0:ncols],
                                                       in0=fw_g[:, 0:ncols],
                                                       in1=fw_l[:, 0:ncols],
                                                       op=ALU.subtract)
                            st_ch = B.cnt["ch"]

                            @B.astep("gpsimd")
                            def _(g, wdst=wdst, rc=rc, ncols=ncols,
                                  st_ch=st_ch):
                                g.wait_ge(B.s["ch"], st_ch)
                                B.wait(g, "dts")
                                s2 = B.semget("dts")
                                g.dma_start(out=wdst[ts(rc, 128), :],
                                            in_=fw_g[:, 0:ncols]
                                            ).then_inc(s2, 16)
                                B.cnt["dts"] += 16
                            if rc + 2 < nrc:
                                ld2(rc + 2)

                    B.join("gpsimd", "dts", "dw")

            cols = sbuf("cols", [128, 12])
            oc1 = sbuf("oc1", [128, 8])
            oc3 = sbuf("oc3", [128, 8])
            ucq = sbuf("ucq", [128, 8])
            o2c = sbuf("o2c", [128, 8])
            mxh = sbuf("mxh", [128, 8])

            def ffn_use(tag, xres):
                cA, cB, cNM, cM2, cVA, cR, cMX, cCL, cT1 = range(9)

                with nc.sbuf_tensor(tag + "I_fm", [128, FC, T], BF16) as I_fm:
                    with (
                        nc.sbuf_tensor(tag + "x_tm", [128, D], F32) as x_tm,
                        nc.sbuf_tensor(tag + "d_tm", [128, D], F32) as d_tm,
                        nc.sbuf_tensor(tag + "sqd", [128, D], BF16) as sqd,
                        nc.sbuf_tensor(tag + "I_tm", [128, D], BF16) as I_tm,
                        nc.psum_tensor(tag + "ps_t", [128, 512], F32) as ps_t,
                        nc.psum_tensor(tag + "ps_tb", [128, 512], BF16) as ps_tb,
                    ):
                        for tb in range(8):
                            for g4 in range(4):
                                @B.step("tensor")
                                def _(t, tb=tb, g4=g4):
                                    last = None
                                    for k in range(4):
                                        c = g4 * 4 + k
                                        last = t.transpose(
                                            ps_t[:, ts(k, 128)],
                                            xres[:, c, ts(tb, 128)], c_identf[:])
                                    return last

                                @B.step("vector")
                                def _(v, g4=g4):
                                    return v.tensor_copy(
                                        x_tm[:, g4 * 512:(g4 + 1) * 512],
                                        ps_t[:, :])

                            @B.step("vector")
                            def _(v):
                                return v.tensor_reduce(
                                    out=cols[:, cA:cA + 1], in_=x_tm[:],
                                    axis=AX.X, op=ALU.add)

                            @B.step("scalar")
                            def _(s):
                                return s.activation(
                                    out=sqd[:], in_=x_tm[:], func=AF.Square,
                                    accum_out=cols[:, cB:cB + 1])

                            @B.step("vector")
                            def _(v):
                                return v.tensor_scalar(
                                    out=cols[:, cNM:cNM + 1],
                                    in0=cols[:, cA:cA + 1],
                                    scalar1=-1.0 / D, scalar2=None, op0=ALU.mult)

                            @B.step("vector")
                            def _(v):
                                v.tensor_tensor(out=cols[:, cM2:cM2 + 1],
                                                in0=cols[:, cNM:cNM + 1],
                                                in1=cols[:, cNM:cNM + 1],
                                                op=ALU.mult)
                                return v.tensor_scalar(
                                    out=cols[:, cVA:cVA + 1],
                                    in0=cols[:, cB:cB + 1], scalar1=1.0 / D,
                                    scalar2=EPS, op0=ALU.mult, op1=ALU.add)

                            @B.step("vector")
                            def _(v):
                                return v.tensor_tensor(out=cols[:, cVA:cVA + 1],
                                                       in0=cols[:, cVA:cVA + 1],
                                                       in1=cols[:, cM2:cM2 + 1],
                                                       op=ALU.subtract)

                            @B.step("vector")
                            def _(v):
                                return v.reciprocal(out=cols[:, cR:cR + 1],
                                                    in_=cols[:, cVA:cVA + 1])

                            @B.step("scalar")
                            def _(s):
                                return s.activation(out=cols[:, cR:cR + 1],
                                                    in_=cols[:, cR:cR + 1],
                                                    func=AF.Sqrt)

                            @B.step("scalar")
                            def _(s):
                                return s.activation(out=d_tm[:], in_=x_tm[:],
                                                    func=AF.Identity,
                                                    bias=cols[:, cNM:cNM + 1])

                            @B.step("vector")
                            def _(v):
                                return v.tensor_reduce(
                                    out=cols[:, cMX:cMX + 1], in_=d_tm[:],
                                    axis=AX.X, op=ALU.max,
                                    apply_absolute_value=True)

                            @B.step("vector")
                            def _(v):
                                return v.tensor_tensor(out=cols[:, cCL:cCL + 1],
                                                       in0=cols[:, cR:cR + 1],
                                                       in1=cols[:, cMX:cMX + 1],
                                                       op=ALU.mult)

                            @B.step("vector")
                            def _(v):
                                return v.tensor_scalar(out=cols[:, cCL:cCL + 1],
                                                       in0=cols[:, cCL:cCL + 1],
                                                       scalar1=EPS, scalar2=None,
                                                       op0=ALU.max)

                            @B.step("vector")
                            def _(v):
                                return v.reciprocal(out=cols[:, cT1:cT1 + 1],
                                                    in_=cols[:, cCL:cCL + 1])

                            @B.step("vector")
                            def _(v):
                                return v.tensor_scalar(out=cols[:, cT1:cT1 + 1],
                                                       in0=cols[:, cT1:cT1 + 1],
                                                       scalar1=127.0, scalar2=None,
                                                       op0=ALU.mult)

                            @B.step("vector")
                            def _(v, tb=tb):
                                v.tensor_tensor(out=cols[:, cT1:cT1 + 1],
                                                in0=cols[:, cT1:cT1 + 1],
                                                in1=cols[:, cR:cR + 1],
                                                op=ALU.mult)
                                v.tensor_scalar(out=oc1[:, tb:tb + 1],
                                                in0=cols[:, cCL:cCL + 1],
                                                scalar1=invsw[:, 0:1],
                                                scalar2=1.0 / 127.0,
                                                op0=ALU.mult, op1=ALU.mult)
                                return v.tensor_scalar(out=oc3[:, tb:tb + 1],
                                                       in0=cols[:, cCL:cCL + 1],
                                                       scalar1=invsw[:, 1:2],
                                                       scalar2=1.0 / 127.0,
                                                       op0=ALU.mult,
                                                       op1=ALU.mult)

                            @B.step("vector")
                            def _(v):
                                return v.tensor_scalar(out=d_tm[:], in0=d_tm[:],
                                                       scalar1=cols[:, cT1:cT1 + 1],
                                                       scalar2=RNDC,
                                                       op0=ALU.mult, op1=ALU.add)

                            @B.step("vector")
                            def _(v):
                                return v.tensor_scalar(out=I_tm[:], in0=d_tm[:],
                                                       scalar1=RNDC, scalar2=None,
                                                       op0=ALU.subtract)

                            for g4 in range(4):
                                @B.step("tensor")
                                def _(t, tb=tb, g4=g4):
                                    last = None
                                    for k in range(4):
                                        c = g4 * 4 + k
                                        last = t.transpose(
                                            ps_tb[:, ts(k, 128)],
                                            I_tm[:, ts(c, 128)], c_identb[:])
                                    return last

                                @B.step("vector")
                                def _(v, tb=tb, g4=g4):
                                    last = None
                                    for k in range(4):
                                        c = g4 * 4 + k
                                        last = v.tensor_copy(
                                            I_fm[:, c, ts(tb, 128)],
                                            ps_tb[:, ts(k, 128)])
                                    return last

                    with (
                        nc.sbuf_tensor(tag + "wA", [128, 2, FC, 512], BF16) as wA,
                        nc.sbuf_tensor(tag + "wB", [128, 2, FC, 512], BF16) as wB,
                        nc.sbuf_tensor(tag + "silu", [128, 2, 512], F32) as silu_sb,
                        nc.sbuf_tensor(tag + "tmp_t", [128, 2, 512], F32) as tmp_t,
                        nc.sbuf_tensor(tag + "rcol", [128, 2], F32) as rcol,
                        nc.psum_tensor(tag + "pA", [128, 2, 512], F32) as pA,
                        nc.psum_tensor(tag + "pB", [128, 2, 512], F32) as pB,
                    ):
                        es = B.semget("es")
                        ev = B.semget("ev")
                        eb = B.semget("eb")
                        em = B.semget("em")
                        dt = B.semget("dt")

                        @B.step("vector")
                        def _(v):
                            return v.memset(mxh[:], 0.0)

                        wmk = [0] * 11
                        hb_end = [0] * 13
                        ev0 = B.cnt["ev"]
                        es0 = B.cnt["es"]
                        dt0 = B.cnt["dt"]

                        def ldw(hb):
                            @B.astep("gpsimd")
                            def _(g):
                                if hb >= 2:
                                    g.wait_ge(B.s["ch"], hb_end[hb - 2])
                                B.wait(g, "dw")
                                for c in range(FC):
                                    B.dma(g, "dw", out=wA[:, hb % 2, c, :],
                                          in_=ternd[1][ts(c, 128),
                                                       hb * 512:(hb + 1) * 512])
                                    B.dma(g, "dw", out=wB[:, hb % 2, c, :],
                                          in_=ternd[3][ts(c, 128),
                                                       hb * 512:(hb + 1) * 512])
                                wmk[hb] = B.cnt["dw"]

                        ldw(0)
                        ldw(1)
                        for hb in range(11):
                            slot = hb % 2
                            for tb in range(8):
                                j = hb * 8 + tb
                                pp = j % 2

                                @B.step("tensor")
                                def _(t, hb=hb, slot=slot, tb=tb, j=j, pp=pp):
                                    B.wait(t, "dw", wmk[hb])
                                    if j >= 2:
                                        t.wait_ge(es, es0 + j - 1)
                                        t.wait_ge(ev, ev0 + j - 1)
                                    for c in range(FC):
                                        t.matmul(pA[:, pp, :],
                                                 I_fm[:, c, ts(tb, 128)],
                                                 wA[:, slot, c, :],
                                                 start=(c == 0),
                                                 stop=(c == FC - 1))
                                    last = None
                                    for c in range(FC):
                                        last = t.matmul(
                                            pB[:, pp, :],
                                            I_fm[:, c, ts(tb, 128)],
                                            wB[:, slot, c, :], start=(c == 0),
                                            stop=(c == FC - 1))
                                    return last
                                tile_ch = B.cnt["ch"]
                                if tb == 7:
                                    hb_end[hb] = tile_ch

                                @B.astep("scalar")
                                def _(s, tb=tb, j=j, pp=pp, tile_ch=tile_ch):
                                    s.wait_ge(B.s["ch"], tile_ch)
                                    inst = s.activation(out=silu_sb[:, pp, :],
                                                        in_=pA[:, pp, :],
                                                        func=AF.Silu,
                                                        scale=oc1[:, tb:tb + 1])
                                    pinc(inst, "es")

                                @B.astep("vector")
                                def _(v, j=j, pp=pp):
                                    v.wait_ge(es, es0 + j + 1)
                                    v.wait_ge(dt, dt0 + 16 * max(0, j - 1))
                                    pinc(v.tensor_tensor(out=tmp_t[:, pp, :],
                                                         in0=silu_sb[:, pp, :],
                                                         in1=pB[:, pp, :],
                                                         op=ALU.mult), "ev")

                                @B.astep("vector")
                                def _(v, j=j, pp=pp):
                                    v.wait_ge(ev, ev0 + j + 1)
                                    pinc(v.tensor_reduce(
                                        out=rcol[:, pp:pp + 1],
                                        in_=tmp_t[:, pp, :], axis=AX.X,
                                        op=ALU.max, apply_absolute_value=True),
                                        "eb")

                                @B.astep("vector")
                                def _(v, tb=tb, j=j, pp=pp, ebw=B.cnt["eb"]):
                                    v.wait_ge(eb, ebw)
                                    pinc(v.tensor_tensor(out=mxh[:, tb:tb + 1],
                                                         in0=mxh[:, tb:tb + 1],
                                                         in1=rcol[:, pp:pp + 1],
                                                         op=ALU.max), "em")

                                @B.astep("sync")
                                def _(s, hb=hb, tb=tb, j=j, pp=pp):
                                    s.wait_ge(ev, ev0 + j + 1)
                                    s2 = B.semget("dt")
                                    s.wait_ge(s2, dt0 + 16 * j)
                                    s.dma_start(
                                        out=tmpscr[ts(tb, 128),
                                                   hb * 512:(hb + 1) * 512],
                                        in_=tmp_t[:, pp, :]).then_inc(s2, 16)
                                    B.cnt["dt"] += 16

                            if hb + 2 < 11:
                                ldw(hb + 2)

                        B.join("vector", "ev", "eb", "em")
                        B.join("sync", "dt")
                        B.join("gpsimd", "dw")
                @B.step("vector")
                def _(v):
                    last = None
                    for tb in range(8):
                        last = v.tensor_tensor(out=mxh[:, tb:tb + 1],
                                               in0=mxh[:, tb:tb + 1],
                                               in1=oc3[:, tb:tb + 1], op=ALU.mult)
                    return last

                @B.step("vector")
                def _(v):
                    return v.tensor_scalar(out=mxh[:], in0=mxh[:], scalar1=EPS,
                                           scalar2=None, op0=ALU.max)

                @B.step("vector")
                def _(v):
                    return v.reciprocal(out=ucq[:], in_=mxh[:])

                @B.step("vector")
                def _(v):
                    last = None
                    for tb in range(8):
                        last = v.tensor_scalar(out=ucq[:, tb:tb + 1],
                                               in0=ucq[:, tb:tb + 1],
                                               scalar1=127.0,
                                               scalar2=oc3[:, tb:tb + 1],
                                               op0=ALU.mult, op1=ALU.mult)
                    return last

                @B.step("vector")
                def _(v):
                    last = None
                    for tb in range(8):
                        last = v.tensor_scalar(out=o2c[:, tb:tb + 1],
                                               in0=mxh[:, tb:tb + 1],
                                               scalar1=invsw[:, 2:3],
                                               scalar2=1.0 / 127.0,
                                               op0=ALU.mult, op1=ALU.mult)
                    return last

                with (
                    nc.sbuf_tensor(tag + "tmp_tm", [128, HID], F32) as tmp_tm,
                    nc.sbuf_tensor(tag + "ih_tm", [128, HID], BF16) as ih_tm,
                    nc.sbuf_tensor(tag + "ihf", [128, HC, 512], BF16) as ihf,
                    nc.sbuf_tensor(tag + "w2st", [128, 2, 512], BF16) as w2st,
                    nc.sbuf_tensor(tag + "out4", [128, 4, D], BF16) as out4,
                    nc.psum_tensor(tag + "ps4", [128, 4, 512], F32) as ps4,
                    nc.psum_tensor(tag + "ps_t2", [128, 512], BF16) as ps_t2,
                    nc.psum_tensor(tag + "ps_tb2", [128, 512], BF16) as ps_tb2,
                ):
                    for tt in range(2):
                        for tb4 in range(4):
                            tb = tt * 4 + tb4

                            @B.step("sync")
                            def _(s, tb=tb):
                                B.wait(s, "dt")
                                B.dma(s, "dl", out=tmp_tm[:],
                                      in_=tmpscr[ts(tb, 128), :])
                                return None

                            @B.step("vector")
                            def _(v, tb=tb):
                                B.wait(v, "dl")
                                return v.tensor_scalar(
                                    out=tmp_tm[:], in0=tmp_tm[:],
                                    scalar1=ucq[:, tb:tb + 1], scalar2=RNDC,
                                    op0=ALU.mult, op1=ALU.add)

                            @B.step("vector")
                            def _(v):
                                return v.tensor_scalar(out=ih_tm[:],
                                                       in0=tmp_tm[:],
                                                       scalar1=RNDC, scalar2=None,
                                                       op0=ALU.subtract)

                            for g11 in range(11):
                                @B.step("tensor")
                                def _(t, g11=g11):
                                    last = None
                                    for k in range(4):
                                        hc = g11 * 4 + k
                                        last = t.transpose(
                                            ps_tb2[:, ts(k, 128)],
                                            ih_tm[:, ts(hc, 128)], c_identb[:])
                                    return last

                                @B.step("vector")
                                def _(v, tb4=tb4, g11=g11):
                                    last = None
                                    for k in range(4):
                                        hc = g11 * 4 + k
                                        last = v.tensor_copy(
                                            ihf[:, hc, ts(tb4, 128)],
                                            ps_tb2[:, ts(k, 128)])
                                    return last

                        w2mk = {}
                        mm_end = {}
                        ev = B.semget("ev")
                        ev0w = B.cnt["ev"]

                        def ldw2(do, hc, tt=tt):
                            @B.astep("gpsimd")
                            def _(g):
                                k = do * HC + hc
                                if k >= 2:
                                    g.wait_ge(B.s["ch"], mm_end.get(k - 2, 0))
                                B.wait(g, "dw")
                                B.dma(g, "dw", out=w2st[:, hc % 2, :],
                                      in_=ternd[2][ts(hc, 128),
                                                   do * 512:(do + 1) * 512])
                                w2mk[(do, hc)] = B.cnt["dw"]

                        ldw2(0, 0)
                        ldw2(0, 1)
                        for do in range(4):
                            for hc in range(HC):
                                @B.step("tensor")
                                def _(t, do=do, hc=hc):
                                    B.wait(t, "dw", w2mk[(do, hc)])
                                    if hc == 0 and do >= 1:
                                        t.wait_ge(ev, ev0w + do)
                                    last = None
                                    for tb4 in range(4):
                                        last = t.matmul(
                                            ps4[:, tb4, :],
                                            ihf[:, hc, ts(tb4, 128)],
                                            w2st[:, hc % 2, :],
                                            start=(hc == 0),
                                            stop=(hc == HC - 1))
                                    return last
                                mm_end[do * HC + hc] = B.cnt["ch"]
                                nk = do * HC + hc + 2
                                if nk < 4 * HC:
                                    ldw2(nk // HC, nk % HC)
                            do_ch = B.cnt["ch"]

                            @B.astep("vector")
                            def _(v, tt=tt, do=do, do_ch=do_ch):
                                v.wait_ge(B.s["ch"], do_ch)
                                last = None
                                for tb4 in range(4):
                                    tb = tt * 4 + tb4
                                    last = v.tensor_scalar(
                                        out=out4[:, tb4, ts(do, 512)],
                                        in0=ps4[:, tb4, :],
                                        scalar1=o2c[:, tb:tb + 1],
                                        scalar2=None, op0=ALU.mult)
                                pinc(last, "ev")

                        B.join("vector", "ev")

                        for tb4 in range(4):
                            tb = tt * 4 + tb4
                            for g4 in range(4):
                                @B.step("tensor")
                                def _(t, tb4=tb4, g4=g4):
                                    last = None
                                    for k in range(4):
                                        c = g4 * 4 + k
                                        last = t.transpose(
                                            ps_t2[:, ts(k, 128)],
                                            out4[:, tb4, ts(c, 128)],
                                            c_identb[:])
                                    return last

                                @B.step("vector")
                                def _(v, tb=tb, g4=g4):
                                    last = None
                                    for k in range(4):
                                        c = g4 * 4 + k
                                        last = v.tensor_tensor(
                                            out=xres[:, c, ts(tb, 128)],
                                            in0=xres[:, c, ts(tb, 128)],
                                            in1=ps_t2[:, ts(k, 128)], op=ALU.add)
                                    return last

            B.semget("dt")
            ffn_prep()
            ffn_use("f1_", xres)

            if _DEV_STOP == "x2":
                sl = [(xres[:, c, :], c * 128, 1024) for c in range(FC)]
                _dev_dump(B, y, sl)
                raise _DevStop()

            bs_negm_bf = sbuf("bs_negm_bf", [1, T], BF16)
            m3_b = sbuf("m3_b", [128, T])
            r3_b = sbuf("r3_b", [128, T])
            c3_b = sbuf("c3_b", [128, T])
            anw_bf = sbuf("anw_bf", [128, FC], BF16)
            anw_f = sbuf("anw_f", [128, FC])
            cwsb = sbuf("cwsb", [128, FC, 12])
            with (
                nc.sbuf_tensor("bs_m", [1, T], F32) as bs_m,
                nc.sbuf_tensor("bs_v", [1, T], F32) as bs_v,
                nc.sbuf_tensor("bs_r3", [1, T], F32) as bs_r3,
                nc.sbuf_tensor("bs_c3", [1, T], F32) as bs_c3,
                nc.sbuf_tensor("b_cb", [128, T], BF16) as b_cb,
                nc.sbuf_tensor("b_sq", [128, T], BF16) as b_sq,
                nc.sbuf_tensor("b_row", [1, T], F32) as b_row,
                nc.psum_tensor("bp0", [1, 512]) as bp0,
                nc.psum_tensor("bp1", [1, 512]) as bp1,
                nc.psum_tensor("bp2", [1, 512]) as bp2,
                nc.psum_tensor("bp3", [1, 512]) as bp3,
                nc.psum_tensor("bpb", [128, T]) as bpb,
            ):
                @B.step("gpsimd")
                def _(g):
                    B.wait(g, "dw")
                    B.dma(g, "dw", out=anw_bf[:, :], in_=I["anw"])
                    return None

                @B.step("sync")
                def _(s):
                    B.dma(s, "dl", out=anw_f[:, :], in_=I["anw"])
                    for j, nm in enumerate(["cwq", "cwk", "cwv"]):
                        B.dma(s, "dl", out=cwsb[:, :, j * 4:(j + 1) * 4],
                              in_=I[nm])
                    return None

                for c in range(FC):
                    @B.step("vector")
                    def _(v, c=c):
                        return v.tensor_copy(b_cb[:, :], xres[:, c, :])

                    @B.step("scalar")
                    def _(s, c=c):
                        return s.activation(out=b_sq[:, :], in_=b_cb[:, :],
                                            func=AF.Square)

                    @B.step("tensor")
                    def _(t, c=c):
                        f, l = (c == 0), (c == FC - 1)
                        t.matmul(bp0[:, :], onesb[:], b_cb[:, 0:512],
                                 start=f, stop=l)
                        t.matmul(bp1[:, :], onesb[:], b_cb[:, 512:1024],
                                 start=f, stop=l)
                        t.matmul(bp2[:, :], onesb[:], b_sq[:, 0:512],
                                 start=f, stop=l)
                        return t.matmul(bp3[:, :], onesb[:], b_sq[:, 512:1024],
                                        start=f, stop=l)

                @B.step("sync")
                def _(s):
                    for c in range(FC):
                        B.dma(s, "dx", out=x2scr[ts(c, 128), :],
                              in_=xres[:, c, :])
                    return None

                @B.step("vector")
                def _(v):
                    v.tensor_scalar(out=bs_m[0:1, 0:512], in0=bp0[:, :],
                                    scalar1=1.0 / D, scalar2=None, op0=ALU.mult)
                    v.tensor_scalar(out=bs_m[0:1, 512:1024], in0=bp1[:, :],
                                    scalar1=1.0 / D, scalar2=None, op0=ALU.mult)
                    v.tensor_scalar(out=bs_v[0:1, 0:512], in0=bp2[:, :],
                                    scalar1=1.0 / D, scalar2=None, op0=ALU.mult)
                    return v.tensor_scalar(out=bs_v[0:1, 512:1024], in0=bp3[:, :],
                                           scalar1=1.0 / D, scalar2=None,
                                           op0=ALU.mult)

                @B.step("vector")
                def _(v):
                    v.tensor_tensor(out=b_row[:], in0=bs_m[:], in1=bs_m[:],
                                    op=ALU.mult)
                    return v.tensor_scalar(out=bs_negm_bf[:], in0=bs_m[:],
                                           scalar1=-1.0, scalar2=None,
                                           op0=ALU.mult)

                @B.step("vector")
                def _(v):
                    return v.tensor_tensor(out=bs_v[:], in0=bs_v[:], in1=b_row[:],
                                           op=ALU.subtract)

                @B.step("vector")
                def _(v):
                    return v.tensor_scalar(out=b_row[:], in0=bs_v[:], scalar1=EPS,
                                           scalar2=None, op0=ALU.add)

                @B.step("vector")
                def _(v):
                    return v.reciprocal(out=bs_r3[:], in_=b_row[:])

                @B.step("vector")
                def _(v):
                    return v.scalar_tensor_tensor(out=bs_c3[:], in0=bs_v[:],
                                                  scalar=EPS, in1=bs_r3[:],
                                                  op0=ALU.bypass, op1=ALU.mult)

                @B.step("vector")
                def _(v):
                    return v.tensor_scalar(out=bs_c3[:], in0=bs_c3[:],
                                           scalar1=EPS, scalar2=None, op0=ALU.add)

                @B.step("vector")
                def _(v):
                    return v.reciprocal(out=bs_c3[:], in_=bs_c3[:])

                @B.step("scalar")
                def _(s):
                    s.activation(out=bs_r3[:], in_=bs_r3[:], func=AF.Sqrt)
                    return s.activation(out=bs_c3[:], in_=bs_c3[:], func=AF.Sqrt)

                @B.step("vector")
                def _(v):
                    return v.tensor_tensor(out=bs_c3[:], in0=bs_c3[:],
                                           in1=bs_r3[:], op=ALU.mult)

                for (row, dst) in [(bs_m, m3_b), (bs_r3, r3_b), (bs_c3, c3_b)]:
                    @B.step("tensor")
                    def _(t, row=row):
                        t.matmul(bpb[:, 0:512], onesf[:], row[0:1, 0:512],
                                 start=True, stop=True)
                        return t.matmul(bpb[:, 512:1024], onesf[:],
                                        row[0:1, 512:1024], start=True, stop=True)

                    @B.step("vector")
                    def _(v, dst=dst):
                        return v.tensor_copy(dst[:, :], bpb[:, :])


            @B.step("sync")
            def _(s):
                B.wait(s, "dx")
                return None

            mgx(xres_g)
            mgx(attn_fm_g)

            x2w_g = nc.sbuf_tensor("x2w", [128, FC, T], BF16, side="right")
            x2w = mge(x2w_g)

            @B.step("gpsimd")
            def _(g):
                B.wait(g, "dw")
                B.wait(g, "dx")
                for c in range(FC):
                    B.dma(g, "dw", out=x2w[:, c, :], in_=x2scr[ts(c, 128), :])
                return None

            for c in range(FC):
                @B.step("vector")
                def _(v, c=c):
                    if c == 0:
                        B.wait(v, "dl")
                        B.wait(v, "dw")
                    return v.tensor_scalar(out=x2w[:, c, :], in0=x2w[:, c, :],
                                           scalar1=anw_f[:, c:c + 1],
                                           scalar2=None, op0=ALU.mult)

            out_q_g = nc.sbuf_tensor("out_q", [128, FC, T], BF16)
            out_q = mge(out_q_g)
            out_k_g = nc.sbuf_tensor("out_k", [128, FC, T], BF16)
            out_k = mge(out_k_g)
            v_tm_g = nc.sbuf_tensor("v_tmb", [128, NCB, 2048], BF16)
            v_tm = mge(v_tm_g)
            sendh_g = nc.sbuf_tensor("sendh", [128, FC * 9], BF16)
            sendh = mge(sendh_g)
            recvh_g = nc.sbuf_tensor("recvh", [128, FC * 9], BF16)
            recvh = mge(recvh_g)
            first3_g = nc.sbuf_tensor("first3", [128, FC * 9], BF16)
            first3 = mge(first3_g)

            def emit_proj_b(wdram, hoff, conv_j, dst, dstname, post_chunk=None):
                with (
                    nc.sbuf_tensor(dstname + "_pad", [128, FC, T + 3], BF16) as pad,
                    nc.sbuf_tensor(dstname + "_wst", [128, 2, FC, 256],
                                   BF16) as wstb,
                    nc.sbuf_tensor(dstname + "_cs", [1, 2, 256], BF16) as csb,
                    nc.sbuf_tensor(dstname + "_acc", [128, T], F32) as acc,
                    nc.psum_tensor(dstname + "_pj", [128, 512]) as pjb,
                    nc.psum_tensor(dstname + "_pj2", [128, 512]) as pjb2,
                    nc.psum_tensor(dstname + "_pc", [1, 512]) as pcb,
                ):
                    def pb_evac1(v, fc_out, tt, pj, pad=pad):
                        return v.tensor_tensor(
                            out=pad[:, fc_out,
                                    3 + tt * 512:3 + (tt + 1) * 512],
                            in0=pj[:, :], in1=c3_b[:, ts(tt, 512)],
                            op=ALU.mult)

                    fm_proj_island(wdram, 0, 8, wstb, csb, [pjb, pjb2], pcb,
                                   FC, lambda c, tt: x2w[:, c, ts(tt, 512)],
                                   lambda c: anw_bf[:, c:c + 1],
                                   bs_negm_bf, pb_evac1, bw=256)

                    @B.step("vector")
                    def _(v):
                        v.tensor_copy(
                            ap3(sendh, hoff, [[FC * 9, 128], [3, FC], [1, 3]]),
                            ap3(pad, 1024, [[FC * (T + 3), 128],
                                            [T + 3, FC], [1, 3]]))
                        return v.tensor_copy(
                            ap3(first3, hoff, [[FC * 9, 128], [3, FC], [1, 3]]),
                            ap3(pad, 3, [[FC * (T + 3), 128], [T + 3, FC],
                                         [1, 3]]))

                    for c in range(FC):
                        @B.step("vector")
                        def _(v, c=c):
                            return v.tensor_scalar(
                                out=acc[:, 3:T], in0=pad[:, c, 6:T + 3],
                                scalar1=cwsb[:, c, conv_j * 4 + 3:conv_j * 4 + 4],
                                scalar2=None, op0=ALU.mult)

                        for i in range(3):
                            @B.step("vector")
                            def _(v, c=c, i=i):
                                return v.scalar_tensor_tensor(
                                    out=acc[:, 3:T],
                                    in0=pad[:, c, 3 + i:3 + i + T - 3],
                                    scalar=cwsb[:, c,
                                                conv_j * 4 + i:conv_j * 4 + i + 1],
                                    in1=acc[:, 3:T], op0=ALU.mult, op1=ALU.add)

                        @B.step("scalar")
                        def _(s, c=c):
                            return s.activation(out=dst[:, c, 3:T],
                                                in_=acc[:, 3:T], func=AF.Silu)

                        if post_chunk is not None:
                            post_chunk(c)

            emit_proj_b(I["Wq"], 0, 0, out_q, "pq")
            emit_proj_b(I["Wk"], FC * 3, 1, out_k, "pk")

            vfirst_g = nc.sbuf_tensor("vfirst", [128, FC, 128], BF16)
            vfirst = mge(vfirst_g)
            vch_g = nc.sbuf_tensor("vch", [128, T], BF16)
            vch = mge(vch_g)
            pvt_g = nc.psum_tensor("pvt", [128, 512], BF16)
            pvt = mge(pvt_g)

            vdst = vch[None] if False else None

            class _VD:
                def __getitem__(self, idx):
                    return vch[:, idx[2]]

            def v_post_chunk(ec):
                @B.step("vector")
                def _(v, ec=ec):
                    return v.tensor_copy(vfirst[:, ec, 3:128], vch[:, 3:128])

                for g4, nk in ((0, 4), (1, 3)):
                    @B.step("tensor")
                    def _(t, ec=ec, g4=g4, nk=nk):
                        last = None
                        for k in range(nk):
                            cb = 1 + g4 * 4 + k
                            last = t.transpose(pvt[:, ts(k, 128)],
                                               vch[:, ts(cb, 128)], c_identb[:])
                        return last

                    @B.step("vector")
                    def _(v, ec=ec, g4=g4, nk=nk):
                        last = None
                        for k in range(nk):
                            cb = 1 + g4 * 4 + k
                            last = v.tensor_copy(v_tm[:, cb, ts(ec, 128)],
                                                 pvt[:, ts(k, 128)])
                        return last

            emit_proj_b(I["Wv"], FC * 6, 2, _VD(), "pv", post_chunk=v_post_chunk)

            @B.step("gpsimd")
            def _(g):
                B.wait(g, "dw")
                B.dma(g, "dw", out=cc_h_in, in_=sendh[:])
                return None

            @B.step("gpsimd")
            def _(g):
                B.wait(g, "dw")
                inst = g.collective_compute(
                    "AllGather", ALU.bypass,
                    replica_groups=[[0, 1], [2, 3], [4, 5], [6, 7]],
                    ins=[cc_h_in.opt()], outs=[cc_h_out.opt()])
                inst.then_inc(B.semget("cc"))
                B.cnt["cc"] += 1
                return None

            @B.step("gpsimd")
            def _(g):
                B.wait(g, "dw")
                B.wait(g, "cc")
                B.dma(g, "dw", out=recvh[:], in_=cc_h_out[0:128, :])
                return None

            @B.step("vector")
            def _(v):
                B.wait(v, "dw")
                return v.tensor_scalar(out=recvh[:], in0=recvh[:],
                                       scalar1=c_mask[:, 0:1], scalar2=None,
                                       op0=ALU.mult)

            with (
                nc.sbuf_tensor("fpad", [128, 6], BF16) as fpad,
                nc.sbuf_tensor("facc", [128, 3], F32) as facc,
            ):
                for dstb, hoff, conv_j, isv in [(out_q, 0, 0, False),
                                                (out_k, FC * 3, 1, False),
                                                (vfirst, FC * 6, 2, True)]:
                    for c in range(FC):
                        @B.step("vector")
                        def _(v, hoff=hoff, c=c):
                            v.tensor_copy(fpad[:, 0:3],
                                          recvh[:, hoff + c * 3:hoff + (c + 1) * 3])
                            return v.tensor_copy(
                                fpad[:, 3:6],
                                first3[:, hoff + c * 3:hoff + (c + 1) * 3])

                        @B.step("vector")
                        def _(v, c=c, conv_j=conv_j):
                            return v.tensor_scalar(
                                out=facc[:, :], in0=fpad[:, 3:6],
                                scalar1=cwsb[:, c, conv_j * 4 + 3:conv_j * 4 + 4],
                                scalar2=None, op0=ALU.mult)

                        for i in range(3):
                            @B.step("vector")
                            def _(v, c=c, i=i, conv_j=conv_j):
                                return v.scalar_tensor_tensor(
                                    out=facc[:, :], in0=fpad[:, i:i + 3],
                                    scalar=cwsb[:, c, conv_j * 4 + i:
                                                conv_j * 4 + i + 1],
                                    in1=facc[:, :], op0=ALU.mult, op1=ALU.add)

                        @B.step("scalar")
                        def _(s, dstb=dstb, c=c, isv=isv):
                            if isv:
                                return s.activation(out=dstb[:, c, 0:3],
                                                    in_=facc[:, :], func=AF.Silu)
                            return s.activation(out=dstb[:, c, 0:3],
                                                in_=facc[:, :], func=AF.Silu)

            for ec in range(FC):
                @B.step("tensor")
                def _(t, ec=ec):
                    return t.transpose(pvt[:, 0:128], vfirst[:, ec, :],
                                       c_identb[:])

                @B.step("vector")
                def _(v, ec=ec):
                    return v.tensor_copy(v_tm[:, 0, ts(ec, 128)], pvt[:, 0:128])

            mgx(pvt_g)
            mgx(vch_g)
            mgx(vfirst_g)

            with (
                nc.sbuf_tensor("rcos", [128, T], F32) as rcos,
                nc.sbuf_tensor("rsin", [128, T], F32) as rsin,
                nc.sbuf_tensor("rt1", [128, T], F32) as rt1,
                nc.sbuf_tensor("rt2", [128, T], F32) as rt2,
                nc.sbuf_tensor("rt3", [128, T], F32) as rt3,
                nc.sbuf_tensor("rt4", [128, T], F32) as rt4,
            ):
                @B.step("sync")
                def _(s):
                    B.dma(s, "dl", out=rcos[:], in_=I["cosT"])
                    B.dma(s, "dl", out=rsin[:], in_=I["sinT"])
                    return None

                first = [True]
                for buf in (out_q, out_k):
                    for h in range(HB):
                        c1, c2 = 2 * h, 2 * h + 1

                        @B.step("vector")
                        def _(v, buf=buf, c1=c1, c2=c2):
                            if first[0]:
                                B.wait(v, "dl")
                                first[0] = False
                            v.tensor_tensor(out=rt1[:], in0=buf[:, c1, :],
                                            in1=rcos[:], op=ALU.mult)
                            v.tensor_tensor(out=rt2[:], in0=buf[:, c2, :],
                                            in1=rsin[:], op=ALU.mult)
                            v.tensor_tensor(out=rt3[:], in0=buf[:, c2, :],
                                            in1=rcos[:], op=ALU.mult)
                            return v.tensor_tensor(out=rt4[:], in0=buf[:, c1, :],
                                                   in1=rsin[:], op=ALU.mult)

                        @B.step("vector")
                        def _(v, buf=buf, c1=c1, c2=c2):
                            v.tensor_tensor(out=buf[:, c1, :], in0=rt1[:],
                                            in1=rt2[:], op=ALU.subtract)
                            return v.tensor_tensor(out=buf[:, c2, :], in0=rt3[:],
                                                   in1=rt4[:], op=ALU.add)

            if _DEV_STOP == "bqkv":
                sl = [(out_q[:, c, :], c * 128, 1024) for c in range(8)]
                sl += [(out_k[:, c, :], 1024 + c * 128, 1024) for c in range(8)]
                _dev_dump(B, y, sl)
                raise _DevStop()
            if _DEV_STOP == "bqkv2":
                sl = [(v_tm[:, cb, 0:1024], cb * 128, 1024) for cb in range(8)]
                sl += [(v_tm[:, cb, 1024:2048], 1024 + cb * 128, 1024)
                       for cb in range(8)]
                _dev_dump(B, y, sl)
                raise _DevStop()

            o_fm_g = nc.sbuf_tensor("o_fm", [128, FC, T], BF16, side="right")
            o_fm = mge(o_fm_g)

            with (
                nc.sbuf_tensor("dm_t", [128, CB], BF16) as dm_t,
                nc.sbuf_tensor("qd_t", [128, CB], F32) as qd_t,
                nc.sbuf_tensor("dk_col", [128, 1], F32) as dk_col,
                nc.sbuf_tensor("qdih", [128, 2, T], BF16) as qdih,
                nc.sbuf_tensor("S_h", [128, 2, 256], F32) as S_h,
                nc.sbuf_tensor("S_hbf", [128, 2, 256], BF16) as S_hbf,
                nc.sbuf_tensor("b_ktm_sb", [128, 256], BF16) as ktm_sb,
                nc.sbuf_tensor("b_sco_b", [128, CB], BF16) as sco_b,
                nc.psum_tensor("b_pktm", [128, 256], BF16) as pktm,
                nc.psum_tensor("b_psc", [128, CB], F32) as psc,
                nc.psum_tensor("po_b", [128, 256], F32) as po_b,
                nc.psum_tensor("b_pinc", [128, 2, 256], F32) as b_pinc,
            ):
                for h in range(HB):
                    @B.step("sync")
                    def _(s, h=h):
                        B.dma(s, "dl", out=dm_t[:], in_=I["dmT"][h])
                        B.dma(s, "dl", out=qd_t[:], in_=I["qdi"][h])
                        B.dma(s, "dl", out=dk_col[:], in_=I["dkc"][h])
                        return None

                    @B.step("vector")
                    def _(v, h=h):
                        B.wait(v, "dl")
                        B.wait(v, "dw")
                        v.memset(S_h[:], 0.0)
                        v.memset(S_hbf[:], 0.0)
                        last = None
                        for dc in range(2):
                            last = v.tensor_tensor(
                                out=ap3(qdih, dc * T,
                                        [[2 * T, 128], [CB, NCB], [1, CB]]),
                                in0=ap3(out_q, (2 * h + dc) * T,
                                        [[FC * T, 128], [CB, NCB], [1, CB]]),
                                in1=ap3(qd_t, 0, [[CB, 128], [0, NCB], [1, CB]]),
                                op=ALU.mult)
                        return last

                    for c in range(NCB):
                        cs = slice(c * CB, (c + 1) * CB)

                        @B.step("tensor")
                        def _(t, h=h, cs=cs):
                            last = None
                            for dc in range(2):
                                last = t.transpose(pktm[:, ts(dc, 128)],
                                                   out_k[:, 2 * h + dc, cs],
                                                   c_identb[:])
                            return last

                        @B.step("vector")
                        def _(v):
                            return v.tensor_scalar(out=ktm_sb[:], in0=pktm[:, :],
                                                   scalar1=dk_col[:, 0:1],
                                                   scalar2=None, op0=ALU.mult)

                        @B.step("tensor")
                        def _(t, h=h, cs=cs):
                            last = None
                            for dc in range(2):
                                last = t.matmul(psc[:, :],
                                                out_k[:, 2 * h + dc, cs],
                                                out_q[:, 2 * h + dc, cs],
                                                start=(dc == 0), stop=(dc == 1))
                            return last

                        @B.step("vector")
                        def _(v):
                            return v.tensor_tensor(out=sco_b[:], in0=psc[:, :],
                                                   in1=dm_t[:], op=ALU.mult)

                        @B.step("tensor")
                        def _(t, h=h, c=c, cs=cs):
                            for ec in range(2):
                                t.matmul(po_b[:, ts(ec, 128)],
                                         v_tm[:, c, h * 256 + ec * 128:
                                              h * 256 + (ec + 1) * 128],
                                         sco_b[:], start=True, stop=False)
                                t.matmul(po_b[:, ts(ec, 128)],
                                         S_hbf[:, 0, ts(ec, 128)],
                                         qdih[:, 0, cs], start=False, stop=False)
                                t.matmul(po_b[:, ts(ec, 128)],
                                         S_hbf[:, 1, ts(ec, 128)],
                                         qdih[:, 1, cs], start=False, stop=True)
                            last = None
                            for dc in range(2):
                                last = t.matmul(b_pinc[:, dc, :],
                                                ktm_sb[:, ts(dc, 128)],
                                                v_tm[:, c, h * 256:(h + 1) * 256],
                                                start=True, stop=True)
                            return last

                        @B.step("vector")
                        def _(v, h=h, c=c):
                            return v.tensor_copy(
                                ap3(o_fm, 2 * h * T + c * CB,
                                    [[FC * T, 128], [T, 2], [1, CB]]),
                                ap3(po_b, 0, [[256, 128], [CB, 2], [1, CB]]))

                        @B.step("vector")
                        def _(v, h=h):
                            last = None
                            for dc in range(2):
                                last = v.scalar_tensor_tensor(
                                    out=S_h[:, dc, :], in0=S_h[:, dc, :],
                                    scalar=GAMMA[h] ** CB, in1=b_pinc[:, dc, :],
                                    op0=ALU.mult, op1=ALU.add)
                            return last

                        @B.step("vector")
                        def _(v):
                            return v.tensor_copy(S_hbf[:, :, :], S_h[:, :, :])

                    @B.step("gpsimd")
                    def _(g, h=h):
                        B.wait(g, "dw")
                        B.dma(g, "dw", out=cc_b_in[:, h * 512:(h + 1) * 512],
                              in_=S_h[:, :, :])
                        return None

            mgx(first3_g)
            mgx(recvh_g)
            mgx(sendh_g)
            mgx(v_tm_g)
            mgx(out_k_g)

            with (
                nc.sbuf_tensor("b_S_in", [128, HB * 512], F32) as S_in,
                nc.sbuf_tensor("b_S_inb", [128, HB * 512], BF16) as S_inb,
                nc.sbuf_tensor("qfx_f", [128, T], F32) as qfx_f,
                nc.sbuf_tensor("qfb", [128, 2, T], BF16) as qfb,
                nc.psum_tensor("b_psfx", [128, 512], F32) as psfx,
            ):
                @B.step("gpsimd")
                def _(g):
                    B.wait(g, "dw")
                    inst = g.collective_compute(
                        "AllGather", ALU.bypass,
                        replica_groups=[[0, 1], [2, 3], [4, 5], [6, 7]],
                        ins=[cc_b_in.opt()], outs=[cc_b_out.opt()])
                    inst.then_inc(B.semget("cc"))
                    B.cnt["cc"] += 1
                    return None

                @B.step("gpsimd")
                def _(g):
                    B.wait(g, "dw")
                    B.wait(g, "cc")
                    B.dma(g, "dw", out=S_in[:], in_=cc_b_out[0:128, :])
                    return None

                @B.step("vector")
                def _(v):
                    B.wait(v, "dw")
                    return v.tensor_scalar(out=S_inb[:], in0=S_in[:],
                                           scalar1=c_mask[:, 0:1], scalar2=None,
                                           op0=ALU.mult)

                for h in range(HB):
                    @B.step("sync")
                    def _(s, h=h):
                        B.dma(s, "dl", out=qfx_f[:], in_=I["qfix"][h])
                        return None

                    @B.step("vector")
                    def _(v, h=h):
                        B.wait(v, "dl")
                        last = None
                        for dc in range(2):
                            last = v.tensor_tensor(out=qfb[:, dc, :],
                                                   in0=out_q[:, 2 * h + dc, :],
                                                   in1=qfx_f[:], op=ALU.mult)
                        return last

                    for ec in range(2):
                        for tt in range(2):
                            @B.step("tensor")
                            def _(t, h=h, ec=ec, tt=tt):
                                last = None
                                for dc in range(2):
                                    last = t.matmul(
                                        psfx[:, :],
                                        S_inb[:, (h * 2 + dc) * 256 +
                                              ec * 128:(h * 2 + dc) * 256 +
                                              (ec + 1) * 128],
                                        qfb[:, dc, ts(tt, 512)],
                                        start=(dc == 0), stop=(dc == 1))
                                return last

                            @B.step("vector")
                            def _(v, h=h, ec=ec, tt=tt):
                                c = 2 * h + ec
                                return v.tensor_tensor(
                                    out=o_fm[:, c, ts(tt, 512)],
                                    in0=o_fm[:, c, ts(tt, 512)],
                                    in1=psfx[:, :], op=ALU.add)

            mgx(out_q_g)

            if _DEV_STOP == "bo":
                sl = [(o_fm[:, c, :], c * 128, 1024) for c in range(FC)]
                _dev_dump(B, y, sl)
                raise _DevStop()

            g_fm_g = nc.sbuf_tensor("g_fm", [128, FC, T], BF16)
            g_fm = mge(g_fm_g)
            with (
                nc.sbuf_tensor("gw_st", [128, 2, FC, 256], BF16) as gw_st,
                nc.sbuf_tensor("gw_cs", [1, 2, 256], BF16) as gw_cs,
                nc.psum_tensor("gpj", [128, 512]) as gpj,
                nc.psum_tensor("gpj2", [128, 512]) as gpj2,
                nc.psum_tensor("gpc", [1, 512]) as gpc,
            ):
                def g_evac1(v, fc_out, tt, pj):
                    return v.tensor_tensor(
                        out=g_fm[:, fc_out, ts(tt, 512)],
                        in0=pj[:, :], in1=c3_b[:, ts(tt, 512)], op=ALU.mult)

                fm_proj_island(I["Wg"], 0, 8, gw_st, gw_cs, [gpj, gpj2], gpc,
                               FC, lambda c, tt: x2w[:, c, ts(tt, 512)],
                               lambda c: anw_bf[:, c:c + 1],
                               bs_negm_bf, g_evac1, bw=256)

            og = x2w
            with (
                nc.sbuf_tensor("gnw_sb", [128, 2], F32) as gnw_sb,
                nc.sbuf_tensor("go_sq", [128, T], BF16) as go_sq,
                nc.sbuf_tensor("go_row", [1, T], F32) as go_row,
                nc.sbuf_tensor("go_b", [128, T], F32) as go_b,
                nc.sbuf_tensor("go_sil", [128, T], BF16) as go_sil,
                nc.psum_tensor("gp0", [1, 512]) as gp0,
                nc.psum_tensor("gp1", [1, 512]) as gp1,
                nc.psum_tensor("gpb", [128, T]) as gpb,
            ):
                @B.step("sync")
                def _(s):
                    B.dma(s, "dl", out=gnw_sb[:], in_=I["gnw"])
                    return None

                for h in range(HB):
                    for ec in range(2):
                        @B.step("scalar")
                        def _(s, h=h, ec=ec):
                            return s.activation(out=go_sq[:],
                                                in_=o_fm[:, 2 * h + ec, :],
                                                func=AF.Square)

                        @B.step("tensor")
                        def _(t, ec=ec):
                            t.matmul(gp0[:, :], onesb[:], go_sq[:, 0:512],
                                     start=(ec == 0), stop=(ec == 1))
                            return t.matmul(gp1[:, :], onesb[:], go_sq[:, 512:1024],
                                            start=(ec == 0), stop=(ec == 1))

                    @B.step("vector")
                    def _(v):
                        v.tensor_scalar(out=go_row[0:1, 0:512], in0=gp0[:, :],
                                        scalar1=1.0 / 256.0, scalar2=EPS,
                                        op0=ALU.mult, op1=ALU.add)
                        return v.tensor_scalar(out=go_row[0:1, 512:1024],
                                               in0=gp1[:, :],
                                               scalar1=1.0 / 256.0, scalar2=EPS,
                                               op0=ALU.mult, op1=ALU.add)

                    @B.step("vector")
                    def _(v):
                        return v.reciprocal(out=go_row[:], in_=go_row[:])

                    @B.step("scalar")
                    def _(s):
                        return s.activation(out=go_row[:], in_=go_row[:],
                                            func=AF.Sqrt)

                    @B.step("tensor")
                    def _(t):
                        t.matmul(gpb[:, 0:512], onesf[:], go_row[0:1, 0:512],
                                 start=True, stop=True)
                        return t.matmul(gpb[:, 512:1024], onesf[:],
                                        go_row[0:1, 512:1024],
                                        start=True, stop=True)

                    for ec in range(2):
                        c = 2 * h + ec

                        @B.step("scalar")
                        def _(s, c=c):
                            return s.activation(out=go_sil[:], in_=g_fm[:, c, :],
                                                func=AF.Silu)

                        @B.step("vector")
                        def _(v, c=c, ec=ec):
                            B.wait(v, "dl")
                            return v.tensor_tensor(out=go_b[:],
                                                   in0=o_fm[:, c, :],
                                                   in1=gpb[:, :], op=ALU.mult)

                        @B.step("vector")
                        def _(v, ec=ec):
                            return v.tensor_scalar(out=go_b[:], in0=go_b[:],
                                                   scalar1=gnw_sb[:, ec:ec + 1],
                                                   scalar2=None, op0=ALU.mult)

                        @B.step("vector")
                        def _(v, c=c):
                            return v.tensor_tensor(out=og[:, c, :], in0=go_b[:],
                                                   in1=go_sil[:], op=ALU.mult)

            mgx(g_fm_g)
            mgx(o_fm_g)

            with (
                nc.sbuf_tensor("wo_st", [128, FC, 512], BF16) as wo_st,
                nc.sbuf_tensor("wo_x2", [128, 512], F32) as wo_x2,
                nc.sbuf_tensor("wo_te", [128, 512], F32) as wo_te,
                nc.psum_tensor("wo_pj", [128, 512]) as wo_pj,
            ):
                for blki in range(4):
                    w_mark = [0]

                    @B.step("gpsimd")
                    def _(g, blki=blki, w_mark=w_mark):
                        B.wait(g, "dw")
                        for c in range(FC):
                            B.dma(g, "dw", out=wo_st[:, c, :],
                                  in_=I["Wo"][ts(c, 128),
                                              blki * 512:(blki + 1) * 512])
                        w_mark[0] = B.cnt["dw"]
                        return None

                    for dc in range(4):
                        fc_out = blki * 4 + dc
                        for tt in range(2):
                            @B.step("sync")
                            def _(s, fc_out=fc_out, tt=tt):
                                B.dma(s, "dl", out=wo_x2[:],
                                      in_=x2scr[ts(fc_out, 128), ts(tt, 512)])
                                return None

                            @B.step("tensor")
                            def _(t, dc=dc, tt=tt, w_mark=w_mark):
                                B.wait(t, "dw", w_mark[0])
                                last = None
                                for c in range(FC):
                                    last = t.matmul(wo_pj[:, :],
                                                    wo_st[:, c, ts(dc, 128)],
                                                    og[:, c, ts(tt, 512)],
                                                    start=(c == 0),
                                                    stop=(c == FC - 1))
                                return last

                            @B.step("vector")
                            def _(v, fc_out=fc_out, tt=tt):
                                B.wait(v, "dl")
                                B.wait(v, "dh")
                                return v.tensor_tensor(
                                    out=wo_te[:], in0=wo_x2[:],
                                    in1=m3_b[:, ts(tt, 512)], op=ALU.subtract)

                            @B.step("vector")
                            def _(v, tt=tt):
                                return v.tensor_tensor(out=wo_te[:], in0=wo_te[:],
                                                       in1=r3_b[:, ts(tt, 512)],
                                                       op=ALU.mult)

                            @B.step("vector")
                            def _(v):
                                return v.tensor_tensor(out=wo_te[:], in0=wo_te[:],
                                                       in1=wo_pj[:, :], op=ALU.add)

                            @B.step("sync")
                            def _(s, fc_out=fc_out, tt=tt):
                                B.dma(s, "dh",
                                      out=hscr[ts(fc_out, 128), ts(tt, 512)],
                                      in_=wo_te[:])
                                return None

            mgx(x2w_g)

            if _DEV_STOP == "bh":
                @B.step("sync")
                def _(s):
                    B.wait(s, "dh")
                    B.dma(s, "do", out=y, in_=hscr)
                    return None

                @B.step("sync")
                def _(s):
                    B.wait(s, "do")
                    return None
                raise _DevStop()

            z_fm_g = nc.sbuf_tensor("z_fm", [128, 22, T], BF16)
            z_fm = mge(z_fm_g)
            gates_g = nc.sbuf_tensor("gates", [128, 22, T], BF16)
            gates = mge(gates_g)
            _mg1 = [nc.sbuf_tensor("y_bf", [128, FC, T], BF16),
                    nc.sbuf_tensor("mn_f", [128, FC], F32),
                    nc.sbuf_tensor("m_hst", [128, T], F32),
                    nc.sbuf_tensor("m_sq", [128, T], BF16),
                    nc.sbuf_tensor("m_row", [1, T], F32),
                    nc.sbuf_tensor("r4_b", [128, T], F32),
                    nc.psum_tensor("mp0", [1, 512]),
                    nc.psum_tensor("mp1", [1, 512]),
                    nc.psum_tensor("mpb", [128, T])]
            (y_bf, mn_f, m_hst, m_sq, m_row, r4_b, mp0, mp1, mpb) = [
                mge(g) for g in _mg1]
            if True:
                @B.step("sync")
                def _(s):
                    B.dma(s, "dl", out=mn_f[:], in_=I["mnw"])
                    return None

                for c in range(FC):
                    @B.step("sync")
                    def _(s, c=c):
                        if c == 0:
                            B.wait(s, "dh")
                        B.dma(s, "dl", out=m_hst[:], in_=hscr[ts(c, 128), :])
                        return None

                    @B.step("scalar")
                    def _(s, c=c):
                        B.wait(s, "dl")
                        return s.activation(out=m_sq[:], in_=m_hst[:],
                                            func=AF.Square)

                    @B.step("vector")
                    def _(v, c=c):
                        return v.tensor_scalar(out=y_bf[:, c, :], in0=m_hst[:],
                                               scalar1=mn_f[:, c:c + 1],
                                               scalar2=None, op0=ALU.mult)

                    @B.step("tensor")
                    def _(t, c=c):
                        f, l = (c == 0), (c == FC - 1)
                        t.matmul(mp0[:, :], onesb[:], m_sq[:, 0:512],
                                 start=f, stop=l)
                        return t.matmul(mp1[:, :], onesb[:], m_sq[:, 512:1024],
                                        start=f, stop=l)

                @B.step("vector")
                def _(v):
                    v.tensor_scalar(out=m_row[0:1, 0:512], in0=mp0[:, :],
                                    scalar1=1.0 / D, scalar2=EPS,
                                    op0=ALU.mult, op1=ALU.add)
                    return v.tensor_scalar(out=m_row[0:1, 512:1024], in0=mp1[:, :],
                                           scalar1=1.0 / D, scalar2=EPS,
                                           op0=ALU.mult, op1=ALU.add)

                @B.step("vector")
                def _(v):
                    return v.reciprocal(out=m_row[:], in_=m_row[:])

                @B.step("scalar")
                def _(s):
                    return s.activation(out=m_row[:], in_=m_row[:], func=AF.Sqrt)

                @B.step("tensor")
                def _(t):
                    t.matmul(mpb[:, 0:512], onesf[:], m_row[0:1, 0:512],
                             start=True, stop=True)
                    return t.matmul(mpb[:, 512:1024], onesf[:],
                                    m_row[0:1, 512:1024], start=True, stop=True)

                @B.step("vector")
                def _(v):
                    return v.tensor_copy(r4_b[:, :], mpb[:, :])

                for c in range(FC):
                    @B.step("vector")
                    def _(v, c=c):
                        return v.tensor_tensor(out=y_bf[:, c, :],
                                               in0=y_bf[:, c, :], in1=r4_b[:, :],
                                               op=ALU.mult)

            for g in reversed(_mg1[1:6]):
                mgx(g)
            _mg2 = [nc.sbuf_tensor("mg_st", [128, FC, 512], BF16),
                    nc.psum_tensor("mg_pj", [128, 512])]
            (mg_st, mg_pj) = [mge(g) for g in _mg2]
            if True:
                for blki in range(11):
                    w_mark = [0]

                    @B.step("gpsimd")
                    def _(g, blki=blki, w_mark=w_mark):
                        B.wait(g, "dw")
                        for c in range(FC):
                            B.dma(g, "dw", out=mg_st[:, c, :],
                                  in_=I["Wgate"][ts(c, 128),
                                                 blki * 512:(blki + 1) * 512])
                        w_mark[0] = B.cnt["dw"]
                        return None

                    for dc in range(4):
                        hc = blki * 4 + dc
                        for tt in range(2):
                            @B.step("tensor")
                            def _(t, dc=dc, tt=tt, w_mark=w_mark):
                                B.wait(t, "dw", w_mark[0])
                                last = None
                                for c in range(FC):
                                    last = t.matmul(mg_pj[:, :],
                                                    mg_st[:, c, ts(dc, 128)],
                                                    y_bf[:, c, ts(tt, 512)],
                                                    start=(c == 0),
                                                    stop=(c == FC - 1))
                                return last

                            if hc < 22:
                                @B.step("scalar")
                                def _(s, hc=hc, tt=tt):
                                    return s.activation(
                                        out=gates[:, hc, ts(tt, 512)],
                                        in_=mg_pj[:, :], func=AF.Silu)
                            else:
                                @B.step("vector")
                                def _(v, hc=hc, tt=tt):
                                    return v.tensor_tensor(
                                        out=z_fm[:, hc - 22, ts(tt, 512)],
                                        in0=gates[:, hc - 22, ts(tt, 512)],
                                        in1=mg_pj[:, :], op=ALU.mult)

            mgx(_mg2[1])
            mgx(_mg2[0])
            mgx(_mg1[0])
            mgx(gates_g)
            xres2_g = nc.sbuf_tensor("xres2", [128, FC, T], F32, side="right")
            xres2 = mge(xres2_g)
            _mg3 = [nc.sbuf_tensor("md_st", [128, 22, 512], BF16),
                    nc.sbuf_tensor("md_x2", [128, 512], F32),
                    nc.sbuf_tensor("md_h", [128, 512], F32),
                    nc.sbuf_tensor("md_te", [128, 512], F32),
                    nc.psum_tensor("md_pj", [128, 512])]
            (md_st, md_x2, md_h, md_te, md_pj) = [mge(g) for g in _mg3]
            if True:
                for blki in range(4):
                    w_mark = [0]

                    @B.step("gpsimd")
                    def _(g, blki=blki, w_mark=w_mark):
                        B.wait(g, "dw")
                        for kc in range(22):
                            B.dma(g, "dw", out=md_st[:, kc, :],
                                  in_=I["Wdown"][ts(kc, 128),
                                                 blki * 512:
                                                 (blki + 1) * 512])
                        w_mark[0] = B.cnt["dw"]
                        return None

                    for dc in range(4):
                        fc_out = blki * 4 + dc
                        for tt in range(2):
                            @B.step("sync")
                            def _(s, fc_out=fc_out, tt=tt):
                                B.dma(s, "dl", out=md_x2[:],
                                      in_=x2scr[ts(fc_out, 128),
                                                ts(tt, 512)])
                                B.dma(s, "dl", out=md_h[:],
                                      in_=hscr[ts(fc_out, 128),
                                               ts(tt, 512)])
                                return None

                            @B.step("tensor")
                            def _(t, dc=dc, tt=tt, w_mark=w_mark):
                                B.wait(t, "dw", w_mark[0])
                                last = None
                                for kc in range(22):
                                    last = t.matmul(
                                        md_pj[:, :],
                                        md_st[:, kc, ts(dc, 128)],
                                        z_fm[:, kc, ts(tt, 512)],
                                        start=(kc == 0), stop=(kc == 21))
                                return last

                            @B.step("vector")
                            def _(v):
                                B.wait(v, "dl")
                                return v.tensor_tensor(out=md_te[:],
                                                       in0=md_h[:],
                                                       in1=md_pj[:, :],
                                                       op=ALU.add)

                            @B.step("vector")
                            def _(v, fc_out=fc_out, tt=tt):
                                return v.tensor_tensor(
                                    out=xres2[:, fc_out, ts(tt, 512)],
                                    in0=md_te[:], in1=md_x2[:],
                                    op=ALU.add)

            for g in reversed(_mg3):
                mgx(g)
            for g in reversed(_mg1[6:9]):
                mgx(g)
            mgx(z_fm_g)

            if _DEV_STOP == "x3":
                sl = [(xres2[:, c, :], c * 128, 1024) for c in range(FC)]
                _dev_dump(B, y, sl)
                raise _DevStop()

            ffn_use("f2_", xres2)

            @B.step("sync")
            def _(s):
                for c in range(FC):
                    B.dma(s, "do", out=y[ts(c, 128), :], in_=xres2[:, c, :])
                return None

            @B.step("sync")
            def _(s):
                B.wait(s, "do")
                return None

      except _DevStop:
          pass
      mg_cleanup()
    return nc
